# version 01
# baseline (speedup 1.0000x reference)
import os
import numpy as np
import ml_dtypes
import concourse.bass as bass
import concourse.mybir as mybir
from concourse.bass_utils import run_bass_kernel_spmd

F32 = mybir.dt.float32
BF16 = mybir.dt.bfloat16
U8 = mybir.dt.uint8
AF = mybir.ActivationFunctionType
OP = mybir.AluOpType
AX = mybir.AxisListType

BF = ml_dtypes.bfloat16

D = 4096
NH, HD, SS, NG, CHUNK = 128, 64, 128, 8, 128
INTER = NH * HD
CONV_DIM = INTER + 2 * NG * SS
PROJ = INTER + CONV_DIM + NH
TOK = 2048
HALF = 1024
NCH = HALF // CHUNK
GRP = INTER // NG
HPG = NH // NG
KD = D // 128
EPS = 1e-5
DT_MIN, DT_MAX = 1e-3, 100.0
N_BLK = 1 + NG * 10



def build_program(dbg_taps=()):
    nc = bass.Bass("TRN2", target_bir_lowering=False)
    AP_IN = dict(kind="ExternalInput")

    x_own = nc.dram_tensor("x_own", [TOK, D], F32, **AP_IN).ap()
    x_pre = nc.dram_tensor("x_pre", [TOK, D], F32, **AP_IN).ap()
    x_halo = nc.dram_tensor("x_halo", [3, D], F32, **AP_IN).ap()
    flags = nc.dram_tensor("flags", [128, 4], F32, **AP_IN).ap()
    w_xbcdt = nc.dram_tensor("w_xbcdt", [N_BLK, KD, 128, 128], F32, **AP_IN).ap()
    w_gate = nc.dram_tensor("w_gate", [16, KD, 128, 512], F32, **AP_IN).ap()
    w_out = nc.dram_tensor("w_out", [8, 64, 128, 512], F32, **AP_IN).ap()
    w_base = nc.dram_tensor("w_base", [8, KD, 128, 512], F32, **AP_IN).ap()
    conv_taps = nc.dram_tensor("conv_taps", [80, 128, 4], F32, **AP_IN).ap()
    conv_bias = nc.dram_tensor("conv_bias", [80, 128], F32, **AP_IN).ap()
    dt_bias = nc.dram_tensor("dt_bias", [128], F32, **AP_IN).ap()
    d_vec = nc.dram_tensor("d_vec", [128], F32, **AP_IN).ap()
    norm_w = nc.dram_tensor("norm_w", [64, 128], F32, **AP_IN).ap()
    wg_thr = nc.dram_tensor("wg_thr", [KD, 128, 16], F32, **AP_IN).ap()
    b_thr = nc.dram_tensor("b_thr", [1], F32, **AP_IN).ap()
    b_base = nc.dram_tensor("b_base", [D], F32, **AP_IN).ap()
    a_flat = nc.dram_tensor("a_flat", [KD, 128, 128], F32, **AP_IN).ap()
    b_flat = nc.dram_tensor("b_flat", [128, D], F32, **AP_IN).ap()

    y_out = nc.dram_tensor("y_out", [TOK, D], F32, kind="ExternalOutput").ap()
    dbg = nc.dram_tensor("dbg", [128, 8192], F32, kind="ExternalOutput").ap()

    gate_sp = nc.dram_tensor("gate_sp", [TOK, INTER], BF16, kind="Internal").ap()
    normed_sp = nc.dram_tensor("normed_sp", [TOK, INTER], BF16, kind="Internal").ap()
    x2_sp = nc.dram_tensor("x2_sp", [TOK, D], F32, kind="Internal").ap()
    cs_dram = nc.dram_tensor("cs_dram", [NH, HALF], F32, kind="Internal").ap()
    cdexp_dram = nc.dram_tensor("cdexp_dram", [NCH, NH], F32, kind="Internal").ap()

    ident_bf_d = nc.inline_tensor(np.eye(128, dtype=BF), name="identbf_c").ap()
    ident_f_d = nc.inline_tensor(np.eye(128, dtype=np.float32), name="identf_c").ap()
    tril_d = nc.inline_tensor(
        (np.arange(128)[None, :] >= np.arange(128)[:, None]).astype(BF), name="tril_c").ap()
    avec_d = nc.inline_tensor(-(np.arange(128, dtype=np.float32) + 1.0)[:, None], name="avec_c").ap()
    onesbf_d = nc.inline_tensor(np.ones((1, 128), dtype=BF), name="onesbf_c").ap()

    sb = {}

    def alloc(name, cols, dtype):
        sb[name] = nc.alloc_sbuf_tensor("s_" + name, [128, cols], dtype).ap()
        return sb[name]

    alloc("identbf", 128, BF16)
    alloc("identf", 128, F32)
    alloc("tril", 128, BF16)
    alloc("avec", 1, F32)
    alloc("onesbf", 128, BF16)
    alloc("flags", 4, F32)
    alloc("dtb", 1, F32)
    alloc("bthr2", 1, F32)
    alloc("drep", 128, F32)
    alloc("convw", 80 * 4, F32)
    alloc("convb", 80, F32)
    alloc("normw", 64, F32)
    alloc("carry", NG * GRP, F32)
    alloc("halo", N_BLK * 4, BF16)
    alloc("halolnT", KD * 4, BF16)
    alloc("dtc_tm", NCH * 128, F32)
    alloc("cs_tm", NCH * 128, F32)
    alloc("dec_tm", NCH * 128, F32)
    alloc("ecs_tm", NCH * 128, F32)
    alloc("cdrep", NCH * 128, F32)
    alloc("cslast", NCH, F32)
    alloc("stats", 8, F32)
    alloc("cpool", 4, F32)
    alloc("tiny", 64, F32)
    alloc("ssq", 8, F32)
    alloc("ffn", 64, F32)
    sb["bthr_raw"] = nc.alloc_sbuf_tensor("s_bthr_raw", [128, 1], F32).ap()

    avail_kb = (nc.sbuf_bytes_remaining - 512) // 1024
    assert avail_kb >= 142, avail_kb
    region = nc.alloc_sbuf_tensor("s_region", [128, avail_kb * 1024], U8).ap()

    def rv(off_kb, cols, dtype, off_b=0):
        esz = {F32: 4, BF16: 2}[dtype]
        o = off_kb * 1024 + off_b
        return region[:, o:o + cols * esz].bitcast(dtype)

    LN_XT = 0
    XTILE = 64
    LNOUT = 96
    GWSL = 64
    GATET = 80
    WBLK = 64
    XG = 80
    GBC = 96
    BTM = 100
    SCM = 102
    XBC = 104
    DIAG = 109
    CSREP = 110
    LMS = 118
    XTM = 122
    XDT = 124
    XDTD = 126
    TVEC = 128
    YO = 132
    YG = 136
    GT2 = 138
    NRM = 136
    SQS = 118
    DTT = 110
    CBF = 140

    ps = nc.alloc_psum_tensor("ps", [128, 4096], F32).ap()

    def bank(i):
        return ps[:, i * 512:(i + 1) * 512]

    def bank2(i):
        return ps[:, i * 512:(i + 2) * 512]

    def bankbf(i, cols=1024):
        return ps.bitcast(BF16)[:, i * 1024:i * 1024 + cols]

    import contextlib
    cm = contextlib.ExitStack()
    sems = {}
    for n in ["sio", "sw", "spe", "sdv", "sac", "spl",
              "swa", "swb", "swc",
              "sxa", "sxb", "scr", "scd", "sgt", "snr", "sga", "sra", "srb", "snl"]:
        sems[n] = [cm.enter_context(nc.semaphore(n)), 0]

    def inc(name, instr, k=1):
        instr.then_inc(sems[name][0], k)
        sems[name][1] += k
        return sems[name][1]

    def n_of(name):
        return sems[name][1]

    def wt(eng, name, val=None):
        v = n_of(name) if val is None else val
        if v > 0:
            eng.wait_ge(sems[name][0], v)

    block_cm = nc.Block()
    block = block_cm.__enter__()

    nc.const_aps.register(F32, EPS, sb["cpool"][:, 0:1])

    @block.vector
    def _(v):
        v.memset(sb["cpool"][:, 0:1], EPS)
        v.drain()
        v.memset(sb["cpool"][:, 2:3], 1.0)
        v.drain()

    DBG = {}
    dbg_used = [0]

    def dbg_dump(tag, ap_src, cols):
        if tag not in dbg_taps:
            return
        off = dbg_used[0]
        assert off + cols <= 8192

        @block.sync
        def _(s):
            wt(s, "sdv"); wt(s, "sac"); wt(s, "spe"); wt(s, "spl")
            inc("sio", s.dma_start(out=dbg[:, off:off + cols], in_=ap_src), 16)

        @block.vector
        def _(v):
            wt(v, "sio")
            i = v.memset(sb["cpool"][:, 3:4], 0.0)
            v.drain()
            inc("sdv", i)
        dbg_used[0] = off + cols
        DBG[tag] = (off, cols)

    @block.sync
    def _(s):
        def ser(instr):
            inc("sio", instr, 16)
            wt(s, "sio")
        ser(s.dma_start(out=sb["identbf"][:, :], in_=ident_bf_d))
        ser(s.dma_start(out=sb["identf"][:, :], in_=ident_f_d))
        ser(s.dma_start(out=sb["tril"][:, :], in_=tril_d))
        ser(s.dma_start(out=sb["avec"][:, :], in_=avec_d))
        ser(s.dma_start(out=sb["onesbf"][0:1, :], in_=onesbf_d))
        ser(s.dma_start(out=sb["flags"][:, :], in_=flags))
        ser(s.dma_start(out=sb["dtb"][:, 0:1], in_=dt_bias.rearrange("(p o) -> p o", o=1)))
        ser(s.dma_start(out=sb["drep"][:, :], in_=d_vec.rearrange("(o h) -> o h", o=1).broadcast_to([128, 128])))
        ser(s.dma_start(out=sb["normw"][:, :], in_=norm_w.rearrange("k p -> p k"), allow_slow_non_contiguous=True))
        ser(s.dma_start(out=sb["bthr_raw"][:, 0:1], in_=b_thr.rearrange("(o e) -> o e", o=1).broadcast_to([128, 1])))
        ser(s.dma_start(out=sb["convb"][:, :], in_=conv_bias.rearrange("b p -> p b"), allow_slow_non_contiguous=True))

    @block.gpsimd
    def _(g):
        inc("swc", g.dma_start(out=sb["convw"][:, :].rearrange("p (b k) -> p b k", b=80), in_=conv_taps.rearrange("b p k -> p b k")), 16)
        wt(g, "swc")

    @block.vector
    def _(v):
        wt(v, "sio")
        i = v.tensor_scalar(sb["bthr2"][:, 0:1], sb["bthr_raw"][:, 0:1], 0.5, None, OP.mult)
        v.drain()
        inc("sdv", i)

    xt_h = rv(XTILE, D, F32)
    lnout = rv(LNOUT, D, BF16)

    @block.sync
    def _(s):
        wt(s, "sio")
        inc("sio", s.dma_start(out=xt_h[0:3, :], in_=x_halo), 16)

    @block.vector
    def _(v):
        wt(v, "sio")
        v.reduce_sum(sb["tiny"][0:3, 0:1], xt_h[0:3, :], AX.X)
        v.drain()
        v.tensor_scalar(sb["tiny"][0:3, 1:2], sb["tiny"][0:3, 0:1], -1.0 / D, None, OP.mult)
        v.drain()
        i = v.tensor_scalar(xt_h[0:3, :], xt_h[0:3, :], sb["tiny"][0:3, 1:2], None, OP.add)
        v.drain()
        inc("sdv", i)

    @block.scalar
    def _(sc):
        wt(sc, "sdv")
        sc.activation(lnout[0:3, :], xt_h[0:3, :], AF.Square, accum_out=sb["tiny"][0:3, 2:3])
        sc.drain()
        sc.activation(sb["tiny"][0:3, 3:4], sb["tiny"][0:3, 2:3], AF.Ln, scale=1.0 / D, bias=EPS)
        sc.drain()
        inc("sac", sc.activation(sb["tiny"][0:3, 4:5], sb["tiny"][0:3, 3:4], AF.Exp, scale=-0.5))

    @block.vector
    def _(v):
        wt(v, "sac")
        i = v.tensor_scalar(lnout[0:3, :], xt_h[0:3, :], sb["tiny"][0:3, 4:5], None, OP.mult)
        v.drain()
        inc("sdv", i)

    @block.vector
    def _(v):
        i = v.memset(ps[:, 0:KD * 2], 0.0)
        v.drain()
        inc("sdv", i)

    @block.tensor
    def _(t):
        wt(t, "sdv")
        for k in range(KD):
            i = t.transpose(bankbf(0)[:, k * 4:k * 4 + 4], lnout[0:4, k * 128:(k + 1) * 128],
                            sb["identbf"][0:4, 0:4])
        inc("spe", i)

    @block.vector
    def _(v):
        wt(v, "spe")
        i = v.tensor_copy(sb["halolnT"][:, :], bankbf(0)[:, 0:KD * 4])
        v.drain()
        inc("sdv", i)

    @block.sync
    def _(s):
        wt(s, "sio")

    KSTAGE = int(os.environ.get("KSTAGE", "99"))

    def emit_ln(src_dram, src_row0, dstv, span, dst_off):
        entry_sdv = n_of("sdv")
        entry_spe = n_of("spe")
        xt_guard = {}
        lnout_guard = [0]
        bank_guard = [0]
        for t8 in range(NCH):
            buf = t8 % 2
            xt = rv(XTILE, D, F32, off_b=buf * D * 4)
            row0 = src_row0 + t8 * 128

            @block.sync
            def _(s, t8=t8, row0=row0, xt=xt):
                if t8 - 2 in xt_guard:
                    wt(s, "sdv", xt_guard[t8 - 2])
                if t8 < 2:
                    wt(s, "sdv", entry_sdv)
                    wt(s, "spe", entry_spe)
                inc("sxa" if t8 % 2 == 0 else "sxb", s.dma_start(out=xt[:, :], in_=src_dram[row0:row0 + 128, :]), 16)

            @block.vector
            def _(v, xt=xt, t8=t8):
                wt(v, "sxa" if t8 % 2 == 0 else "sxb")
                v.reduce_sum(sb["stats"][:, 0:1], xt[:, :], AX.X)
                v.drain()
                v.tensor_scalar(sb["stats"][:, 1:2], sb["stats"][:, 0:1], -1.0 / D, None, OP.mult)
                v.drain()
                i = v.tensor_scalar(xt[:, :], xt[:, :], sb["stats"][:, 1:2], None, OP.add)
                v.drain()
                inc("sdv", i)

            @block.scalar
            def _(sc, xt=xt):
                wt(sc, "sdv")
                for q in range(4):
                    sc.activation(lnout[:, q * 1024:(q + 1) * 1024], xt[:, q * 1024:(q + 1) * 1024],
                                  AF.Square, accum_out=sb["stats"][:, 2 + q:3 + q])
                sc.drain()
                inc("sac", sc.activation(sb["cpool"][:, 1:2], sb["cpool"][:, 0:1], AF.Copy))

            @block.vector
            def _(v):
                wt(v, "sac")
                i = v.reduce_sum(sb["stats"][:, 6:7], sb["stats"][:, 2:6], AX.X)
                v.drain()
                inc("sdv", i)

            @block.scalar
            def _(sc):
                wt(sc, "sdv")
                sc.activation(sb["stats"][:, 7:8], sb["stats"][:, 6:7], AF.Ln, scale=1.0 / D, bias=EPS)
                sc.drain()
                inc("sac", sc.activation(sb["stats"][:, 7:8], sb["stats"][:, 7:8], AF.Exp, scale=-0.5))

            @block.vector
            def _(v, xt=xt):
                wt(v, "sac")
                if lnout_guard[0]:
                    wt(v, "spe", lnout_guard[0])
                i = v.tensor_scalar(lnout[:, :], xt[:, :], sb["stats"][:, 7:8], None, OP.mult)
                v.drain()
                inc("sdv", i)
            xt_guard[t8] = n_of("sdv")

            @block.tensor
            def _(t):
                wt(t, "sdv")
                if bank_guard[0]:
                    wt(t, "sdv", bank_guard[0])
                for k in range(KD):
                    b4 = k // 8
                    i = t.transpose(bankbf(b4)[:, (k % 8) * 128:(k % 8) * 128 + 128],
                                    lnout[:, k * 128:(k + 1) * 128], sb["identbf"][:, :])
                    if k == KD - 1:
                        inc("spe", i)
            lnout_guard[0] = n_of("spe")

            @block.vector
            def _(v, t8=t8):
                wt(v, "spe")
                for b4 in range(4):
                    i = v.tensor_copy(
                        dstv.rearrange("p (k c) -> p k c", c=span)[:, b4 * 8:(b4 + 1) * 8,
                                                                  dst_off + t8 * 128: dst_off + t8 * 128 + 128],
                        bankbf(b4).rearrange("p (k c) -> p k c", k=8))
                    if b4 == 3:
                        inc("sdv", i)
            bank_guard[0] = n_of("sdv")

    def emit_gate(half):
        slab_guard = [0, 0]
        evac_of = {}
        spill_guard = [0]
        lnv = rv(LN_XT, KD * HALF, BF16)
        for fb in range(16):
            for kh in range(4):
                buf = (fb * 4 + kh) % 2
                wsl = rv(GWSL, 8 * 512, BF16, off_b=buf * 8 * 1024)

                wsem = "swa" if buf == 0 else "swb"

                @block.gpsimd
                def _(g, fb=fb, kh=kh, buf=buf, wsl=wsl, wsem=wsem):
                    if slab_guard[buf]:
                        wt(g, "spe", slab_guard[buf])
                    wt(g, "sdv", ipb["lnguard"])
                    inc(wsem, g.dma_start(out=wsl.rearrange("p (k f) -> p k f", k=8),
                                          in_=w_gate[fb, kh * 8:(kh + 1) * 8].rearrange("k d f -> d k f")), 16)

                @block.tensor
                def _(t, fb=fb, kh=kh, wsl=wsl, wsem=wsem):
                    wt(t, wsem)
                    for tb in range(NCH):
                        if kh == 0:
                            key = (fb - 1, tb)
                            if key in evac_of:
                                wt(t, "sac", evac_of[key])
                        for k8 in range(8):
                            k = kh * 8 + k8
                            i = t.matmul(bank(tb),
                                         lnv.rearrange("p (k c) -> p k c", k=KD)[:, k, tb * 128:tb * 128 + 128],
                                         wsl[:, k8 * 512:(k8 + 1) * 512],
                                         start=(k == 0), stop=(k == KD - 1))
                    inc("spe", i)
                slab_guard[buf] = n_of("spe")
            pe_end = n_of("spe")

            for tb in range(NCH):
                @block.scalar
                def _(sc, tb=tb, fb=fb):
                    wt(sc, "spe", pe_end)
                    if spill_guard[0]:
                        wt(sc, "sga", spill_guard[0])
                    i = sc.activation(rv(GATET, 512, BF16), bank(tb), AF.Silu)
                    evac_of[(fb, tb)] = inc("sac", i)

                @block.sync
                def _(s, tb=tb, fb=fb):
                    wt(s, "sac")
                    spill_guard[0] = inc("sga", s.dma_start(
                        out=gate_sp[half * HALF + tb * 128: half * HALF + (tb + 1) * 128,
                                    fb * 512:(fb + 1) * 512],
                        in_=rv(GATET, 512, BF16)), 16)

    def emit_dt():
        za = rv(DTT, HALF, F32)
        zb = rv(DTT, HALF, F32, off_b=4096)
        dtc = rv(YO, HALF, F32)
        cs = rv(TVEC, HALF, F32)

        @block.vector
        def _(v):
            wt(v, "spe")
            i = v.tensor_scalar(za[:, :], ps[:, 0:HALF], sb["dtb"][:, 0:1], None, OP.add)
            v.drain()
            if os.environ.get("KTAP_DT"):
                i = v.tensor_copy(rv(YG, HALF, F32)[:, :], za[:, :])
                v.drain()
            inc("sdv", i)

        @block.scalar
        def _(sc):
            wt(sc, "sdv")
            sc.activation(zb[:, :], za[:, :], AF.Abs)
            sc.drain()
            sc.activation(zb[:, :], zb[:, :], AF.Exp, scale=-1.0)
            sc.drain()
            inc("sac", sc.activation(zb[:, :], zb[:, :], AF.Ln, bias=1.0))

        @block.vector
        def _(v):
            wt(v, "sac")
            v.tensor_scalar(dtc[:, :], za[:, :], 0.0, None, OP.max)
            v.drain()
            v.tensor_tensor(dtc[:, :], dtc[:, :], zb[:, :], OP.add)
            v.drain()
            v.tensor_scalar(dtc[:, :], dtc[:, :], DT_MIN, DT_MAX, OP.max, OP.min)
            v.drain()
            v.tensor_scalar(za[:, :], dtc[:, :], sb["avec"][:, 0:1], None, OP.mult)
            v.drain()
            for c in range(NCH):
                v.tensor_tensor_scan(cs[:, c * 128:(c + 1) * 128],
                                     sb["cpool"][:, 2:3].broadcast_to([128, 128]),
                                     za[:, c * 128:(c + 1) * 128], 0.0, OP.mult, OP.add)
            v.drain()
            for c in range(NCH):
                v.tensor_scalar(zb[:, c * 128:(c + 1) * 128], cs[:, c * 128:(c + 1) * 128],
                                cs[:, c * 128 + 127:c * 128 + 128], None, OP.subtract)
            v.drain()
            i = v.tensor_copy(sb["cslast"][:, :], cs[:, 127::128])
            v.drain()
            inc("sdv", i)

        @block.scalar
        def _(sc):
            wt(sc, "sdv")
            sc.activation(zb[:, :], zb[:, :], AF.Exp, scale=-1.0)
            sc.drain()
            sc.activation(sb["tiny"][:, 0:NCH], sb["cslast"][:, :], AF.Exp)
            sc.drain()
            inc("sac", sc.activation(sb["cpool"][:, 1:2], sb["cpool"][:, 0:1], AF.Copy))

        @block.sync
        def _(s):
            wt(s, "sdv"); wt(s, "sac")
            inc("scd", s.dma_start(out=cs_dram, in_=cs[:, :]), 16)
            wt(s, "scd")
            inc("scd", s.dma_start(out=cdexp_dram, in_=sb["tiny"][:, 0:NCH],
                                   allow_slow_non_contiguous=True), 16)
            wt(s, "scd")
            inc("scd", s.dma_start(out=sb["cdrep"][:, :].rearrange("p (c h) -> p c h", c=NCH),
                                   in_=cdexp_dram.rearrange("(o c) h -> o c h", o=1).broadcast_to([128, NCH, 128])), 16)

        for c in range(NCH):
            @block.tensor
            def _(t, c=c):
                wt(t, "sdv"); wt(t, "sac")
                t.transpose(bank(0)[:, 0:128], dtc[:, c * 128:(c + 1) * 128], sb["identf"][:, :])
                t.transpose(bank(1)[:, 0:128], cs[:, c * 128:(c + 1) * 128], sb["identf"][:, :])
                i = t.transpose(bank(2)[:, 0:128], zb[:, c * 128:(c + 1) * 128], sb["identf"][:, :])
                inc("spe", i)

            @block.vector
            def _(v, c=c):
                wt(v, "spe")
                v.tensor_copy(sb["dtc_tm"][:, c * 128:(c + 1) * 128], bank(0)[:, 0:128])
                v.tensor_copy(sb["cs_tm"][:, c * 128:(c + 1) * 128], bank(1)[:, 0:128])
                i = v.tensor_copy(sb["dec_tm"][:, c * 128:(c + 1) * 128], bank(2)[:, 0:128])
                v.drain()
                inc("sdv", i)

        @block.scalar
        def _(sc):
            wt(sc, "sdv")
            inc("sac", sc.activation(sb["ecs_tm"][:, :], sb["cs_tm"][:, :], AF.Exp))

    ipb = {"buf": 0, "xbuf": 0, "pe_guard": [0, 0], "evac_guard": 0,
           "conv_pe": [0, 0], "silu_guard": 0, "lnguard": 0, "gate_io": 0, "bank_sac": 0}

    def emit_inproj_block(blk, dst_kb, is_r, first_half):
        buf = ipb["buf"]; ipb["buf"] ^= 1
        wsl = rv(WBLK, KD * 128, BF16, off_b=buf * KD * 256)
        is_conv = blk != 0
        cb = blk - 1
        lnv = rv(LN_XT, KD * HALF, BF16)

        wsem = "swa" if buf == 0 else "swb"

        @block.gpsimd
        def _(g, buf=buf, wsl=wsl, blk=blk, wsem=wsem):
            if ipb["pe_guard"][buf]:
                wt(g, "spe", ipb["pe_guard"][buf])
            wt(g, "sdv", ipb["lnguard"])
            inc(wsem, g.dma_start(out=wsl.rearrange("p (k f) -> p k f", k=KD), in_=w_xbcdt[blk].rearrange("k d f -> d k f")), 16)

        @block.tensor
        def _(t, wsl=wsl, blk=blk, wsem=wsem):
            wt(t, wsem)
            if ipb["evac_guard"]:
                wt(t, "sdv", ipb["evac_guard"])
            if ipb["bank_sac"]:
                wt(t, "sac", ipb["bank_sac"])
            for tt in range(2):
                for k in range(KD):
                    i = t.matmul(bank(tt), wsl[:, k * 128:(k + 1) * 128],
                                 lnv.rearrange("p (k c) -> p k c", k=KD)[:, k, tt * 512:(tt + 1) * 512],
                                 start=(k == 0), stop=(k == KD - 1))
            if is_conv and (not is_r) and first_half:
                wt(t, "sac", ipb["silu_guard"])
                for k in range(KD):
                    i = t.matmul(bank(2)[:, 0:3], wsl[:, k * 128:(k + 1) * 128],
                                 sb["halolnT"][:, :].rearrange("p (k c) -> p k c", k=KD)[:, k, 0:3],
                                 start=(k == 0), stop=(k == KD - 1))
            inc("spe", i)
        ipb["pe_guard"][buf] = n_of("spe")

        if not is_conv:
            emit_dt()
            ipb["evac_guard"] = n_of("sdv")
            return

        xbuf = ipb["xbuf"]; ipb["xbuf"] ^= 1
        xb = rv(XBC, HALF + 8, BF16, off_b=xbuf * 2560)

        @block.vector
        def _(v, xb=xb, xbuf=xbuf, blk=blk, cb=cb):
            wt(v, "spe")
            if ipb["conv_pe"][xbuf]:
                wt(v, "spe", ipb["conv_pe"][xbuf])
            if is_r and first_half:
                v.memset(xb[:, 0:3], 0.0)
            elif is_r or (not first_half):
                v.tensor_copy(xb[:, 0:3], sb["halo"][:, blk * 4:blk * 4 + 3])
            else:
                v.tensor_scalar(xb[:, 0:3], bank(2)[:, 0:3], sb["flags"][:, 1:2], None, OP.mult)
            v.tensor_copy(xb[:, 3:3 + HALF], ps[:, 0:HALF])
            v.drain()
            v.tensor_copy(sb["halo"][:, blk * 4:blk * 4 + 3], xb[:, HALF:HALF + 3])
            for k in range(4):
                i = v.tensor_scalar(rv(DIAG, 512, BF16)[:, k * 128:(k + 1) * 128], sb["identbf"][:, :],
                                    sb["convw"][:, cb * 4 + k:cb * 4 + k + 1], None, OP.mult)
            v.drain()
            inc("sdv", i)
        ipb["evac_guard"] = n_of("sdv")

        @block.tensor
        def _(t, xb=xb):
            wt(t, "sdv")
            wt(t, "sac", ipb["silu_guard"])
            for tt in range(2):
                for k in range(4):
                    i = t.matmul(bank(2 + tt), rv(DIAG, 512, BF16)[:, k * 128:(k + 1) * 128],
                                 xb[:, tt * 512 + k: tt * 512 + k + 512],
                                 start=(k == 0), stop=(k == 3))
            inc("spe", i)
        ipb["conv_pe"][xbuf] = n_of("spe")

        @block.scalar
        def _(sc, dst_kb=dst_kb, cb=cb):
            wt(sc, "spe")
            if ipb["gate_io"]:
                wt(sc, "sga", ipb["gate_io"])
            dstv = rv(dst_kb[0], HALF, BF16, off_b=dst_kb[1])
            sc.activation(dstv[:, 0:512], bank(2), AF.Silu, bias=sb["convb"][:, cb:cb + 1])
            i = sc.activation(dstv[:, 512:1024], bank(3), AF.Silu, bias=sb["convb"][:, cb:cb + 1])
            inc("sac", i)
        ipb["silu_guard"] = n_of("sac")

    def emit_group_ssd(g, half, is_r):
        gbc = rv(GBC, 2 * HALF, BF16)
        xg = rv(XG, 8 * HALF, BF16)
        btm = rv(BTM, NCH * 128, BF16)
        scm = rv(SCM, NCH * 128, BF16)
        xtm = rv(XTM, HALF, BF16)
        xdt = rv(XDT, HALF, BF16)
        xdtd = rv(XDTD, HALF, BF16)
        tvec = rv(TVEC, GRP, F32)
        yo = rv(YO, GRP, F32)
        yg = rv(YG, GRP, BF16)
        gt2 = rv(GT2, GRP, BF16)
        nrm = rv(NRM, GRP, BF16)
        csrep = rv(CSREP, 2048, F32)
        lms = rv(LMS, 2048, BF16)

        for cq in range(2):
            @block.tensor
            def _(t, cq=cq):
                wt(t, "sac", ipb["silu_guard"])
                wt(t, "sdv")
                for c in range(cq * 4, cq * 4 + 4):
                    i = t.transpose(bankbf(4)[:, (c % 4) * 128:(c % 4) * 128 + 128],
                                    gbc[:, c * 128:(c + 1) * 128], sb["identbf"][:, :])
                    if not is_r:
                        i = t.matmul(bank(5)[:, (c % 4) * 128:(c % 4) * 128 + 128],
                                     gbc[:, c * 128:(c + 1) * 128],
                                     gbc[:, HALF + c * 128: HALF + (c + 1) * 128],
                                     start=True, stop=True)
                inc("spe", i)

            @block.vector
            def _(v, cq=cq):
                wt(v, "spe")
                for c in range(cq * 4, cq * 4 + 4):
                    i = v.tensor_copy(btm[:, c * 128:(c + 1) * 128],
                                      bankbf(4)[:, (c % 4) * 128:(c % 4) * 128 + 128])
                if not is_r:
                    for c in range(cq * 4, cq * 4 + 4):
                        i = v.scalar_tensor_tensor(scm[:, c * 128:(c + 1) * 128],
                                                   bank(5)[:, (c % 4) * 128:(c % 4) * 128 + 128],
                                                   1.0, sb["tril"][:, :], OP.mult, OP.mult)
                v.drain()
                inc("sdv", i)

        if not is_r:
            @block.vector
            def _(v):
                wt(v, "spl"); wt(v, "sdv")
                i = v.tensor_copy(rv(CBF, GRP, BF16)[:, :], sb["carry"][:, g * GRP:(g + 1) * GRP])
                v.drain()
                inc("sdv", i)

        gate_rd = [0]
        for c in range(NCH):
            @block.tensor
            def _(t, c=c):
                wt(t, "sdv")
                for j in range(8):
                    i = t.transpose(bankbf(4)[:, j * 128:(j + 1) * 128],
                                    xg[:, j * HALF + c * 128:j * HALF + (c + 1) * 128],
                                    sb["identbf"][:, :])
                inc("spe", i)

            @block.vector
            def _(v, c=c):
                wt(v, "spe")
                wt(v, "spl")
                v.tensor_copy(xtm[:, :], bankbf(4))
                v.drain()
                v.tensor_tensor(xdt.rearrange("p (h r) -> p h r", h=HPG),
                                xtm.rearrange("p (h r) -> p h r", h=HPG),
                                sb["dtc_tm"][:, c * 128 + g * HPG: c * 128 + g * HPG + HPG]
                                .rearrange("p (h o) -> p h o", o=1).broadcast_to([128, HPG, 64]), OP.mult)
                v.drain()
                i = v.tensor_tensor(xdtd.rearrange("p (h r) -> p h r", h=HPG),
                                    xdt.rearrange("p (h r) -> p h r", h=HPG),
                                    sb["dec_tm"][:, c * 128 + g * HPG: c * 128 + g * HPG + HPG]
                                    .rearrange("p (h o) -> p h o", o=1).broadcast_to([128, HPG, 64]), OP.mult)
                v.drain()
                inc("sdv", i)
            xdt_done = n_of("sdv")

            if not is_r:
                @block.tensor
                def _(t, c=c, xdt_done=xdt_done):
                    wt(t, "sdv", xdt_done)
                    for hh in range(2):
                        i = t.matmul(bank(hh), gbc[:, HALF + c * 128:HALF + (c + 1) * 128],
                                     rv(CBF, GRP, BF16)[:, hh * 512:(hh + 1) * 512],
                                     start=True, stop=True)
                    inc("spe", i)

            @block.tensor
            def _(t, c=c, xdt_done=xdt_done):
                wt(t, "sdv", xdt_done)
                for hh in range(2):
                    i = t.matmul(bank(2 + hh), btm[:, c * 128:(c + 1) * 128],
                                 xdtd[:, hh * 512:(hh + 1) * 512],
                                 start=True, stop=True)
                inc("spe", i)
            states_done = n_of("spe")

            @block.gpsimd
            def _(gp, c=c, states_done=states_done):
                wt(gp, "spe", states_done)
                wt(gp, "scd")
                i = gp.tensor_tensor(tvec.rearrange("p (h r) -> p h r", h=HPG),
                                     sb["carry"][:, g * GRP:(g + 1) * GRP].rearrange("p (h r) -> p h r", h=HPG),
                                     sb["cdrep"][:, c * 128 + g * HPG: c * 128 + g * HPG + HPG]
                                     .rearrange("p (h o) -> p h o", o=1).broadcast_to([128, HPG, 64]), OP.mult)
                inc("spl", i)

            @block.vector
            def _(v, c=c):
                wt(v, "spl")
                v.tensor_tensor(sb["carry"][:, g * GRP:(g + 1) * GRP], tvec[:, :],
                                bank2(2), OP.add)
                v.drain()
                if not is_r:
                    i = v.tensor_copy(rv(CBF, GRP, BF16)[:, :], sb["carry"][:, g * GRP:(g + 1) * GRP])
                    v.drain()
                else:
                    i = v.memset(sb["cpool"][:, 3:4], 0.0)
                inc("sdv", i)
            carry_done = n_of("sdv")

            if is_r:
                continue

            @block.sync
            def _(s, c=c, carry_done=carry_done):
                wt(s, "sdv", carry_done)
                inc("scr", s.dma_start(
                    out=csrep.rearrange("p (h l) -> p h l", h=HPG),
                    in_=cs_dram[g * HPG:(g + 1) * HPG, c * 128:(c + 1) * 128]
                    .rearrange("(o h) l -> o h l", o=1).broadcast_to([128, HPG, 128])), 16)

            @block.vector
            def _(v, c=c):
                wt(v, "scr")
                i = v.tensor_tensor(csrep.rearrange("p (h l) -> p h l", h=HPG),
                                    csrep.rearrange("p (h l) -> p h l", h=HPG),
                                    sb["cs_tm"][:, c * 128 + g * HPG: c * 128 + g * HPG + HPG]
                                    .rearrange("p (h o) -> p h o", o=1).broadcast_to([128, HPG, 128]), OP.subtract)
                v.drain()
                inc("sdv", i)

            @block.scalar
            def _(sc, c=c):
                wt(sc, "sdv")
                inc("sac", sc.activation(lms[:, :], csrep[:, :], AF.Exp))

            @block.vector
            def _(v, c=c):
                wt(v, "sac")
                i = v.scalar_tensor_tensor(lms.rearrange("p (h l) -> p h l", h=HPG),
                                           lms.rearrange("p (h l) -> p h l", h=HPG), 1.0,
                                           scm[:, c * 128:(c + 1) * 128]
                                           .unsqueeze(1).broadcast_to([128, HPG, 128]),
                                           OP.min, OP.mult)
                v.drain()
                inc("sdv", i)
            lms_done = n_of("sdv")

            @block.tensor
            def _(t, c=c, lms_done=lms_done):
                wt(t, "sdv", lms_done)
                for h in range(HPG):
                    i = t.matmul(ps[:, 6 * 512 + h * 64: 6 * 512 + (h + 1) * 64],
                                 lms[:, h * 128:(h + 1) * 128],
                                 xdt[:, h * 64:(h + 1) * 64], start=True, stop=True)
                inc("spe", i)
            ydiag_done = n_of("spe")

            @block.sync
            def _(s, c=c):
                if gate_rd[0]:
                    wt(s, "sdv", gate_rd[0])
                wt(s, "sga", ipb["gate_io"])
                inc("sgt", s.dma_start(out=gt2[:, :],
                                       in_=gate_sp[half * HALF + c * 128: half * HALF + (c + 1) * 128,
                                                   g * GRP:(g + 1) * GRP]), 16)

            @block.vector
            def _(v, c=c, ydiag_done=ydiag_done):
                wt(v, "spe", ydiag_done)
                wt(v, "sgt")
                v.tensor_tensor(yo.rearrange("p (h r) -> p h r", h=HPG),
                                bank2(0).rearrange("p (h r) -> p h r", h=HPG),
                                sb["ecs_tm"][:, c * 128 + g * HPG: c * 128 + g * HPG + HPG]
                                .rearrange("p (h o) -> p h o", o=1).broadcast_to([128, HPG, 64]), OP.mult)
                v.drain()
                i = v.tensor_tensor(yo[:, :], yo[:, :], ps[:, 6 * 512:8 * 512], OP.add)
                v.drain()
                inc("sdv", i)

            @block.gpsimd
            def _(gp, c=c):
                wt(gp, "sdv")
                i = gp.tensor_tensor(tvec.rearrange("p (h r) -> p h r", h=HPG),
                                     xtm.rearrange("p (h r) -> p h r", h=HPG),
                                     sb["drep"][:, g * HPG:(g + 1) * HPG]
                                     .rearrange("p (h o) -> p h o", o=1).broadcast_to([128, HPG, 64]), OP.mult)
                inc("spl", i)

            @block.vector
            def _(v, c=c):
                wt(v, "spl")
                wt(v, "snr")
                v.tensor_tensor(yo[:, :], yo[:, :], tvec[:, :], OP.add)
                v.drain()
                i = v.tensor_tensor(yg[:, :], yo[:, :], gt2[:, :], OP.mult)
                v.drain()
                inc("sdv", i)
            gate_rd[0] = n_of("sdv")

            @block.scalar
            def _(sc, c=c):
                wt(sc, "sdv")
                sc.activation(rv(SQS, GRP, BF16)[:, :], yg[:, :], AF.Square, accum_out=sb["ssq"][:, 0:1])
                sc.drain()
                sc.activation(sb["ssq"][:, 1:2], sb["ssq"][:, 0:1], AF.Ln, scale=1.0 / GRP, bias=EPS)
                sc.drain()
                inc("sac", sc.activation(sb["ssq"][:, 2:3], sb["ssq"][:, 1:2], AF.Exp, scale=-0.5))

            @block.vector
            def _(v, c=c):
                wt(v, "sac")
                wt(v, "sio")
                i = v.tensor_scalar(nrm[:, :], yg[:, :], sb["ssq"][:, 2:3], None, OP.mult)
                v.drain()
                inc("sdv", i)

            @block.sync
            def _(s, c=c):
                wt(s, "sdv")
                inc("snr", s.dma_start(
                    out=normed_sp[half * HALF + c * 128: half * HALF + (c + 1) * 128,
                                  g * GRP:(g + 1) * GRP],
                    in_=nrm[:, :]), 16)

    def do_pass(is_r, half):
        src = x_pre if is_r else x_own
        emit_ln(src, half * HALF, rv(LN_XT, KD * HALF, BF16), HALF, 0)
        ipb["lnguard"] = n_of("sdv")
        ipb["gate_io"] = 0
        if (not is_r) and KSTAGE >= 3:
            emit_gate(half)
            ipb["gate_io"] = n_of("sga")
            ipb["pe_guard"] = [n_of("spe"), n_of("spe")]
            ipb["bank_sac"] = n_of("sac")
        emit_inproj_block(0, None, is_r, half == 0)
        if KSTAGE < 2:
            return
        for g in range(NG):
            base = 1 + g * 10
            emit_inproj_block(base, (GBC, 0), is_r, half == 0)
            if not is_r:
                emit_inproj_block(base + 1, (GBC, HALF * 2), is_r, half == 0)
            for j in range(8):
                emit_inproj_block(base + 2 + j, (XG, j * HALF * 2), is_r, half == 0)
            if KSTAGE >= 4:
                emit_group_ssd(g, half, is_r)

    @block.vector
    def _(v):
        wt(v, "sio")
        i = v.memset(sb["carry"][:, :], 0.0)
        v.drain()
        inc("sdv", i)

    if KSTAGE >= 5:
        do_pass(True, 0)
        do_pass(True, 1)

        @block.vector
        def _(v):
            wt(v, "spl"); wt(v, "sdv")
            i = v.tensor_scalar(sb["carry"][:, :], sb["carry"][:, :], sb["flags"][:, 0:1], None, OP.mult)
            v.drain()
            inc("sdv", i)

        if "cin" in dbg_taps:
            dbg_dump("cin", sb["carry"][:, 0:4096], 4096)

    do_pass(False, 0)
    if KSTAGE >= 6:
        do_pass(False, 1)

    if "dtc" in dbg_taps:
        dbg_dump("dtc", rv(YO, HALF, F32)[:, :], HALF)
    if "cs" in dbg_taps:
        dbg_dump("cs", rv(TVEC, HALF, F32)[:, :], HALF)
    if "carry" in dbg_taps:
        dbg_dump("carry", sb["carry"][:, 0:4096], 4096)
    if "xg" in dbg_taps:
        dbg_dump("xg", rv(XG, 8 * HALF, BF16).bitcast(F32), 4096)
    if "bc" in dbg_taps:
        dbg_dump("bc", rv(GBC, 2 * HALF, BF16).bitcast(F32), 1024)
    if "y" in dbg_taps:
        dbg_dump("y", rv(YO, GRP, F32), 1024)
    if "nrm" in dbg_taps:
        dbg_dump("nrm", rv(YG, GRP, BF16).bitcast(F32), 512)

    NTQ, NSL, WS3, T3A, T3B = 0, 64, 80, 88, 91
    if KSTAGE >= 7:
        slab_guard = [0, 0]
        for tq in range(4):
            nT = rv(NTQ, 64 * 512, BF16)
            tile_guard = [0]
            for tb in range(4):
                @block.sync
                def _(s, tq=tq, tb=tb):
                    if tile_guard[0]:
                        wt(s, "spe", tile_guard[0])
                    else:
                        wt(s, "spe")
                    wt(s, "sdv")
                    wt(s, "snr")
                    inc("snl", s.dma_start(out=rv(NSL, INTER, BF16),
                                           in_=normed_sp[tq * 512 + tb * 128: tq * 512 + (tb + 1) * 128, :]), 16)

                for kq in range(2):
                    @block.tensor
                    def _(t, tb=tb, kq=kq):
                        wt(t, "snl")
                        wt(t, "sdv")
                        for k32 in range(32):
                            k = kq * 32 + k32
                            i = t.transpose(bankbf(k32 // 8)[:, (k % 8) * 128:(k % 8) * 128 + 128],
                                            rv(NSL, INTER, BF16)[:, k * 128:(k + 1) * 128],
                                            sb["identbf"][:, :])
                        inc("spe", i)
                    if kq == 1:
                        tile_guard[0] = n_of("spe")

                    @block.vector
                    def _(v, tb=tb, kq=kq):
                        wt(v, "spe")
                        for k32 in range(32):
                            k = kq * 32 + k32
                            i = v.tensor_scalar(
                                nT.rearrange("p (k c) -> p k c", k=64)[:, k, tb * 128:(tb + 1) * 128],
                                bankbf(k32 // 8)[:, (k % 8) * 128:(k % 8) * 128 + 128],
                                sb["normw"][:, k:k + 1], None, OP.mult)
                        v.drain()
                        inc("sdv", i)

            for ob in range(8):
                for ks in range(16):
                    buf = ks % 2
                    wsl = rv(WS3, 4 * 512, BF16, off_b=buf * 4 * 1024)

                    wsem = "swa" if buf == 0 else "swb"

                    @block.gpsimd
                    def _(g, ks=ks, ob=ob, buf=buf, wsl=wsl, wsem=wsem):
                        wt(g, "spe", slab_guard[buf] if slab_guard[buf] else n_of("spe"))
                        inc(wsem, g.dma_start(out=wsl.rearrange("p (k f) -> p k f", k=4),
                                              in_=w_out[ob, ks * 4:(ks + 1) * 4].rearrange("k d f -> d k f")), 16)

                    @block.tensor
                    def _(t, ks=ks, ob=ob, wsl=wsl, wsem=wsem):
                        wt(t, wsem)
                        if ks == 0:
                            wt(t, "sdv")
                        for tb in range(4):
                            for k4 in range(4):
                                k = ks * 4 + k4
                                i = t.matmul(bank(tb),
                                             nT.rearrange("p (k c) -> p k c", k=64)[:, k, tb * 128:(tb + 1) * 128],
                                             wsl[:, k4 * 512:(k4 + 1) * 512],
                                             start=(k == 0), stop=(k == 63))
                        inc("spe", i)
                    slab_guard[buf] = n_of("spe")
                pe_last = n_of("spe")

                for tb in range(4):
                    row0 = tq * 512 + tb * 128

                    @block.sync
                    def _(s, tb=tb, ob=ob, row0=row0):
                        wt(s, "sdv")
                        inc("sra", s.dma_start(out=rv(T3A, 512, F32),
                                               in_=x_own[row0:row0 + 128, ob * 512:(ob + 1) * 512]), 16)

                    @block.vector
                    def _(v, tb=tb, pe_last=pe_last):
                        wt(v, "spe", pe_last)
                        wt(v, "sra")
                        wt(v, "srb")
                        i = v.tensor_tensor(rv(T3B, 512, F32), bank(tb), rv(T3A, 512, F32), OP.add)
                        v.drain()
                        inc("sdv", i)

                    @block.sync
                    def _(s, tb=tb, ob=ob, row0=row0):
                        wt(s, "sdv")
                        inc("srb", s.dma_start(out=x2_sp[row0:row0 + 128, ob * 512:(ob + 1) * 512],
                                               in_=rv(T3B, 512, F32)), 16)

    LN2Q, AFL4, BFL4, LNO4, WS4, T4A, T4B = 0, 64, 72, 80, 88, 120, 123
    BBR, WGT, HWT = 126, 134, 136
    if KSTAGE >= 8:
        ln2 = rv(LN2Q, KD * HALF, BF16)

        @block.gpsimd
        def _(g):
            wt(g, "spe")
            inc("swc", g.dma_start(out=rv(AFL4, KD * 128, BF16).rearrange("p (k e) -> p k e", k=KD),
                                  in_=a_flat.rearrange("k d e -> d k e")), 16)
            wt(g, "swc")
            inc("swc", g.dma_start(out=rv(BFL4, D, BF16), in_=b_flat), 16)
            wt(g, "swc")
            inc("swc", g.dma_start(out=region[0:1, BBR * 1024:BBR * 1024 + D * 2].bitcast(BF16),
                                  in_=b_base.rearrange("(o e) -> o e", o=1)), 16)
            wt(g, "swc")
            inc("swc", g.dma_start(out=rv(WGT, KD * 16, BF16).rearrange("p (k e) -> p k e", k=KD),
                                  in_=wg_thr.rearrange("k d e -> d k e")), 16)
            wt(g, "swc")

        slab4 = [0, 0]
        for tbs in range(2):
            lnog = [0]
            bankg = [0]
            xtg = {}
            lnob = rv(LNO4, D, BF16)
            for tb8 in range(8):
                tb = tbs * 8 + tb8
                buf = tb8 % 2
                xt = rv(T4A, 0, F32) if False else rv(NSL, 2 * D, F32, off_b=buf * D * 4)
                xt = rv(WS4, D, F32, off_b=buf * D * 4)

                @block.sync
                def _(s, tb=tb, xt=xt, tb8=tb8):
                    if tb8 - 2 in xtg:
                        wt(s, "sdv", xtg[tb8 - 2])
                    if tb8 < 2:
                        wt(s, "spe")
                        wt(s, "sdv")
                        wt(s, "srb")
                    inc("sxa" if tb8 % 2 == 0 else "sxb",
                        s.dma_start(out=xt[:, :], in_=x2_sp[tb * 128:(tb + 1) * 128, :]), 16)

                @block.vector
                def _(v, xt=xt, tb8=tb8):
                    wt(v, "sxa" if tb8 % 2 == 0 else "sxb")
                    v.reduce_sum(sb["stats"][:, 0:1], xt[:, :], AX.X)
                    v.drain()
                    v.tensor_scalar(sb["stats"][:, 1:2], sb["stats"][:, 0:1], -1.0 / D, None, OP.mult)
                    v.drain()
                    i = v.tensor_scalar(xt[:, :], xt[:, :], sb["stats"][:, 1:2], None, OP.add)
                    v.drain()
                    inc("sdv", i)

                @block.scalar
                def _(sc, xt=xt):
                    wt(sc, "sdv")
                    for q in range(4):
                        sc.activation(lnob[:, q * 1024:(q + 1) * 1024], xt[:, q * 1024:(q + 1) * 1024],
                                      AF.Square, accum_out=sb["stats"][:, 2 + q:3 + q])
                    sc.drain()
                    inc("sac", sc.activation(sb["cpool"][:, 1:2], sb["cpool"][:, 0:1], AF.Copy))

                @block.vector
                def _(v):
                    wt(v, "sac")
                    i = v.reduce_sum(sb["stats"][:, 6:7], sb["stats"][:, 2:6], AX.X)
                    v.drain()
                    inc("sdv", i)

                @block.scalar
                def _(sc):
                    wt(sc, "sdv")
                    sc.activation(sb["stats"][:, 7:8], sb["stats"][:, 6:7], AF.Ln, scale=1.0 / D, bias=EPS)
                    sc.drain()
                    inc("sac", sc.activation(sb["stats"][:, 7:8], sb["stats"][:, 7:8], AF.Exp, scale=-0.5))

                @block.vector
                def _(v, xt=xt):
                    wt(v, "sac")
                    if lnog[0]:
                        wt(v, "spe", lnog[0])
                    i = v.tensor_scalar(lnob[:, :], xt[:, :], sb["stats"][:, 7:8], None, OP.mult)
                    v.drain()
                    inc("sdv", i)
                xtg[tb8] = n_of("sdv")

                @block.tensor
                def _(t):
                    wt(t, "sdv")
                    if bankg[0]:
                        wt(t, "sdv", bankg[0])
                    for k in range(KD):
                        i = t.transpose(bankbf(k // 8)[:, (k % 8) * 128:(k % 8) * 128 + 128],
                                        lnob[:, k * 128:(k + 1) * 128], sb["identbf"][:, :])
                        if k == KD - 1:
                            inc("spe", i)
                lnog[0] = n_of("spe")

                @block.vector
                def _(v, tb8=tb8):
                    wt(v, "spe")
                    for b4 in range(4):
                        i = v.tensor_copy(
                            ln2.rearrange("p (k c) -> p k c", k=KD)[:, b4 * 8:(b4 + 1) * 8,
                                                                    tb8 * 128:(tb8 + 1) * 128],
                            bankbf(b4).rearrange("p (k c) -> p k c", k=8))
                        if b4 == 3:
                            inc("sdv", i)
                bankg[0] = n_of("sdv")

                @block.tensor
                def _(t, tb8=tb8):
                    wt(t, "sdv"); wt(t, "swc")
                    for k in range(KD):
                        t.matmul(ps[:, 4 * 512:4 * 512 + 16],
                                 ln2.rearrange("p (k c) -> p k c", k=KD)[:, k, tb8 * 128:(tb8 + 1) * 128],
                                 rv(WGT, KD * 16, BF16)[:, k * 16:(k + 1) * 16],
                                 start=(k == 0), stop=(k == KD - 1))
                    for k in range(KD):
                        i = t.matmul(ps[:, 5 * 512:5 * 512 + 128],
                                     ln2.rearrange("p (k c) -> p k c", k=KD)[:, k, tb8 * 128:(tb8 + 1) * 128],
                                     rv(AFL4, KD * 128, BF16)[:, k * 128:(k + 1) * 128],
                                     start=(k == 0), stop=(k == KD - 1))
                    inc("spe", i)

                @block.vector
                def _(v):
                    wt(v, "spe")
                    f = sb["ffn"]
                    v.tensor_copy(f[:, 0:16], ps[:, 4 * 512:4 * 512 + 16])
                    v.drain()
                    v.reduce_max(f[:, 16:17], f[:, 0:8], AX.X)
                    v.drain()
                    i = v.tensor_scalar(f[:, 24:32], f[:, 0:8], f[:, 16:17], None, OP.subtract)
                    v.drain()
                    inc("sdv", i)

                @block.scalar
                def _(sc):
                    wt(sc, "sdv")
                    sc.activation(sb["ffn"][:, 32:40], sb["ffn"][:, 24:32], AF.Exp)
                    sc.drain()
                    inc("sac", sc.activation(sb["ffn"][:, 17:18], sb["ffn"][:, 8:9], AF.Tanh,
                                             scale=0.5, bias=sb["bthr2"][:, 0:1]))

                @block.vector
                def _(v):
                    wt(v, "sac")
                    f = sb["ffn"]
                    v.reduce_sum(f[:, 18:19], f[:, 32:40], AX.X)
                    v.drain()
                    v.reciprocal(f[:, 19:20], f[:, 18:19])
                    v.drain()
                    v.tensor_scalar(f[:, 40:48], f[:, 32:40], f[:, 19:20], None, OP.mult)
                    v.drain()
                    v.tensor_scalar(f[:, 20:21], f[:, 17:18], 0.0625, 0.0625, OP.mult, OP.add)
                    v.drain()
                    v.tensor_scalar(f[:, 48:56], f[:, 40:48], f[:, 20:21], None, OP.subtract)
                    v.drain()
                    v.tensor_scalar(f[:, 48:56], f[:, 48:56], 0.0, None, OP.max)
                    v.drain()
                    v.reduce_sum(f[:, 21:22], f[:, 48:56], AX.X)
                    v.drain()
                    v.tensor_scalar(f[:, 22:23], f[:, 21:22], 0.0, None, OP.is_equal)
                    v.drain()
                    v.tensor_tensor(f[:, 21:22], f[:, 21:22], f[:, 22:23], OP.add)
                    v.drain()
                    v.reciprocal(f[:, 23:24], f[:, 21:22])
                    v.drain()
                    v.tensor_scalar(f[:, 48:56], f[:, 48:56], f[:, 23:24], 2.0, OP.mult, OP.mult)
                    v.drain()
                    i = v.tensor_tensor(rv(T4A, 128, BF16)[:, 0:128].rearrange("p (e r) -> p e r", e=8),
                                        ps[:, 5 * 512:5 * 512 + 128].rearrange("p (e r) -> p e r", e=8),
                                        f[:, 48:56].rearrange("p (e o) -> p e o", o=1).broadcast_to([128, 8, 16]), OP.mult)
                    v.drain()
                    inc("sdv", i)

                @block.tensor
                def _(t):
                    wt(t, "sdv")
                    i = t.transpose(bankbf(6)[:, 0:128], rv(T4A, 128, BF16)[:, 0:128], sb["identbf"][:, :])
                    inc("spe", i)

                @block.vector
                def _(v, tb=tb):
                    wt(v, "spe")
                    i = v.tensor_copy(rv(HWT, 16 * 128, BF16)[:, tb * 128:(tb + 1) * 128], bankbf(6)[:, 0:128])
                    v.drain()
                    inc("sdv", i)

            for ob in range(8):
                for ks in range(4):
                    buf = ks % 2
                    wsl = rv(WS4, 8 * 512, BF16, off_b=buf * 8 * 1024)

                    wsem = "swa" if buf == 0 else "swb"

                    @block.gpsimd
                    def _(g, ob=ob, ks=ks, buf=buf, wsl=wsl, wsem=wsem):
                        if slab4[buf]:
                            wt(g, "spe", slab4[buf])
                        if ob == 0 and ks < 2:
                            wt(g, "sac")
                            wt(g, "sdv")
                        inc(wsem, g.dma_start(out=wsl.rearrange("p (k f) -> p k f", k=8),
                                              in_=w_base[ob, ks * 8:(ks + 1) * 8].rearrange("k d f -> d k f")), 16)

                    @block.tensor
                    def _(t, ob=ob, tbs=tbs, ks=ks, wsl=wsl, wsem=wsem):
                        wt(t, wsem)
                        if ks == 0:
                            wt(t, "sdv")
                        for tb8 in range(8):
                            for k8 in range(8):
                                k = ks * 8 + k8
                                i = t.matmul(bank(tb8),
                                             ln2.rearrange("p (k c) -> p k c", k=KD)[:, k, tb8 * 128:(tb8 + 1) * 128],
                                             wsl[:, k8 * 512:(k8 + 1) * 512],
                                             start=(k == 0), stop=False)
                        inc("spe", i)
                    slab4[buf] = n_of("spe")

                @block.tensor
                def _(t, ob=ob, tbs=tbs):
                    wt(t, "swc")
                    for tb8 in range(8):
                        tb = tbs * 8 + tb8
                        t.matmul(bank(tb8), sb["onesbf"][0:1, 0:128],
                                 region[0:1, BBR * 1024:BBR * 1024 + D * 2].bitcast(BF16)[0:1, ob * 512:(ob + 1) * 512],
                                 start=False, stop=False)
                        i = t.matmul(bank(tb8), rv(HWT, 16 * 128, BF16)[:, tb * 128:(tb + 1) * 128],
                                     rv(BFL4, D, BF16)[:, ob * 512:(ob + 1) * 512],
                                     start=False, stop=True)
                    inc("spe", i)
                mm_done = n_of("spe")

                for tb8 in range(8):
                    tb = tbs * 8 + tb8
                    row0 = tb * 128

                    @block.sync
                    def _(s, ob=ob, row0=row0):
                        wt(s, "sdv")
                        inc("sra", s.dma_start(out=rv(T4A, 512, F32),
                                               in_=x2_sp[row0:row0 + 128, ob * 512:(ob + 1) * 512]), 16)

                    @block.vector
                    def _(v, tb8=tb8, mm_done=mm_done):
                        wt(v, "spe", mm_done)
                        wt(v, "sra")
                        wt(v, "srb")
                        i = v.tensor_tensor(rv(T4B, 512, F32), bank(tb8), rv(T4A, 512, F32), OP.add)
                        v.drain()
                        inc("sdv", i)

                    @block.sync
                    def _(s, ob=ob, row0=row0):
                        wt(s, "sdv")
                        inc("srb", s.dma_start(out=y_out[row0:row0 + 128, ob * 512:(ob + 1) * 512],
                                               in_=rv(T4B, 512, F32)), 16)

    @block.sync
    def _(s):
        for nm in ["sio", "sw", "swa", "swb", "swc", "sga", "snr", "srb", "scd", "sra",
                   "sxa", "sxb", "scr", "sgt", "snl"]:
            wt(s, nm)

    @block.vector
    def _(v):
        wt(v, "srb")

    block_cm.__exit__(None, None, None)
    cm.close()
    return nc, DBG



def _stage(inputs):
    ip = np.ascontiguousarray(inputs["L0_attn_in_proj_w"])
    rows = [np.arange(INTER + CONV_DIM, PROJ)]
    for g in range(NG):
        rows.append(np.arange(2 * INTER + g * SS, 2 * INTER + (g + 1) * SS))
        rows.append(np.arange(2 * INTER + NG * SS + g * SS, 2 * INTER + NG * SS + (g + 1) * SS))
        for j in range(8):
            rows.append(np.arange(INTER + g * GRP + j * 128, INTER + g * GRP + (j + 1) * 128))
    ridx = np.concatenate(rows)
    w_sub = ip[ridx]
    w_xbcdt_s = np.ascontiguousarray(w_sub.reshape(N_BLK, 128, KD, 128).transpose(0, 2, 3, 1))
    w_gate_s = np.ascontiguousarray(ip[0:INTER].reshape(16, 512, KD, 128).transpose(0, 2, 3, 1))
    wo = np.ascontiguousarray(inputs["L0_attn_out_proj_w"])
    w_out_s = np.ascontiguousarray(wo.T.reshape(64, 128, 8, 512).transpose(2, 0, 1, 3))
    wb = np.ascontiguousarray(inputs["L0_ffn_W_base"])
    w_base_s = np.ascontiguousarray(wb.T.reshape(KD, 128, 8, 512).transpose(2, 0, 1, 3))

    cw = np.ascontiguousarray(inputs["L0_attn_conv_w"]).reshape(CONV_DIM, 4)
    cb_ = np.ascontiguousarray(inputs["L0_attn_conv_b"])
    crows = []
    for g in range(NG):
        crows.append(np.arange(INTER + g * SS, INTER + (g + 1) * SS))
        crows.append(np.arange(INTER + NG * SS + g * SS, INTER + NG * SS + (g + 1) * SS))
        for j in range(8):
            crows.append(np.arange(g * GRP + j * 128, g * GRP + (j + 1) * 128))
    cidx = np.concatenate(crows)
    conv_taps_s = np.ascontiguousarray(cw[cidx].reshape(80, 128, 4))
    conv_bias_s = np.ascontiguousarray(cb_[cidx].reshape(80, 128))

    wg = np.ascontiguousarray(inputs["L0_ffn_W_gate"])
    wthr = np.ascontiguousarray(inputs["L0_ffn_W_thr"])
    wgt = np.zeros((D, 16), np.float32)
    wgt[:, 0:8] = wg.T
    wgt[:, 8:9] = wthr.T
    wg_thr_s = np.ascontiguousarray(wgt.reshape(KD, 128, 16))

    A = np.ascontiguousarray(inputs["L0_ffn_A"])
    a_flat_s = np.ascontiguousarray(A.reshape(128, D).T.reshape(KD, 128, 128))
    Bm = np.ascontiguousarray(inputs["L0_ffn_B"])
    b_flat_s = np.ascontiguousarray(Bm.transpose(0, 2, 1).reshape(128, D))

    norm_w_s = np.ascontiguousarray(inputs["L0_attn_norm_w"]).reshape(64, 128)

    common = dict(
        w_xbcdt=w_xbcdt_s, w_gate=w_gate_s, w_out=w_out_s, w_base=w_base_s,
        conv_taps=conv_taps_s, conv_bias=conv_bias_s,
        dt_bias=np.ascontiguousarray(inputs["L0_attn_dt_bias"]),
        d_vec=np.ascontiguousarray(inputs["L0_attn_D"]),
        norm_w=norm_w_s, wg_thr=wg_thr_s,
        b_thr=np.ascontiguousarray(inputs["L0_ffn_b_thr"]),
        b_base=np.ascontiguousarray(inputs["L0_ffn_b_base"]),
        a_flat=a_flat_s, b_flat=b_flat_s,
    )

    x = np.ascontiguousarray(inputs["L0_attn_hidden_states"])
    in_maps = []
    for c in range(8):
        b, h = c // 2, c % 2
        m = dict(common)
        m["x_own"] = np.ascontiguousarray(x[b, h * TOK:(h + 1) * TOK])
        m["x_pre"] = np.ascontiguousarray(x[b, 0:TOK])
        m["x_halo"] = np.ascontiguousarray(x[b, TOK - 3:TOK]) if h == 1 else np.ascontiguousarray(x[b, 0:3])
        fl = np.zeros((128, 4), np.float32)
        fl[:, 0] = float(h)
        fl[:, 1] = float(h)
        m["flags"] = fl
        in_maps.append(m)
    return in_maps


_CACHE = {}


def kernel(**inputs):
    taps = tuple(t for t in os.environ.get("KDBG", "").split(",") if t)
    nc, DBG = build_program(taps)
    in_maps = _stage(inputs)
    res = run_bass_kernel_spmd(nc, in_maps, core_ids=list(range(8)))
    _CACHE["res"] = res
    _CACHE["DBG"] = DBG
    out = np.zeros((4, 4096, D), np.float32)
    for c in range(8):
        b, h = c // 2, c % 2
        out[b, h * TOK:(h + 1) * TOK] = res.results[c]["y_out"]
    return out


# version 03
# speedup vs baseline: 1.0204x; 1.0204x over previous
import os
import numpy as np
import ml_dtypes
import concourse.bass as bass
import concourse.mybir as mybir
from concourse.bass_utils import run_bass_kernel_spmd

F32 = mybir.dt.float32
BF16 = mybir.dt.bfloat16
U8 = mybir.dt.uint8
AF = mybir.ActivationFunctionType
OP = mybir.AluOpType
AX = mybir.AxisListType

BF = ml_dtypes.bfloat16

D = 4096
NH, HD, SS, NG, CHUNK = 128, 64, 128, 8, 128
INTER = NH * HD
CONV_DIM = INTER + 2 * NG * SS
PROJ = INTER + CONV_DIM + NH
TOK = 2048
HALF = 1024
NCH = HALF // CHUNK
GRP = INTER // NG
HPG = NH // NG
KD = D // 128
EPS = 1e-5
DT_MIN, DT_MAX = 1e-3, 100.0
N_BLK = 1 + NG * 10



def build_program(dbg_taps=()):
    nc = bass.Bass("TRN2", target_bir_lowering=False)
    AP_IN = dict(kind="ExternalInput")

    x_own = nc.dram_tensor("x_own", [TOK, D], F32, **AP_IN).ap()
    x_pre = nc.dram_tensor("x_pre", [TOK, D], F32, **AP_IN).ap()
    x_halo = nc.dram_tensor("x_halo", [3, D], F32, **AP_IN).ap()
    flags = nc.dram_tensor("flags", [128, 4], F32, **AP_IN).ap()
    w_xbcdt = nc.dram_tensor("w_xbcdt", [N_BLK, KD, 128, 128], F32, **AP_IN).ap()
    w_gate = nc.dram_tensor("w_gate", [16, KD, 128, 512], F32, **AP_IN).ap()
    w_out = nc.dram_tensor("w_out", [8, 64, 128, 512], F32, **AP_IN).ap()
    w_base = nc.dram_tensor("w_base", [8, KD, 128, 512], F32, **AP_IN).ap()
    conv_taps = nc.dram_tensor("conv_taps", [80, 128, 4], F32, **AP_IN).ap()
    conv_bias = nc.dram_tensor("conv_bias", [80, 128], F32, **AP_IN).ap()
    dt_bias = nc.dram_tensor("dt_bias", [128], F32, **AP_IN).ap()
    d_vec = nc.dram_tensor("d_vec", [128], F32, **AP_IN).ap()
    norm_w = nc.dram_tensor("norm_w", [64, 128], F32, **AP_IN).ap()
    wg_thr = nc.dram_tensor("wg_thr", [KD, 128, 16], F32, **AP_IN).ap()
    b_thr = nc.dram_tensor("b_thr", [1], F32, **AP_IN).ap()
    b_base = nc.dram_tensor("b_base", [D], F32, **AP_IN).ap()
    a_flat = nc.dram_tensor("a_flat", [KD, 128, 128], F32, **AP_IN).ap()
    b_flat = nc.dram_tensor("b_flat", [128, D], F32, **AP_IN).ap()

    y_out = nc.dram_tensor("y_out", [TOK, D], F32, kind="ExternalOutput").ap()
    dbg = nc.dram_tensor("dbg", [128, 8192], F32, kind="ExternalOutput").ap()

    w_out_bf = nc.dram_tensor("w_out_bf", [8, 64, 128, 512], BF16, kind="Internal").ap()
    w_base_bf = nc.dram_tensor("w_base_bf", [8, KD, 128, 512], BF16, kind="Internal").ap()
    gate_sp = nc.dram_tensor("gate_sp", [TOK, INTER], BF16, kind="Internal").ap()
    normed_sp = nc.dram_tensor("normed_sp", [TOK, INTER], BF16, kind="Internal").ap()
    x2_sp = nc.dram_tensor("x2_sp", [TOK, D], F32, kind="Internal").ap()
    cs_dram = nc.dram_tensor("cs_dram", [NH, HALF], F32, kind="Internal").ap()
    cdexp_dram = nc.dram_tensor("cdexp_dram", [NCH, NH], F32, kind="Internal").ap()

    ident_bf_d = nc.inline_tensor(np.eye(128, dtype=BF), name="identbf_c").ap()
    ident_f_d = nc.inline_tensor(np.eye(128, dtype=np.float32), name="identf_c").ap()
    tril_d = nc.inline_tensor(
        (np.arange(128)[None, :] >= np.arange(128)[:, None]).astype(BF), name="tril_c").ap()
    avec_d = nc.inline_tensor(-(np.arange(128, dtype=np.float32) + 1.0)[:, None], name="avec_c").ap()
    onesbf_d = nc.inline_tensor(np.ones((1, 128), dtype=BF), name="onesbf_c").ap()

    sb = {}

    def alloc(name, cols, dtype):
        sb[name] = nc.alloc_sbuf_tensor("s_" + name, [128, cols], dtype).ap()
        return sb[name]

    alloc("identbf", 128, BF16)
    alloc("identf", 128, F32)
    alloc("tril", 128, BF16)
    alloc("avec", 1, F32)
    alloc("onesbf", 128, BF16)
    alloc("flags", 4, F32)
    alloc("dtb", 1, F32)
    alloc("bthr2", 1, F32)
    alloc("drep", 128, F32)
    alloc("convw", 80 * 4, F32)
    alloc("convb", 80, F32)
    alloc("normw", 64, F32)
    alloc("carry", NG * GRP, F32)
    alloc("halo", N_BLK * 4, BF16)
    alloc("halolnT", KD * 4, BF16)
    alloc("dtc_tm", NCH * 128, F32)
    alloc("cs_tm", NCH * 128, F32)
    alloc("dec_tm", NCH * 128, F32)
    alloc("ecs_tm", NCH * 128, F32)
    alloc("cdrep", NCH * 128, F32)
    alloc("cslast", NCH, F32)
    alloc("stats", 16, F32)
    alloc("cpool", 4, F32)
    alloc("tiny", 64, F32)
    alloc("ssq", 8, F32)
    alloc("ffn", 64, F32)
    alloc("sink", 1024, BF16)
    sb["bthr_raw"] = nc.alloc_sbuf_tensor("s_bthr_raw", [128, 1], F32).ap()

    avail_kb = (nc.sbuf_bytes_remaining - 512) // 1024
    assert avail_kb >= 142, avail_kb
    region = nc.alloc_sbuf_tensor("s_region", [128, avail_kb * 1024], U8).ap()

    def rv(off_kb, cols, dtype, off_b=0):
        esz = {F32: 4, BF16: 2}[dtype]
        o = off_kb * 1024 + off_b
        return region[:, o:o + cols * esz].bitcast(dtype)

    LN_XT = 0
    XTILE = 64
    LNOUT = 96
    GWSL = 64
    GATET = 80
    WBLK = 64
    XG = 80
    GBC = 96
    BTM = 100
    SCM = 102
    XBC = 104
    DIAG = 109
    CSREP = 110
    LMS = 118
    XTM = 122
    XDT = 124
    XDTD = 126
    TVEC = 128
    YO = 132
    YG = 136
    GT2 = 138
    NRM = 136
    SQS = 118
    DTT = 110
    CBF = 140

    ps = nc.alloc_psum_tensor("ps", [128, 4096], F32).ap()

    def bank(i):
        return ps[:, i * 512:(i + 1) * 512]

    def bank2(i):
        return ps[:, i * 512:(i + 2) * 512]

    def bankbf(i, cols=1024):
        return ps.bitcast(BF16)[:, i * 1024:i * 1024 + cols]

    import contextlib
    cm = contextlib.ExitStack()
    sems = {}
    for n in ["sio", "sw", "spe", "sdv", "sac", "spl",
              "swa", "swb", "swc", "swd",
              "sxa", "sxb", "scr", "scd", "sgt", "snr", "sga", "sra", "srb", "snl"]:
        sems[n] = [cm.enter_context(nc.semaphore(n)), 0]

    def inc(name, instr, k=1):
        instr.then_inc(sems[name][0], k)
        sems[name][1] += k
        return sems[name][1]

    def n_of(name):
        return sems[name][1]

    def wt(eng, name, val=None):
        v = n_of(name) if val is None else val
        if v > 0:
            eng.wait_ge(sems[name][0], v)

    block_cm = nc.Block()
    block = block_cm.__enter__()

    nc.const_aps.register(F32, EPS, sb["cpool"][:, 0:1])

    @block.vector
    def _(v):
        v.memset(sb["cpool"][:, 0:1], EPS)
        v.drain()
        v.memset(sb["cpool"][:, 2:3], 1.0)
        v.drain()

    DBG = {}
    dbg_used = [0]

    def dbg_dump(tag, ap_src, cols):
        if tag not in dbg_taps:
            return
        off = dbg_used[0]
        assert off + cols <= 8192

        @block.sync
        def _(s):
            wt(s, "sdv"); wt(s, "sac"); wt(s, "spe"); wt(s, "spl")
            inc("sio", s.dma_start(out=dbg[:, off:off + cols], in_=ap_src), 16)

        @block.vector
        def _(v):
            wt(v, "sio")
            i = v.memset(sb["cpool"][:, 3:4], 0.0)
            v.drain()
            inc("sdv", i)
        dbg_used[0] = off + cols
        DBG[tag] = (off, cols)

    @block.sync
    def _(s):
        def ser(instr):
            inc("sio", instr, 16)
            wt(s, "sio")
        ser(s.dma_start(out=sb["identbf"][:, :], in_=ident_bf_d))
        ser(s.dma_start(out=sb["identf"][:, :], in_=ident_f_d))
        ser(s.dma_start(out=sb["tril"][:, :], in_=tril_d))
        ser(s.dma_start(out=sb["avec"][:, :], in_=avec_d))
        ser(s.dma_start(out=sb["onesbf"][0:1, :], in_=onesbf_d))
        ser(s.dma_start(out=sb["flags"][:, :], in_=flags))
        ser(s.dma_start(out=sb["dtb"][:, 0:1], in_=dt_bias.rearrange("(p o) -> p o", o=1)))
        ser(s.dma_start(out=sb["drep"][:, :], in_=d_vec.rearrange("(o h) -> o h", o=1).broadcast_to([128, 128])))
        ser(s.dma_start(out=sb["normw"][:, :], in_=norm_w.rearrange("k p -> p k"), allow_slow_non_contiguous=True))
        ser(s.dma_start(out=sb["bthr_raw"][:, 0:1], in_=b_thr.rearrange("(o e) -> o e", o=1).broadcast_to([128, 1])))
        ser(s.dma_start(out=sb["convb"][:, :], in_=conv_bias.rearrange("b p -> p b"), allow_slow_non_contiguous=True))

    @block.gpsimd
    def _(g):
        inc("swc", g.dma_start(out=sb["convw"][:, :].rearrange("p (b k) -> p b k", b=80), in_=conv_taps.rearrange("b p k -> p b k")), 16)
        wt(g, "swc")

    @block.vector
    def _(v):
        wt(v, "sio")
        i = v.tensor_scalar(sb["bthr2"][:, 0:1], sb["bthr_raw"][:, 0:1], 0.5, None, OP.mult)
        inc("sdv", i)

    xt_h = rv(XTILE, D, F32)
    lnout = rv(LNOUT, D, BF16)

    @block.sync
    def _(s):
        wt(s, "sio")
        inc("sio", s.dma_start(out=xt_h[0:3, :], in_=x_halo), 16)

    @block.vector
    def _(v):
        wt(v, "sio")
        v.reduce_sum(sb["tiny"][0:3, 0:1], xt_h[0:3, :], AX.X)
        v.drain()
        v.tensor_scalar(sb["tiny"][0:3, 1:2], sb["tiny"][0:3, 0:1], -1.0 / D, None, OP.mult)
        v.drain()
        i = v.tensor_scalar(xt_h[0:3, :], xt_h[0:3, :], sb["tiny"][0:3, 1:2], None, OP.add)
        v.drain()
        inc("sdv", i)

    @block.scalar
    def _(sc):
        wt(sc, "sdv")
        sc.activation(lnout[0:3, :], xt_h[0:3, :], AF.Square, accum_out=sb["tiny"][0:3, 2:3])
        sc.drain()
        sc.activation(sb["tiny"][0:3, 3:4], sb["tiny"][0:3, 2:3], AF.Ln, scale=1.0 / D, bias=EPS)
        sc.drain()
        inc("sac", sc.activation(sb["tiny"][0:3, 4:5], sb["tiny"][0:3, 3:4], AF.Exp, scale=-0.5))

    @block.vector
    def _(v):
        wt(v, "sac")
        i = v.tensor_scalar(lnout[0:3, :], xt_h[0:3, :], sb["tiny"][0:3, 4:5], None, OP.mult)
        v.drain()
        inc("sdv", i)

    @block.vector
    def _(v):
        i = v.memset(ps[:, 0:KD * 2], 0.0)
        inc("sdv", i)

    @block.tensor
    def _(t):
        wt(t, "sdv")
        for k in range(KD):
            i = t.transpose(bankbf(0)[:, k * 4:k * 4 + 4], lnout[0:4, k * 128:(k + 1) * 128],
                            sb["identbf"][0:4, 0:4])
        inc("spe", i)

    @block.vector
    def _(v):
        wt(v, "spe")
        i = v.tensor_copy(sb["halolnT"][:, :], bankbf(0)[:, 0:KD * 4])
        inc("sdv", i)

    @block.sync
    def _(s):
        wt(s, "sio")

    KSTAGE = int(os.environ.get("KSTAGE", "99"))

    cast_chunks = ([("wout", ob, q) for ob in range(8) for q in range(4)]
                   + [("wbase", ob, q) for ob in range(8) for q in range(2)])
    WOUT_SWD = 32 * 16
    WBASE_SWD = 48 * 16
    cast_emitted = [0]

    def emit_cast_chunk(n=1):
        for _ in range(n):
            if cast_emitted[0] >= len(cast_chunks):
                return
            kind, ob, q = cast_chunks[cast_emitted[0]]
            cast_emitted[0] += 1

            @block.gpsimd
            def _(g, kind=kind, ob=ob, q=q):
                if kind == "wout":
                    inc("swd", g.dma_start(out=w_out_bf[ob, q * 16:(q + 1) * 16],
                                           in_=w_out[ob, q * 16:(q + 1) * 16]), 16)
                else:
                    inc("swd", g.dma_start(out=w_base_bf[ob, q * 16:(q + 1) * 16],
                                           in_=w_base[ob, q * 16:(q + 1) * 16]), 16)

    def flush_casts():
        emit_cast_chunk(len(cast_chunks))

    def emit_ln(src_dram, src_row0, dstv, span, dst_off):
        entry_sdv = n_of("sdv")
        entry_spe = n_of("spe")
        entry_sac = n_of("sac")
        xt_guard = {}
        lnout_guard = [0]
        bank_guard = [0]
        for t8 in range(NCH):
            buf = t8 % 2
            xt = rv(XTILE, D, F32, off_b=buf * D * 4)
            st = sb["stats"][:, buf * 8:buf * 8 + 8]
            row0 = src_row0 + t8 * 128

            @block.sync
            def _(s, t8=t8, row0=row0, xt=xt):
                if t8 - 2 in xt_guard:
                    wt(s, "sac", xt_guard[t8 - 2])
                if t8 < 2:
                    wt(s, "sdv", entry_sdv)
                    wt(s, "spe", entry_spe)
                    wt(s, "sac", entry_sac)
                inc("sxa" if t8 % 2 == 0 else "sxb", s.dma_start(out=xt[:, :], in_=src_dram[row0:row0 + 128, :]), 16)

            @block.vector
            def _(v, xt=xt, t8=t8, st=st):
                wt(v, "sxa" if t8 % 2 == 0 else "sxb")
                v.reduce_sum(st[:, 0:1], xt[:, :], AX.X)
                v.drain()
                i = v.tensor_scalar(st[:, 1:2], st[:, 0:1], -1.0 / D, None, OP.mult)
                inc("sdv", i)

            @block.scalar
            def _(sc, xt=xt, st=st):
                wt(sc, "sdv")
                for q in range(4):
                    i = sc.activation(sb["sink"][:, :], xt[:, q * 1024:(q + 1) * 1024],
                                      AF.Square, bias=st[:, 1:2],
                                      accum_out=st[:, 2 + q:3 + q])
                inc("sac", i)

            @block.vector
            def _(v, st=st):
                wt(v, "sac")
                i = v.reduce_sum(st[:, 6:7], st[:, 2:6], AX.X)
                inc("sdv", i)

            @block.scalar
            def _(sc, st=st):
                wt(sc, "sdv")
                sc.activation(st[:, 7:8], st[:, 6:7], AF.Ln, scale=1.0 / D, bias=EPS)
                sc.drain()
                inc("sac", sc.activation(st[:, 7:8], st[:, 7:8], AF.Exp, scale=-0.5))

            @block.vector
            def _(v, st=st):
                wt(v, "sac")
                i = v.tensor_tensor(st[:, 0:1], st[:, 1:2], st[:, 7:8], OP.mult)
                inc("sdv", i)

            @block.scalar
            def _(sc, xt=xt, st=st):
                wt(sc, "sdv")
                if lnout_guard[0]:
                    wt(sc, "spe", lnout_guard[0])
                i = sc.activation(lnout[:, :], xt[:, :], AF.Identity,
                                  scale=st[:, 7:8], bias=st[:, 0:1])
                inc("sac", i)
            xt_guard[t8] = n_of("sac")

            @block.tensor
            def _(t):
                wt(t, "sac")
                if bank_guard[0]:
                    wt(t, "sdv", bank_guard[0])
                for k in range(KD):
                    b4 = k // 8
                    i = t.transpose(bankbf(b4)[:, (k % 8) * 128:(k % 8) * 128 + 128],
                                    lnout[:, k * 128:(k + 1) * 128], sb["identbf"][:, :])
                    if k == KD - 1:
                        inc("spe", i)
            lnout_guard[0] = n_of("spe")

            @block.vector
            def _(v, t8=t8):
                wt(v, "spe")
                for b4 in range(4):
                    i = v.tensor_copy(
                        dstv.rearrange("p (k c) -> p k c", c=span)[:, b4 * 8:(b4 + 1) * 8,
                                                                  dst_off + t8 * 128: dst_off + t8 * 128 + 128],
                        bankbf(b4).rearrange("p (k c) -> p k c", k=8))
                    if b4 == 3:
                        inc("sdv", i)
            bank_guard[0] = n_of("sdv")

    def emit_gate(half):
        slab_guard = [0, 0]
        evac_of = {}
        spill_guard = [0]
        lnv = rv(LN_XT, KD * HALF, BF16)
        for fb in range(16):
            for kh in range(4):
                buf = (fb * 4 + kh) % 2
                wsl = rv(GWSL, 8 * 512, BF16, off_b=buf * 8 * 1024)

                wsem = "swa" if buf == 0 else "swb"

                @block.gpsimd
                def _(g, fb=fb, kh=kh, buf=buf, wsl=wsl, wsem=wsem):
                    if slab_guard[buf]:
                        wt(g, "spe", slab_guard[buf])
                    wt(g, "sdv", ipb["lnguard"])
                    inc(wsem, g.dma_start(out=wsl.rearrange("p (k f) -> p k f", k=8),
                                          in_=w_gate[fb, kh * 8:(kh + 1) * 8].rearrange("k d f -> d k f")), 16)

                @block.tensor
                def _(t, fb=fb, kh=kh, wsl=wsl, wsem=wsem):
                    wt(t, wsem)
                    for tb in range(NCH):
                        if kh == 0:
                            key = (fb - 1, tb)
                            if key in evac_of:
                                wt(t, "sac", evac_of[key])
                        for k8 in range(8):
                            k = kh * 8 + k8
                            i = t.matmul(bank(tb),
                                         lnv.rearrange("p (k c) -> p k c", k=KD)[:, k, tb * 128:tb * 128 + 128],
                                         wsl[:, k8 * 512:(k8 + 1) * 512],
                                         start=(k == 0), stop=(k == KD - 1))
                    inc("spe", i)
                slab_guard[buf] = n_of("spe")
            pe_end = n_of("spe")

            for tb in range(NCH):
                @block.scalar
                def _(sc, tb=tb, fb=fb):
                    wt(sc, "spe", pe_end)
                    if spill_guard[0]:
                        wt(sc, "sga", spill_guard[0])
                    i = sc.activation(rv(GATET, 512, BF16), bank(tb), AF.Silu)
                    evac_of[(fb, tb)] = inc("sac", i)

                @block.sync
                def _(s, tb=tb, fb=fb):
                    wt(s, "sac")
                    spill_guard[0] = inc("sga", s.dma_start(
                        out=gate_sp[half * HALF + tb * 128: half * HALF + (tb + 1) * 128,
                                    fb * 512:(fb + 1) * 512],
                        in_=rv(GATET, 512, BF16)), 16)

    def emit_dt():
        za = rv(DTT, HALF, F32)
        zb = rv(DTT, HALF, F32, off_b=4096)
        dtc = rv(YO, HALF, F32)
        cs = rv(TVEC, HALF, F32)

        @block.vector
        def _(v):
            wt(v, "spe")
            i = v.tensor_scalar(za[:, :], ps[:, 0:HALF], sb["dtb"][:, 0:1], None, OP.add)
            inc("sdv", i)

        @block.scalar
        def _(sc):
            wt(sc, "sdv")
            sc.activation(zb[:, :], za[:, :], AF.Abs)
            sc.drain()
            sc.activation(zb[:, :], zb[:, :], AF.Exp, scale=-1.0)
            sc.drain()
            inc("sac", sc.activation(zb[:, :], zb[:, :], AF.Ln, bias=1.0))

        @block.vector
        def _(v):
            wt(v, "sac")
            v.tensor_scalar(dtc[:, :], za[:, :], 0.0, None, OP.max)
            v.drain()
            v.tensor_tensor(dtc[:, :], dtc[:, :], zb[:, :], OP.add)
            v.drain()
            v.tensor_scalar(dtc[:, :], dtc[:, :], DT_MIN, DT_MAX, OP.max, OP.min)
            v.drain()
            v.tensor_scalar(za[:, :], dtc[:, :], sb["avec"][:, 0:1], None, OP.mult)
            v.drain()
            for c in range(NCH):
                v.tensor_tensor_scan(cs[:, c * 128:(c + 1) * 128],
                                     sb["cpool"][:, 2:3].broadcast_to([128, 128]),
                                     za[:, c * 128:(c + 1) * 128], 0.0, OP.mult, OP.add)
            v.drain()
            for c in range(NCH):
                v.tensor_scalar(zb[:, c * 128:(c + 1) * 128], cs[:, c * 128:(c + 1) * 128],
                                cs[:, c * 128 + 127:c * 128 + 128], None, OP.subtract)
            i = v.tensor_copy(sb["cslast"][:, :], cs[:, 127::128])
            inc("sdv", i)

        @block.scalar
        def _(sc):
            wt(sc, "sdv")
            sc.activation(zb[:, :], zb[:, :], AF.Exp, scale=-1.0)
            i = sc.activation(sb["tiny"][:, 0:NCH], sb["cslast"][:, :], AF.Exp)
            inc("sac", i)

        @block.sync
        def _(s):
            wt(s, "sdv"); wt(s, "sac")
            inc("scd", s.dma_start(out=cs_dram, in_=cs[:, :]), 16)
            wt(s, "scd")
            inc("scd", s.dma_start(out=cdexp_dram, in_=sb["tiny"][:, 0:NCH],
                                   allow_slow_non_contiguous=True), 16)
            wt(s, "scd")
            inc("scd", s.dma_start(out=sb["cdrep"][:, :].rearrange("p (c h) -> p c h", c=NCH),
                                   in_=cdexp_dram.rearrange("(o c) h -> o c h", o=1).broadcast_to([128, NCH, 128])), 16)

        for c in range(NCH):
            @block.tensor
            def _(t, c=c):
                wt(t, "sdv"); wt(t, "sac")
                t.transpose(bank(0)[:, 0:128], dtc[:, c * 128:(c + 1) * 128], sb["identf"][:, :])
                t.transpose(bank(1)[:, 0:128], cs[:, c * 128:(c + 1) * 128], sb["identf"][:, :])
                i = t.transpose(bank(2)[:, 0:128], zb[:, c * 128:(c + 1) * 128], sb["identf"][:, :])
                inc("spe", i)

            @block.vector
            def _(v, c=c):
                wt(v, "spe")
                v.tensor_copy(sb["dtc_tm"][:, c * 128:(c + 1) * 128], bank(0)[:, 0:128])
                v.tensor_copy(sb["cs_tm"][:, c * 128:(c + 1) * 128], bank(1)[:, 0:128])
                i = v.tensor_copy(sb["dec_tm"][:, c * 128:(c + 1) * 128], bank(2)[:, 0:128])
                inc("sdv", i)

        @block.scalar
        def _(sc):
            wt(sc, "sdv")
            inc("sac", sc.activation(sb["ecs_tm"][:, :], sb["cs_tm"][:, :], AF.Exp))

    ipb = {"buf": 0, "xbuf": 0, "pe_guard": [0, 0], "evac_guard": 0,
           "conv_pe": [0, 0], "silu_guard": 0, "lnguard": 0, "gate_io": 0, "bank_sac": 0}

    def emit_inproj_block(blk, dst_kb, is_r, first_half):
        buf = ipb["buf"]; ipb["buf"] ^= 1
        wsl = rv(WBLK, KD * 128, BF16, off_b=buf * KD * 256)
        is_conv = blk != 0
        cb = blk - 1
        lnv = rv(LN_XT, KD * HALF, BF16)

        wsem = "swa" if buf == 0 else "swb"

        @block.gpsimd
        def _(g, buf=buf, wsl=wsl, blk=blk, wsem=wsem):
            if ipb["pe_guard"][buf]:
                wt(g, "spe", ipb["pe_guard"][buf])
            wt(g, "sdv", ipb["lnguard"])
            inc(wsem, g.dma_start(out=wsl.rearrange("p (k f) -> p k f", k=KD), in_=w_xbcdt[blk].rearrange("k d f -> d k f")), 16)

        @block.tensor
        def _(t, wsl=wsl, blk=blk, wsem=wsem):
            wt(t, wsem)
            if ipb["evac_guard"]:
                wt(t, "sdv", ipb["evac_guard"])
            if ipb["bank_sac"]:
                wt(t, "sac", ipb["bank_sac"])
            for tt in range(2):
                for k in range(KD):
                    i = t.matmul(bank(tt), wsl[:, k * 128:(k + 1) * 128],
                                 lnv.rearrange("p (k c) -> p k c", k=KD)[:, k, tt * 512:(tt + 1) * 512],
                                 start=(k == 0), stop=(k == KD - 1))
            if is_conv and (not is_r) and first_half:
                wt(t, "sac", ipb["silu_guard"])
                for k in range(KD):
                    i = t.matmul(bank(2)[:, 0:3], wsl[:, k * 128:(k + 1) * 128],
                                 sb["halolnT"][:, :].rearrange("p (k c) -> p k c", k=KD)[:, k, 0:3],
                                 start=(k == 0), stop=(k == KD - 1))
            inc("spe", i)
        ipb["pe_guard"][buf] = n_of("spe")

        if not is_conv:
            emit_dt()
            ipb["evac_guard"] = n_of("sdv")
            return

        xbuf = ipb["xbuf"]; ipb["xbuf"] ^= 1
        xb = rv(XBC, HALF + 8, BF16, off_b=xbuf * 2560)

        @block.vector
        def _(v, xb=xb, xbuf=xbuf, blk=blk, cb=cb):
            wt(v, "spe")
            if ipb["conv_pe"][xbuf]:
                wt(v, "spe", ipb["conv_pe"][xbuf])
            if is_r and first_half:
                v.memset(xb[:, 0:3], 0.0)
            elif is_r or (not first_half):
                v.tensor_copy(xb[:, 0:3], sb["halo"][:, blk * 4:blk * 4 + 3])
            else:
                v.tensor_scalar(xb[:, 0:3], bank(2)[:, 0:3], sb["flags"][:, 1:2], None, OP.mult)
            v.tensor_copy(xb[:, 3:3 + HALF], ps[:, 0:HALF])
            v.drain()
            v.tensor_copy(sb["halo"][:, blk * 4:blk * 4 + 3], xb[:, HALF:HALF + 3])
            for k in range(4):
                i = v.tensor_scalar(rv(DIAG, 512, BF16)[:, k * 128:(k + 1) * 128], sb["identbf"][:, :],
                                    sb["convw"][:, cb * 4 + k:cb * 4 + k + 1], None, OP.mult)
            inc("sdv", i)
        ipb["evac_guard"] = n_of("sdv")

        @block.tensor
        def _(t, xb=xb):
            wt(t, "sdv")
            wt(t, "sac", ipb["silu_guard"])
            for tt in range(2):
                for k in range(4):
                    i = t.matmul(bank(2 + tt), rv(DIAG, 512, BF16)[:, k * 128:(k + 1) * 128],
                                 xb[:, tt * 512 + k: tt * 512 + k + 512],
                                 start=(k == 0), stop=(k == 3))
            inc("spe", i)
        ipb["conv_pe"][xbuf] = n_of("spe")

        @block.scalar
        def _(sc, dst_kb=dst_kb, cb=cb):
            wt(sc, "spe")
            if ipb["gate_io"]:
                wt(sc, "sga", ipb["gate_io"])
            dstv = rv(dst_kb[0], HALF, BF16, off_b=dst_kb[1])
            sc.activation(dstv[:, 0:512], bank(2), AF.Silu, bias=sb["convb"][:, cb:cb + 1])
            i = sc.activation(dstv[:, 512:1024], bank(3), AF.Silu, bias=sb["convb"][:, cb:cb + 1])
            inc("sac", i)
        ipb["silu_guard"] = n_of("sac")

    def emit_group_ssd(g, half, is_r):
        gbc = rv(GBC, 2 * HALF, BF16)
        xg = rv(XG, 8 * HALF, BF16)
        btm = rv(BTM, NCH * 128, BF16)
        scm = rv(SCM, NCH * 128, BF16)
        xtm = rv(XTM, HALF, BF16)
        xdt = rv(XDT, HALF, BF16)
        xdtd = rv(XDTD, HALF, BF16)
        tvec = rv(TVEC, GRP, F32)
        yo = rv(YO, GRP, F32)
        yg = rv(YG, GRP, BF16)
        gt2 = rv(GT2, GRP, BF16)
        nrm = rv(NRM, GRP, BF16)
        csrep = rv(CSREP, 2048, F32)
        lms = rv(LMS, 2048, BF16)

        for cq in range(2):
            @block.tensor
            def _(t, cq=cq):
                wt(t, "sac", ipb["silu_guard"])
                wt(t, "sdv")
                for c in range(cq * 4, cq * 4 + 4):
                    i = t.transpose(bankbf(4)[:, (c % 4) * 128:(c % 4) * 128 + 128],
                                    gbc[:, c * 128:(c + 1) * 128], sb["identbf"][:, :])
                    if not is_r:
                        i = t.matmul(bank(5)[:, (c % 4) * 128:(c % 4) * 128 + 128],
                                     gbc[:, c * 128:(c + 1) * 128],
                                     gbc[:, HALF + c * 128: HALF + (c + 1) * 128],
                                     start=True, stop=True)
                inc("spe", i)

            @block.vector
            def _(v, cq=cq):
                wt(v, "spe")
                for c in range(cq * 4, cq * 4 + 4):
                    i = v.tensor_copy(btm[:, c * 128:(c + 1) * 128],
                                      bankbf(4)[:, (c % 4) * 128:(c % 4) * 128 + 128])
                if not is_r:
                    for c in range(cq * 4, cq * 4 + 4):
                        i = v.scalar_tensor_tensor(scm[:, c * 128:(c + 1) * 128],
                                                   bank(5)[:, (c % 4) * 128:(c % 4) * 128 + 128],
                                                   1.0, sb["tril"][:, :], OP.mult, OP.mult)
                inc("sdv", i)

        if not is_r:
            @block.vector
            def _(v):
                wt(v, "spl"); wt(v, "sdv")
                i = v.tensor_copy(rv(CBF, GRP, BF16)[:, :], sb["carry"][:, g * GRP:(g + 1) * GRP])
                inc("sdv", i)

        gate_rd = [0]
        for c in range(NCH):
            @block.tensor
            def _(t, c=c):
                wt(t, "sdv")
                for j in range(8):
                    i = t.transpose(bankbf(4)[:, j * 128:(j + 1) * 128],
                                    xg[:, j * HALF + c * 128:j * HALF + (c + 1) * 128],
                                    sb["identbf"][:, :])
                inc("spe", i)

            @block.vector
            def _(v, c=c):
                wt(v, "spe")
                wt(v, "spl")
                v.tensor_copy(xtm[:, :], bankbf(4))
                v.drain()
                v.tensor_tensor(xdt.rearrange("p (h r) -> p h r", h=HPG),
                                xtm.rearrange("p (h r) -> p h r", h=HPG),
                                sb["dtc_tm"][:, c * 128 + g * HPG: c * 128 + g * HPG + HPG]
                                .rearrange("p (h o) -> p h o", o=1).broadcast_to([128, HPG, 64]), OP.mult)
                v.drain()
                i = v.tensor_tensor(xdtd.rearrange("p (h r) -> p h r", h=HPG),
                                    xdt.rearrange("p (h r) -> p h r", h=HPG),
                                    sb["dec_tm"][:, c * 128 + g * HPG: c * 128 + g * HPG + HPG]
                                    .rearrange("p (h o) -> p h o", o=1).broadcast_to([128, HPG, 64]), OP.mult)
                inc("sdv", i)
            xdt_done = n_of("sdv")

            if not is_r:
                @block.tensor
                def _(t, c=c, xdt_done=xdt_done):
                    wt(t, "sdv", xdt_done)
                    for hh in range(2):
                        i = t.matmul(bank(hh), gbc[:, HALF + c * 128:HALF + (c + 1) * 128],
                                     rv(CBF, GRP, BF16)[:, hh * 512:(hh + 1) * 512],
                                     start=True, stop=True)
                    inc("spe", i)

            @block.tensor
            def _(t, c=c, xdt_done=xdt_done):
                wt(t, "sdv", xdt_done)
                for hh in range(2):
                    i = t.matmul(bank(2 + hh), btm[:, c * 128:(c + 1) * 128],
                                 xdtd[:, hh * 512:(hh + 1) * 512],
                                 start=True, stop=True)
                inc("spe", i)
            states_done = n_of("spe")

            @block.gpsimd
            def _(gp, c=c, states_done=states_done):
                wt(gp, "spe", states_done)
                wt(gp, "scd")
                i = gp.tensor_tensor(tvec.rearrange("p (h r) -> p h r", h=HPG),
                                     sb["carry"][:, g * GRP:(g + 1) * GRP].rearrange("p (h r) -> p h r", h=HPG),
                                     sb["cdrep"][:, c * 128 + g * HPG: c * 128 + g * HPG + HPG]
                                     .rearrange("p (h o) -> p h o", o=1).broadcast_to([128, HPG, 64]), OP.mult)
                inc("spl", i)

            @block.vector
            def _(v, c=c):
                wt(v, "spl")
                v.tensor_tensor(sb["carry"][:, g * GRP:(g + 1) * GRP], tvec[:, :],
                                bank2(2), OP.add)
                v.drain()
                if not is_r:
                    i = v.tensor_copy(rv(CBF, GRP, BF16)[:, :], sb["carry"][:, g * GRP:(g + 1) * GRP])
                else:
                    i = v.memset(sb["cpool"][:, 3:4], 0.0)
                inc("sdv", i)
            carry_done = n_of("sdv")

            if is_r:
                continue

            @block.sync
            def _(s, c=c, carry_done=carry_done):
                wt(s, "sdv", carry_done)
                inc("scr", s.dma_start(
                    out=csrep.rearrange("p (h l) -> p h l", h=HPG),
                    in_=cs_dram[g * HPG:(g + 1) * HPG, c * 128:(c + 1) * 128]
                    .rearrange("(o h) l -> o h l", o=1).broadcast_to([128, HPG, 128])), 16)

            @block.vector
            def _(v, c=c):
                wt(v, "scr")
                i = v.tensor_tensor(csrep.rearrange("p (h l) -> p h l", h=HPG),
                                    csrep.rearrange("p (h l) -> p h l", h=HPG),
                                    sb["cs_tm"][:, c * 128 + g * HPG: c * 128 + g * HPG + HPG]
                                    .rearrange("p (h o) -> p h o", o=1).broadcast_to([128, HPG, 128]), OP.subtract)
                inc("sdv", i)

            @block.scalar
            def _(sc, c=c):
                wt(sc, "sdv")
                inc("sac", sc.activation(lms[:, :], csrep[:, :], AF.Exp))

            @block.vector
            def _(v, c=c):
                wt(v, "sac")
                i = v.scalar_tensor_tensor(lms.rearrange("p (h l) -> p h l", h=HPG),
                                           lms.rearrange("p (h l) -> p h l", h=HPG), 1.0,
                                           scm[:, c * 128:(c + 1) * 128]
                                           .unsqueeze(1).broadcast_to([128, HPG, 128]),
                                           OP.min, OP.mult)
                inc("sdv", i)
            lms_done = n_of("sdv")

            @block.tensor
            def _(t, c=c, lms_done=lms_done):
                wt(t, "sdv", lms_done)
                for h in range(HPG):
                    i = t.matmul(ps[:, 6 * 512 + h * 64: 6 * 512 + (h + 1) * 64],
                                 lms[:, h * 128:(h + 1) * 128],
                                 xdt[:, h * 64:(h + 1) * 64], start=True, stop=True)
                inc("spe", i)
            ydiag_done = n_of("spe")

            @block.sync
            def _(s, c=c):
                if gate_rd[0]:
                    wt(s, "sdv", gate_rd[0])
                wt(s, "sga", ipb["gate_io"])
                inc("sgt", s.dma_start(out=gt2[:, :],
                                       in_=gate_sp[half * HALF + c * 128: half * HALF + (c + 1) * 128,
                                                   g * GRP:(g + 1) * GRP]), 16)

            @block.vector
            def _(v, c=c, ydiag_done=ydiag_done):
                wt(v, "spe", ydiag_done)
                wt(v, "sgt")
                v.tensor_tensor(yo.rearrange("p (h r) -> p h r", h=HPG),
                                bank2(0).rearrange("p (h r) -> p h r", h=HPG),
                                sb["ecs_tm"][:, c * 128 + g * HPG: c * 128 + g * HPG + HPG]
                                .rearrange("p (h o) -> p h o", o=1).broadcast_to([128, HPG, 64]), OP.mult)
                v.drain()
                i = v.tensor_tensor(yo[:, :], yo[:, :], ps[:, 6 * 512:8 * 512], OP.add)
                inc("sdv", i)

            @block.gpsimd
            def _(gp, c=c):
                wt(gp, "sdv")
                i = gp.tensor_tensor(tvec.rearrange("p (h r) -> p h r", h=HPG),
                                     xtm.rearrange("p (h r) -> p h r", h=HPG),
                                     sb["drep"][:, g * HPG:(g + 1) * HPG]
                                     .rearrange("p (h o) -> p h o", o=1).broadcast_to([128, HPG, 64]), OP.mult)
                inc("spl", i)

            @block.vector
            def _(v, c=c):
                wt(v, "spl")
                wt(v, "snr")
                v.tensor_tensor(yo[:, :], yo[:, :], tvec[:, :], OP.add)
                v.drain()
                i = v.tensor_tensor(yg[:, :], yo[:, :], gt2[:, :], OP.mult)
                inc("sdv", i)
            gate_rd[0] = n_of("sdv")

            @block.scalar
            def _(sc, c=c):
                wt(sc, "sdv")
                sc.activation(rv(SQS, GRP, BF16)[:, :], yg[:, :], AF.Square, accum_out=sb["ssq"][:, 0:1])
                sc.drain()
                sc.activation(sb["ssq"][:, 1:2], sb["ssq"][:, 0:1], AF.Ln, scale=1.0 / GRP, bias=EPS)
                sc.drain()
                inc("sac", sc.activation(sb["ssq"][:, 2:3], sb["ssq"][:, 1:2], AF.Exp, scale=-0.5))

            @block.vector
            def _(v, c=c):
                wt(v, "sac")
                wt(v, "sio")
                i = v.tensor_scalar(nrm[:, :], yg[:, :], sb["ssq"][:, 2:3], None, OP.mult)
                inc("sdv", i)

            @block.sync
            def _(s, c=c):
                wt(s, "sdv")
                inc("snr", s.dma_start(
                    out=normed_sp[half * HALF + c * 128: half * HALF + (c + 1) * 128,
                                  g * GRP:(g + 1) * GRP],
                    in_=nrm[:, :]), 16)

    def do_pass(is_r, half):
        src = x_pre if is_r else x_own
        emit_ln(src, half * HALF, rv(LN_XT, KD * HALF, BF16), HALF, 0)
        ipb["lnguard"] = n_of("sdv")
        ipb["gate_io"] = 0
        if (not is_r) and KSTAGE >= 3:
            emit_gate(half)
            ipb["gate_io"] = n_of("sga")
            ipb["pe_guard"] = [n_of("spe"), n_of("spe")]
            ipb["bank_sac"] = n_of("sac")
        emit_inproj_block(0, None, is_r, half == 0)
        if KSTAGE < 2:
            return
        for g in range(NG):
            base = 1 + g * 10
            emit_inproj_block(base, (GBC, 0), is_r, half == 0)
            if not is_r:
                emit_inproj_block(base + 1, (GBC, HALF * 2), is_r, half == 0)
            for j in range(8):
                emit_inproj_block(base + 2 + j, (XG, j * HALF * 2), is_r, half == 0)
            if KSTAGE >= 4:
                emit_group_ssd(g, half, is_r)
            if is_r:
                emit_cast_chunk(3)

    @block.vector
    def _(v):
        wt(v, "sio")
        i = v.memset(sb["carry"][:, :], 0.0)
        inc("sdv", i)

    if KSTAGE >= 5:
        do_pass(True, 0)
        do_pass(True, 1)

        @block.vector
        def _(v):
            wt(v, "spl"); wt(v, "sdv")
            i = v.tensor_scalar(sb["carry"][:, :], sb["carry"][:, :], sb["flags"][:, 0:1], None, OP.mult)
            inc("sdv", i)

        if "cin" in dbg_taps:
            dbg_dump("cin", sb["carry"][:, 0:4096], 4096)

    do_pass(False, 0)
    if KSTAGE >= 6:
        do_pass(False, 1)

    if "dtc" in dbg_taps:
        dbg_dump("dtc", rv(YO, HALF, F32)[:, :], HALF)
    if "cs" in dbg_taps:
        dbg_dump("cs", rv(TVEC, HALF, F32)[:, :], HALF)
    if "carry" in dbg_taps:
        dbg_dump("carry", sb["carry"][:, 0:4096], 4096)
    if "xg" in dbg_taps:
        dbg_dump("xg", rv(XG, 8 * HALF, BF16).bitcast(F32), 4096)
    if "bc" in dbg_taps:
        dbg_dump("bc", rv(GBC, 2 * HALF, BF16).bitcast(F32), 1024)
    if "y" in dbg_taps:
        dbg_dump("y", rv(YO, GRP, F32), 1024)
    if "nrm" in dbg_taps:
        dbg_dump("nrm", rv(YG, GRP, BF16).bitcast(F32), 512)

    flush_casts()
    NTQ, NSL, WS3, T3A, T3B = 0, 64, 80, 88, 91
    if KSTAGE >= 7:
        slab_guard = [0, 0]
        for tq in range(4):
            nT = rv(NTQ, 64 * 512, BF16)
            tile_guard = [0]
            for tb in range(4):
                @block.sync
                def _(s, tq=tq, tb=tb):
                    if tile_guard[0]:
                        wt(s, "spe", tile_guard[0])
                    else:
                        wt(s, "spe")
                    wt(s, "sdv")
                    wt(s, "snr")
                    inc("snl", s.dma_start(out=rv(NSL, INTER, BF16),
                                           in_=normed_sp[tq * 512 + tb * 128: tq * 512 + (tb + 1) * 128, :]), 16)

                for kq in range(2):
                    @block.tensor
                    def _(t, tb=tb, kq=kq):
                        wt(t, "snl")
                        wt(t, "sdv")
                        for k32 in range(32):
                            k = kq * 32 + k32
                            i = t.transpose(bankbf(k32 // 8)[:, (k % 8) * 128:(k % 8) * 128 + 128],
                                            rv(NSL, INTER, BF16)[:, k * 128:(k + 1) * 128],
                                            sb["identbf"][:, :])
                        inc("spe", i)
                    if kq == 1:
                        tile_guard[0] = n_of("spe")

                    @block.vector
                    def _(v, tb=tb, kq=kq):
                        wt(v, "spe")
                        for k32 in range(32):
                            k = kq * 32 + k32
                            i = v.tensor_scalar(
                                nT.rearrange("p (k c) -> p k c", k=64)[:, k, tb * 128:(tb + 1) * 128],
                                bankbf(k32 // 8)[:, (k % 8) * 128:(k % 8) * 128 + 128],
                                sb["normw"][:, k:k + 1], None, OP.mult)
                        inc("sdv", i)

            for ob in range(8):
                for ks in range(16):
                    buf = ks % 2
                    wsl = rv(WS3, 4 * 512, BF16, off_b=buf * 4 * 1024)

                    wsem = "swa" if buf == 0 else "swb"

                    def ldw3(g, ks=ks, ob=ob, buf=buf, wsl=wsl, wsem=wsem, tq=tq):
                        if tq == 0 and ob == 0 and ks < 2:
                            g.wait_ge(sems["swd"][0], WOUT_SWD)
                        wt(g, "spe", slab_guard[buf] if slab_guard[buf] else n_of("spe"))
                        inc(wsem, g.dma_start(out=wsl.rearrange("p (k f) -> p k f", k=4),
                                              in_=w_out_bf[ob, ks * 4:(ks + 1) * 4].rearrange("k d f -> d k f")), 16)
                    if buf == 0:
                        block.scalar(ldw3)
                    else:
                        block.sync(ldw3)

                    @block.tensor
                    def _(t, ks=ks, ob=ob, wsl=wsl, wsem=wsem):
                        wt(t, wsem)
                        if ks == 0:
                            wt(t, "sdv")
                        for tb in range(4):
                            for k4 in range(4):
                                k = ks * 4 + k4
                                i = t.matmul(bank(tb),
                                             nT.rearrange("p (k c) -> p k c", k=64)[:, k, tb * 128:(tb + 1) * 128],
                                             wsl[:, k4 * 512:(k4 + 1) * 512],
                                             start=(k == 0), stop=(k == 63))
                        inc("spe", i)
                    slab_guard[buf] = n_of("spe")
                pe_last = n_of("spe")

                for tb in range(4):
                    row0 = tq * 512 + tb * 128

                    @block.sync
                    def _(s, tb=tb, ob=ob, row0=row0):
                        wt(s, "sdv")
                        inc("sra", s.dma_start(out=rv(T3A, 512, F32),
                                               in_=x_own[row0:row0 + 128, ob * 512:(ob + 1) * 512]), 16)

                    @block.vector
                    def _(v, tb=tb, pe_last=pe_last):
                        wt(v, "spe", pe_last)
                        wt(v, "sra")
                        wt(v, "srb")
                        i = v.tensor_tensor(rv(T3B, 512, F32), bank(tb), rv(T3A, 512, F32), OP.add)
                        inc("sdv", i)

                    @block.sync
                    def _(s, tb=tb, ob=ob, row0=row0):
                        wt(s, "sdv")
                        inc("srb", s.dma_start(out=x2_sp[row0:row0 + 128, ob * 512:(ob + 1) * 512],
                                               in_=rv(T3B, 512, F32)), 16)

    LN2Q, AFL4, BFL4, LNO4, WS4, T4A, T4B = 0, 64, 72, 80, 88, 120, 123
    BBR, WGT, HWT = 126, 134, 136
    if KSTAGE >= 8:
        ln2 = rv(LN2Q, KD * HALF, BF16)

        @block.gpsimd
        def _(g):
            wt(g, "spe")
            inc("swc", g.dma_start(out=rv(AFL4, KD * 128, BF16).rearrange("p (k e) -> p k e", k=KD),
                                  in_=a_flat.rearrange("k d e -> d k e")), 16)
            wt(g, "swc")
            inc("swc", g.dma_start(out=rv(BFL4, D, BF16), in_=b_flat), 16)
            wt(g, "swc")
            inc("swc", g.dma_start(out=region[0:1, BBR * 1024:BBR * 1024 + D * 2].bitcast(BF16),
                                  in_=b_base.rearrange("(o e) -> o e", o=1)), 16)
            wt(g, "swc")
            inc("swc", g.dma_start(out=rv(WGT, KD * 16, BF16).rearrange("p (k e) -> p k e", k=KD),
                                  in_=wg_thr.rearrange("k d e -> d k e")), 16)
            wt(g, "swc")

        slab4 = [0, 0]
        for tbs in range(2):
            lnog = [0]
            bankg = [0]
            xtg = {}
            lnob = rv(LNO4, D, BF16)
            for tb8 in range(8):
                tb = tbs * 8 + tb8
                buf = tb8 % 2
                xt = rv(T4A, 0, F32) if False else rv(NSL, 2 * D, F32, off_b=buf * D * 4)
                xt = rv(WS4, D, F32, off_b=buf * D * 4)

                st = sb["stats"][:, (tb8 % 2) * 8:(tb8 % 2) * 8 + 8]

                @block.sync
                def _(s, tb=tb, xt=xt, tb8=tb8):
                    if tb8 - 2 in xtg:
                        wt(s, "sac", xtg[tb8 - 2])
                    if tb8 < 2:
                        wt(s, "spe")
                        wt(s, "sdv")
                        wt(s, "sac")
                        wt(s, "srb")
                    inc("sxa" if tb8 % 2 == 0 else "sxb",
                        s.dma_start(out=xt[:, :], in_=x2_sp[tb * 128:(tb + 1) * 128, :]), 16)

                @block.vector
                def _(v, xt=xt, tb8=tb8, st=st):
                    wt(v, "sxa" if tb8 % 2 == 0 else "sxb")
                    v.reduce_sum(st[:, 0:1], xt[:, :], AX.X)
                    v.drain()
                    i = v.tensor_scalar(st[:, 1:2], st[:, 0:1], -1.0 / D, None, OP.mult)
                    inc("sdv", i)

                @block.scalar
                def _(sc, xt=xt, st=st):
                    wt(sc, "sdv")
                    for q in range(4):
                        i = sc.activation(sb["sink"][:, :], xt[:, q * 1024:(q + 1) * 1024],
                                          AF.Square, bias=st[:, 1:2],
                                          accum_out=st[:, 2 + q:3 + q])
                    inc("sac", i)

                @block.vector
                def _(v, st=st):
                    wt(v, "sac")
                    i = v.reduce_sum(st[:, 6:7], st[:, 2:6], AX.X)
                    inc("sdv", i)

                @block.scalar
                def _(sc, st=st):
                    wt(sc, "sdv")
                    sc.activation(st[:, 7:8], st[:, 6:7], AF.Ln, scale=1.0 / D, bias=EPS)
                    sc.drain()
                    inc("sac", sc.activation(st[:, 7:8], st[:, 7:8], AF.Exp, scale=-0.5))

                @block.vector
                def _(v, st=st):
                    wt(v, "sac")
                    i = v.tensor_tensor(st[:, 0:1], st[:, 1:2], st[:, 7:8], OP.mult)
                    inc("sdv", i)

                @block.scalar
                def _(sc, xt=xt, st=st):
                    wt(sc, "sdv")
                    if lnog[0]:
                        wt(sc, "spe", lnog[0])
                    i = sc.activation(lnob[:, :], xt[:, :], AF.Identity,
                                      scale=st[:, 7:8], bias=st[:, 0:1])
                    inc("sac", i)
                xtg[tb8] = n_of("sac")

                @block.tensor
                def _(t):
                    wt(t, "sac")
                    if bankg[0]:
                        wt(t, "sdv", bankg[0])
                    for k in range(KD):
                        i = t.transpose(bankbf(k // 8)[:, (k % 8) * 128:(k % 8) * 128 + 128],
                                        lnob[:, k * 128:(k + 1) * 128], sb["identbf"][:, :])
                        if k == KD - 1:
                            inc("spe", i)
                lnog[0] = n_of("spe")

                @block.vector
                def _(v, tb8=tb8):
                    wt(v, "spe")
                    for b4 in range(4):
                        i = v.tensor_copy(
                            ln2.rearrange("p (k c) -> p k c", k=KD)[:, b4 * 8:(b4 + 1) * 8,
                                                                    tb8 * 128:(tb8 + 1) * 128],
                            bankbf(b4).rearrange("p (k c) -> p k c", k=8))
                        if b4 == 3:
                            inc("sdv", i)
                bankg[0] = n_of("sdv")

                @block.tensor
                def _(t, tb8=tb8):
                    wt(t, "sdv"); wt(t, "swc")
                    for k in range(KD):
                        t.matmul(ps[:, 4 * 512:4 * 512 + 16],
                                 ln2.rearrange("p (k c) -> p k c", k=KD)[:, k, tb8 * 128:(tb8 + 1) * 128],
                                 rv(WGT, KD * 16, BF16)[:, k * 16:(k + 1) * 16],
                                 start=(k == 0), stop=(k == KD - 1))
                    for k in range(KD):
                        i = t.matmul(ps[:, 5 * 512:5 * 512 + 128],
                                     ln2.rearrange("p (k c) -> p k c", k=KD)[:, k, tb8 * 128:(tb8 + 1) * 128],
                                     rv(AFL4, KD * 128, BF16)[:, k * 128:(k + 1) * 128],
                                     start=(k == 0), stop=(k == KD - 1))
                    inc("spe", i)

                @block.scalar
                def _(sc):
                    wt(sc, "spe")
                    sc.activation(sb["ffn"][:, 32:40], ps[:, 4 * 512:4 * 512 + 8], AF.Exp)
                    inc("sac", sc.activation(sb["ffn"][:, 17:18], ps[:, 4 * 512 + 8:4 * 512 + 9], AF.Tanh,
                                             scale=0.5, bias=sb["bthr2"][:, 0:1]))

                @block.vector
                def _(v):
                    wt(v, "sac")
                    f = sb["ffn"]
                    v.tensor_scalar(f[:, 20:21], f[:, 17:18], 0.0625, 0.0625, OP.mult, OP.add)
                    v.reduce_sum(f[:, 18:19], f[:, 32:40], AX.X)
                    v.drain()
                    v.reciprocal(f[:, 19:20], f[:, 18:19])
                    v.drain()
                    v.tensor_scalar(f[:, 40:48], f[:, 32:40], f[:, 19:20], None, OP.mult)
                    v.drain()
                    v.tensor_scalar(f[:, 48:56], f[:, 40:48], f[:, 20:21], None, OP.subtract)
                    v.drain()
                    v.tensor_scalar(f[:, 48:56], f[:, 48:56], 0.0, None, OP.max)
                    v.drain()
                    v.reduce_sum(f[:, 21:22], f[:, 48:56], AX.X)
                    v.drain()
                    v.tensor_scalar(f[:, 22:23], f[:, 21:22], 0.0, None, OP.is_equal)
                    v.drain()
                    v.tensor_tensor(f[:, 21:22], f[:, 21:22], f[:, 22:23], OP.add)
                    v.drain()
                    v.reciprocal(f[:, 23:24], f[:, 21:22])
                    v.drain()
                    v.tensor_scalar(f[:, 48:56], f[:, 48:56], f[:, 23:24], 2.0, OP.mult, OP.mult)
                    v.drain()
                    i = v.tensor_tensor(rv(T4A, 128, BF16)[:, 0:128].rearrange("p (e r) -> p e r", e=8),
                                        ps[:, 5 * 512:5 * 512 + 128].rearrange("p (e r) -> p e r", e=8),
                                        f[:, 48:56].rearrange("p (e o) -> p e o", o=1).broadcast_to([128, 8, 16]), OP.mult)
                    inc("sdv", i)

                @block.tensor
                def _(t):
                    wt(t, "sdv")
                    i = t.transpose(bankbf(6)[:, 0:128], rv(T4A, 128, BF16)[:, 0:128], sb["identbf"][:, :])
                    inc("spe", i)

                @block.vector
                def _(v, tb=tb):
                    wt(v, "spe")
                    i = v.tensor_copy(rv(HWT, 16 * 128, BF16)[:, tb * 128:(tb + 1) * 128], bankbf(6)[:, 0:128])
                    inc("sdv", i)

            for ob in range(8):
                for ks in range(4):
                    buf = ks % 2
                    wsl = rv(WS4, 8 * 512, BF16, off_b=buf * 8 * 1024)

                    wsem = "swa" if buf == 0 else "swb"

                    def ldw4(g, ob=ob, ks=ks, buf=buf, wsl=wsl, wsem=wsem, tbs=tbs):
                        if tbs == 0 and ob == 0 and ks < 2:
                            g.wait_ge(sems["swd"][0], WBASE_SWD)
                        if slab4[buf]:
                            wt(g, "spe", slab4[buf])
                        if ob == 0 and ks < 2:
                            wt(g, "sac")
                            wt(g, "sdv")
                        inc(wsem, g.dma_start(out=wsl.rearrange("p (k f) -> p k f", k=8),
                                              in_=w_base_bf[ob, ks * 8:(ks + 1) * 8].rearrange("k d f -> d k f")), 16)
                    if buf == 0:
                        block.scalar(ldw4)
                    else:
                        block.sync(ldw4)

                    @block.tensor
                    def _(t, ob=ob, tbs=tbs, ks=ks, wsl=wsl, wsem=wsem):
                        wt(t, wsem)
                        if ks == 0:
                            wt(t, "sdv")
                        for tb8 in range(8):
                            for k8 in range(8):
                                k = ks * 8 + k8
                                i = t.matmul(bank(tb8),
                                             ln2.rearrange("p (k c) -> p k c", k=KD)[:, k, tb8 * 128:(tb8 + 1) * 128],
                                             wsl[:, k8 * 512:(k8 + 1) * 512],
                                             start=(k == 0), stop=False)
                        inc("spe", i)
                    slab4[buf] = n_of("spe")

                @block.tensor
                def _(t, ob=ob, tbs=tbs):
                    wt(t, "swc")
                    for tb8 in range(8):
                        tb = tbs * 8 + tb8
                        t.matmul(bank(tb8), sb["onesbf"][0:1, 0:128],
                                 region[0:1, BBR * 1024:BBR * 1024 + D * 2].bitcast(BF16)[0:1, ob * 512:(ob + 1) * 512],
                                 start=False, stop=False)
                        i = t.matmul(bank(tb8), rv(HWT, 16 * 128, BF16)[:, tb * 128:(tb + 1) * 128],
                                     rv(BFL4, D, BF16)[:, ob * 512:(ob + 1) * 512],
                                     start=False, stop=True)
                    inc("spe", i)
                mm_done = n_of("spe")

                for tb8 in range(8):
                    tb = tbs * 8 + tb8
                    row0 = tb * 128

                    @block.sync
                    def _(s, ob=ob, row0=row0):
                        wt(s, "sdv")
                        inc("sra", s.dma_start(out=rv(T4A, 512, F32),
                                               in_=x2_sp[row0:row0 + 128, ob * 512:(ob + 1) * 512]), 16)

                    @block.vector
                    def _(v, tb8=tb8, mm_done=mm_done):
                        wt(v, "spe", mm_done)
                        wt(v, "sra")
                        wt(v, "srb")
                        i = v.tensor_tensor(rv(T4B, 512, F32), bank(tb8), rv(T4A, 512, F32), OP.add)
                        inc("sdv", i)

                    @block.sync
                    def _(s, ob=ob, row0=row0):
                        wt(s, "sdv")
                        inc("srb", s.dma_start(out=y_out[row0:row0 + 128, ob * 512:(ob + 1) * 512],
                                               in_=rv(T4B, 512, F32)), 16)

    @block.sync
    def _(s):
        for nm in ["sio", "sw", "swa", "swb", "swc", "swd", "sga", "snr", "srb", "scd", "sra",
                   "sxa", "sxb", "scr", "sgt", "snl"]:
            wt(s, nm)

    @block.vector
    def _(v):
        wt(v, "srb")

    block_cm.__exit__(None, None, None)
    cm.close()
    return nc, DBG



def _stage(inputs):
    ip = np.ascontiguousarray(inputs["L0_attn_in_proj_w"])
    rows = [np.arange(INTER + CONV_DIM, PROJ)]
    for g in range(NG):
        rows.append(np.arange(2 * INTER + g * SS, 2 * INTER + (g + 1) * SS))
        rows.append(np.arange(2 * INTER + NG * SS + g * SS, 2 * INTER + NG * SS + (g + 1) * SS))
        for j in range(8):
            rows.append(np.arange(INTER + g * GRP + j * 128, INTER + g * GRP + (j + 1) * 128))
    ridx = np.concatenate(rows)
    w_sub = ip[ridx]
    w_xbcdt_s = np.ascontiguousarray(w_sub.reshape(N_BLK, 128, KD, 128).transpose(0, 2, 3, 1))
    w_gate_s = np.ascontiguousarray(ip[0:INTER].reshape(16, 512, KD, 128).transpose(0, 2, 3, 1))
    wo = np.ascontiguousarray(inputs["L0_attn_out_proj_w"])
    w_out_s = np.ascontiguousarray(wo.T.reshape(64, 128, 8, 512).transpose(2, 0, 1, 3))
    wb = np.ascontiguousarray(inputs["L0_ffn_W_base"])
    w_base_s = np.ascontiguousarray(wb.T.reshape(KD, 128, 8, 512).transpose(2, 0, 1, 3))

    cw = np.ascontiguousarray(inputs["L0_attn_conv_w"]).reshape(CONV_DIM, 4)
    cb_ = np.ascontiguousarray(inputs["L0_attn_conv_b"])
    crows = []
    for g in range(NG):
        crows.append(np.arange(INTER + g * SS, INTER + (g + 1) * SS))
        crows.append(np.arange(INTER + NG * SS + g * SS, INTER + NG * SS + (g + 1) * SS))
        for j in range(8):
            crows.append(np.arange(g * GRP + j * 128, g * GRP + (j + 1) * 128))
    cidx = np.concatenate(crows)
    conv_taps_s = np.ascontiguousarray(cw[cidx].reshape(80, 128, 4))
    conv_bias_s = np.ascontiguousarray(cb_[cidx].reshape(80, 128))

    wg = np.ascontiguousarray(inputs["L0_ffn_W_gate"])
    wthr = np.ascontiguousarray(inputs["L0_ffn_W_thr"])
    wgt = np.zeros((D, 16), np.float32)
    wgt[:, 0:8] = wg.T
    wgt[:, 8:9] = wthr.T
    wg_thr_s = np.ascontiguousarray(wgt.reshape(KD, 128, 16))

    A = np.ascontiguousarray(inputs["L0_ffn_A"])
    a_flat_s = np.ascontiguousarray(A.reshape(128, D).T.reshape(KD, 128, 128))
    Bm = np.ascontiguousarray(inputs["L0_ffn_B"])
    b_flat_s = np.ascontiguousarray(Bm.transpose(0, 2, 1).reshape(128, D))

    norm_w_s = np.ascontiguousarray(inputs["L0_attn_norm_w"]).reshape(64, 128)

    common = dict(
        w_xbcdt=w_xbcdt_s, w_gate=w_gate_s, w_out=w_out_s, w_base=w_base_s,
        conv_taps=conv_taps_s, conv_bias=conv_bias_s,
        dt_bias=np.ascontiguousarray(inputs["L0_attn_dt_bias"]),
        d_vec=np.ascontiguousarray(inputs["L0_attn_D"]),
        norm_w=norm_w_s, wg_thr=wg_thr_s,
        b_thr=np.ascontiguousarray(inputs["L0_ffn_b_thr"]),
        b_base=np.ascontiguousarray(inputs["L0_ffn_b_base"]),
        a_flat=a_flat_s, b_flat=b_flat_s,
    )

    x = np.ascontiguousarray(inputs["L0_attn_hidden_states"])
    in_maps = []
    for c in range(8):
        b, h = c // 2, c % 2
        m = dict(common)
        m["x_own"] = np.ascontiguousarray(x[b, h * TOK:(h + 1) * TOK])
        m["x_pre"] = np.ascontiguousarray(x[b, 0:TOK])
        m["x_halo"] = np.ascontiguousarray(x[b, TOK - 3:TOK]) if h == 1 else np.ascontiguousarray(x[b, 0:3])
        fl = np.zeros((128, 4), np.float32)
        fl[:, 0] = float(h)
        fl[:, 1] = float(h)
        m["flags"] = fl
        in_maps.append(m)
    return in_maps


_CACHE = {}


def kernel(**inputs):
    taps = tuple(t for t in os.environ.get("KDBG", "").split(",") if t)
    nc, DBG = build_program(taps)
    in_maps = _stage(inputs)
    res = run_bass_kernel_spmd(nc, in_maps, core_ids=list(range(8)))
    _CACHE["res"] = res
    _CACHE["DBG"] = DBG
    out = np.zeros((4, 4096, D), np.float32)
    for c in range(8):
        b, h = c // 2, c % 2
        out[b, h * TOK:(h + 1) * TOK] = res.results[c]["y_out"]
    return out


# version 04
# speedup vs baseline: 1.1147x; 1.0924x over previous
import os
import numpy as np
import ml_dtypes
import concourse.bass as bass
import concourse.mybir as mybir
from concourse.bass_utils import run_bass_kernel_spmd

F32 = mybir.dt.float32
BF16 = mybir.dt.bfloat16
U8 = mybir.dt.uint8
AF = mybir.ActivationFunctionType
OP = mybir.AluOpType
AX = mybir.AxisListType

BF = ml_dtypes.bfloat16

D = 4096
NH, HD, SS, NG, CHUNK = 128, 64, 128, 8, 128
INTER = NH * HD
CONV_DIM = INTER + 2 * NG * SS
PROJ = INTER + CONV_DIM + NH
TOK = 2048
HALF = 1024
NCH = HALF // CHUNK
GRP = INTER // NG
HPG = NH // NG
KD = D // 128
EPS = 1e-5
DT_MIN, DT_MAX = 1e-3, 100.0
N_BLK = 1 + NG * 10



def build_program(dbg_taps=()):
    nc = bass.Bass("TRN2", target_bir_lowering=False)
    AP_IN = dict(kind="ExternalInput")

    x_own = nc.dram_tensor("x_own", [TOK, D], F32, **AP_IN).ap()
    x_pre = nc.dram_tensor("x_pre", [TOK, D], F32, **AP_IN).ap()
    x_halo = nc.dram_tensor("x_halo", [3, D], F32, **AP_IN).ap()
    flags = nc.dram_tensor("flags", [128, 4], F32, **AP_IN).ap()
    w_xbcdt = nc.dram_tensor("w_xbcdt", [N_BLK, KD, 128, 128], F32, **AP_IN).ap()
    w_gate = nc.dram_tensor("w_gate", [16, KD, 128, 512], F32, **AP_IN).ap()
    w_out = nc.dram_tensor("w_out", [8, 64, 128, 512], F32, **AP_IN).ap()
    w_base = nc.dram_tensor("w_base", [8, KD, 128, 512], F32, **AP_IN).ap()
    conv_taps = nc.dram_tensor("conv_taps", [80, 128, 4], F32, **AP_IN).ap()
    conv_bias = nc.dram_tensor("conv_bias", [80, 128], F32, **AP_IN).ap()
    dt_bias = nc.dram_tensor("dt_bias", [128], F32, **AP_IN).ap()
    d_vec = nc.dram_tensor("d_vec", [128], F32, **AP_IN).ap()
    norm_w = nc.dram_tensor("norm_w", [64, 128], F32, **AP_IN).ap()
    wg_thr = nc.dram_tensor("wg_thr", [KD, 128, 16], F32, **AP_IN).ap()
    b_thr = nc.dram_tensor("b_thr", [1], F32, **AP_IN).ap()
    b_base = nc.dram_tensor("b_base", [D], F32, **AP_IN).ap()
    a_flat = nc.dram_tensor("a_flat", [KD, 128, 128], F32, **AP_IN).ap()
    b_flat = nc.dram_tensor("b_flat", [128, D], F32, **AP_IN).ap()

    y_out = nc.dram_tensor("y_out", [TOK, D], F32, kind="ExternalOutput").ap()
    dbg = nc.dram_tensor("dbg", [128, 8192], F32, kind="ExternalOutput").ap()

    w_out_bf = nc.dram_tensor("w_out_bf", [8, 64, 128, 512], BF16, kind="Internal").ap()
    w_base_bf = nc.dram_tensor("w_base_bf", [8, KD, 128, 512], BF16, kind="Internal").ap()
    gate_sp = nc.dram_tensor("gate_sp", [TOK, INTER], BF16, kind="Internal").ap()
    normed_sp = nc.dram_tensor("normed_sp", [TOK, INTER], BF16, kind="Internal").ap()
    x2_sp = nc.dram_tensor("x2_sp", [TOK, D], F32, kind="Internal").ap()
    cs_dram = nc.dram_tensor("cs_dram", [NH, HALF], F32, kind="Internal").ap()
    cdexp_dram = nc.dram_tensor("cdexp_dram", [NCH, NH], F32, kind="Internal").ap()

    ident_bf_d = nc.inline_tensor(np.eye(128, dtype=BF), name="identbf_c").ap()
    ident_f_d = nc.inline_tensor(np.eye(128, dtype=np.float32), name="identf_c").ap()
    tril_d = nc.inline_tensor(
        (np.arange(128)[None, :] >= np.arange(128)[:, None]).astype(BF), name="tril_c").ap()
    avec_d = nc.inline_tensor(-(np.arange(128, dtype=np.float32) + 1.0)[:, None], name="avec_c").ap()
    onesbf_d = nc.inline_tensor(np.ones((1, 128), dtype=BF), name="onesbf_c").ap()

    sb = {}

    def alloc(name, cols, dtype):
        sb[name] = nc.alloc_sbuf_tensor("s_" + name, [128, cols], dtype).ap()
        return sb[name]

    alloc("identbf", 128, BF16)
    alloc("identf", 128, F32)
    alloc("tril", 128, BF16)
    alloc("avec", 1, F32)
    alloc("onesbf", 128, BF16)
    alloc("flags", 4, F32)
    alloc("dtb", 1, F32)
    alloc("bthr2", 1, F32)
    alloc("drep", 128, F32)
    alloc("convw", 80 * 4, F32)
    alloc("convb", 80, F32)
    alloc("normw", 64, F32)
    alloc("carry", NG * GRP, F32)
    alloc("halo", N_BLK * 4, BF16)
    alloc("halolnT", KD * 4, BF16)
    alloc("dtc_tm", NCH * 128, BF16)
    alloc("cs_tm", NCH * 128, F32)
    alloc("dec_tm", NCH * 128, BF16)
    alloc("ecs_tm", NCH * 128, BF16)
    alloc("ddt_tm", NCH * 128, BF16)
    alloc("cdrep", NCH * 128, F32)
    alloc("cslast", NCH, F32)
    alloc("stats", 16, F32)
    alloc("cpool", 4, F32)
    alloc("tiny", 64, F32)
    alloc("ssq", 8, F32)
    alloc("ffn", 64, F32)
    alloc("sink", 1024, BF16)
    sb["bthr_raw"] = nc.alloc_sbuf_tensor("s_bthr_raw", [128, 1], F32).ap()

    avail_kb = (nc.sbuf_bytes_remaining - 512) // 1024
    assert avail_kb >= 142, avail_kb
    region = nc.alloc_sbuf_tensor("s_region", [128, avail_kb * 1024], U8).ap()

    def rv(off_kb, cols, dtype, off_b=0):
        esz = {F32: 4, BF16: 2}[dtype]
        o = off_kb * 1024 + off_b
        return region[:, o:o + cols * esz].bitcast(dtype)

    LN_XT = 0
    XTILE = 64
    LNOUT = 96
    GWSL = 64
    GATET = 80
    WBLK = 64
    XG = 80
    GBC = 96
    BTM = 100
    SCM = 102
    XBC = 104
    DIAG = 109
    CSREP = 110
    LMS = 118
    XDT = 126
    XDTD = 130
    TVEC = 134
    TVD = 138
    YO = 140
    YG = 144
    GT2 = 148
    CBF = 150
    DTT = 110
    RALL = 118

    ps = nc.alloc_psum_tensor("ps", [128, 4096], F32).ap()

    def bank(i):
        return ps[:, i * 512:(i + 1) * 512]

    def bank2(i):
        return ps[:, i * 512:(i + 2) * 512]

    def bankbf(i, cols=1024):
        return ps.bitcast(BF16)[:, i * 1024:i * 1024 + cols]

    import contextlib
    cm = contextlib.ExitStack()
    sems = {}
    for n in ["sio", "sw", "spe", "sdv", "sac", "spl",
              "swa", "swb", "swc", "swd",
              "sxa", "sxb", "scr", "scd", "sgt", "snr", "sga", "sra", "srb", "snl"]:
        sems[n] = [cm.enter_context(nc.semaphore(n)), 0]

    def inc(name, instr, k=1):
        instr.then_inc(sems[name][0], k)
        sems[name][1] += k
        return sems[name][1]

    def n_of(name):
        return sems[name][1]

    def wt(eng, name, val=None):
        v = n_of(name) if val is None else val
        if v > 0:
            eng.wait_ge(sems[name][0], v)

    block_cm = nc.Block()
    block = block_cm.__enter__()

    nc.const_aps.register(F32, EPS, sb["cpool"][:, 0:1])

    @block.vector
    def _(v):
        v.memset(sb["cpool"][:, 0:1], EPS)
        v.drain()
        v.memset(sb["cpool"][:, 2:3], 1.0)
        v.drain()

    DBG = {}
    dbg_used = [0]

    def dbg_dump(tag, ap_src, cols):
        if tag not in dbg_taps:
            return
        off = dbg_used[0]
        assert off + cols <= 8192

        @block.sync
        def _(s):
            wt(s, "sdv"); wt(s, "sac"); wt(s, "spe"); wt(s, "spl")
            inc("sio", s.dma_start(out=dbg[:, off:off + cols], in_=ap_src), 16)

        @block.vector
        def _(v):
            wt(v, "sio")
            i = v.memset(sb["cpool"][:, 3:4], 0.0)
            v.drain()
            inc("sdv", i)
        dbg_used[0] = off + cols
        DBG[tag] = (off, cols)

    @block.sync
    def _(s):
        def ser(instr):
            inc("sio", instr, 16)
            wt(s, "sio")
        ser(s.dma_start(out=sb["identbf"][:, :], in_=ident_bf_d))
        ser(s.dma_start(out=sb["identf"][:, :], in_=ident_f_d))
        ser(s.dma_start(out=sb["tril"][:, :], in_=tril_d))
        ser(s.dma_start(out=sb["avec"][:, :], in_=avec_d))
        ser(s.dma_start(out=sb["onesbf"][0:1, :], in_=onesbf_d))
        ser(s.dma_start(out=sb["flags"][:, :], in_=flags))
        ser(s.dma_start(out=sb["dtb"][:, 0:1], in_=dt_bias.rearrange("(p o) -> p o", o=1)))
        ser(s.dma_start(out=sb["drep"][:, :], in_=d_vec.rearrange("(o h) -> o h", o=1).broadcast_to([128, 128])))
        ser(s.dma_start(out=sb["normw"][:, :], in_=norm_w.rearrange("k p -> p k"), allow_slow_non_contiguous=True))
        ser(s.dma_start(out=sb["bthr_raw"][:, 0:1], in_=b_thr.rearrange("(o e) -> o e", o=1).broadcast_to([128, 1])))
        ser(s.dma_start(out=sb["convb"][:, :], in_=conv_bias.rearrange("b p -> p b"), allow_slow_non_contiguous=True))

    @block.gpsimd
    def _(g):
        inc("swc", g.dma_start(out=sb["convw"][:, :].rearrange("p (b k) -> p b k", b=80), in_=conv_taps.rearrange("b p k -> p b k")), 16)
        wt(g, "swc")

    @block.vector
    def _(v):
        wt(v, "sio")
        i = v.tensor_scalar(sb["bthr2"][:, 0:1], sb["bthr_raw"][:, 0:1], 0.5, None, OP.mult)
        inc("sdv", i)

    xt_h = rv(XTILE, D, F32)
    lnout = rv(LNOUT, D, BF16)

    @block.sync
    def _(s):
        wt(s, "sio")
        inc("sio", s.dma_start(out=xt_h[0:3, :], in_=x_halo), 16)

    @block.vector
    def _(v):
        wt(v, "sio")
        v.reduce_sum(sb["tiny"][0:3, 0:1], xt_h[0:3, :], AX.X)
        v.drain()
        v.tensor_scalar(sb["tiny"][0:3, 1:2], sb["tiny"][0:3, 0:1], -1.0 / D, None, OP.mult)
        v.drain()
        i = v.tensor_scalar(xt_h[0:3, :], xt_h[0:3, :], sb["tiny"][0:3, 1:2], None, OP.add)
        v.drain()
        inc("sdv", i)

    @block.scalar
    def _(sc):
        wt(sc, "sdv")
        sc.activation(lnout[0:3, :], xt_h[0:3, :], AF.Square, accum_out=sb["tiny"][0:3, 2:3])
        sc.drain()
        sc.activation(sb["tiny"][0:3, 3:4], sb["tiny"][0:3, 2:3], AF.Ln, scale=1.0 / D, bias=EPS)
        sc.drain()
        inc("sac", sc.activation(sb["tiny"][0:3, 4:5], sb["tiny"][0:3, 3:4], AF.Exp, scale=-0.5))

    @block.vector
    def _(v):
        wt(v, "sac")
        i = v.tensor_scalar(lnout[0:3, :], xt_h[0:3, :], sb["tiny"][0:3, 4:5], None, OP.mult)
        v.drain()
        inc("sdv", i)

    @block.vector
    def _(v):
        i = v.memset(ps[:, 0:KD * 2], 0.0)
        inc("sdv", i)

    @block.tensor
    def _(t):
        wt(t, "sdv")
        for k in range(KD):
            i = t.transpose(bankbf(0)[:, k * 4:k * 4 + 4], lnout[0:4, k * 128:(k + 1) * 128],
                            sb["identbf"][0:4, 0:4])
        inc("spe", i)

    @block.vector
    def _(v):
        wt(v, "spe")
        i = v.tensor_copy(sb["halolnT"][:, :], bankbf(0)[:, 0:KD * 4])
        inc("sdv", i)

    @block.sync
    def _(s):
        wt(s, "sio")

    KSTAGE = int(os.environ.get("KSTAGE", "99"))

    cast_chunks = ([("wout", ob, q) for ob in range(8) for q in range(4)]
                   + [("wbase", ob, q) for ob in range(8) for q in range(2)])
    WOUT_SWD = 32 * 16
    WBASE_SWD = 48 * 16
    cast_emitted = [0]

    def emit_cast_chunk(n=1):
        for _ in range(n):
            if cast_emitted[0] >= len(cast_chunks):
                return
            kind, ob, q = cast_chunks[cast_emitted[0]]
            cast_emitted[0] += 1

            @block.gpsimd
            def _(g, kind=kind, ob=ob, q=q):
                if kind == "wout":
                    inc("swd", g.dma_start(out=w_out_bf[ob, q * 16:(q + 1) * 16],
                                           in_=w_out[ob, q * 16:(q + 1) * 16]), 16)
                else:
                    inc("swd", g.dma_start(out=w_base_bf[ob, q * 16:(q + 1) * 16],
                                           in_=w_base[ob, q * 16:(q + 1) * 16]), 16)

    def flush_casts():
        emit_cast_chunk(len(cast_chunks))

    def emit_ln(src_dram, src_row0, dstv, span, dst_off):
        entry_sdv = n_of("sdv")
        entry_spe = n_of("spe")
        entry_sac = n_of("sac")
        xt_guard = {}
        lnout_guard = [0]
        bank_guard = [0]
        for t8 in range(NCH):
            buf = t8 % 2
            xt = rv(XTILE, D, F32, off_b=buf * D * 4)
            st = sb["stats"][:, buf * 8:buf * 8 + 8]
            row0 = src_row0 + t8 * 128

            @block.sync
            def _(s, t8=t8, row0=row0, xt=xt):
                if t8 - 2 in xt_guard:
                    wt(s, "sac", xt_guard[t8 - 2])
                if t8 < 2:
                    wt(s, "sdv", entry_sdv)
                    wt(s, "spe", entry_spe)
                    wt(s, "sac", entry_sac)
                inc("sxa" if t8 % 2 == 0 else "sxb", s.dma_start(out=xt[:, :], in_=src_dram[row0:row0 + 128, :]), 16)

            @block.vector
            def _(v, xt=xt, t8=t8, st=st):
                wt(v, "sxa" if t8 % 2 == 0 else "sxb")
                v.reduce_sum(st[:, 0:1], xt[:, :], AX.X)
                v.drain()
                i = v.tensor_scalar(st[:, 1:2], st[:, 0:1], -1.0 / D, None, OP.mult)
                inc("sdv", i)

            @block.scalar
            def _(sc, xt=xt, st=st):
                wt(sc, "sdv")
                for q in range(4):
                    i = sc.activation(sb["sink"][:, :], xt[:, q * 1024:(q + 1) * 1024],
                                      AF.Square, bias=st[:, 1:2],
                                      accum_out=st[:, 2 + q:3 + q])
                inc("sac", i)

            @block.vector
            def _(v, st=st):
                wt(v, "sac")
                i = v.reduce_sum(st[:, 6:7], st[:, 2:6], AX.X)
                inc("sdv", i)

            @block.scalar
            def _(sc, st=st):
                wt(sc, "sdv")
                sc.activation(st[:, 7:8], st[:, 6:7], AF.Ln, scale=1.0 / D, bias=EPS)
                sc.drain()
                inc("sac", sc.activation(st[:, 7:8], st[:, 7:8], AF.Exp, scale=-0.5))

            @block.vector
            def _(v, st=st):
                wt(v, "sac")
                i = v.tensor_tensor(st[:, 0:1], st[:, 1:2], st[:, 7:8], OP.mult)
                inc("sdv", i)

            @block.scalar
            def _(sc, xt=xt, st=st):
                wt(sc, "sdv")
                if lnout_guard[0]:
                    wt(sc, "spe", lnout_guard[0])
                i = sc.activation(lnout[:, :], xt[:, :], AF.Identity,
                                  scale=st[:, 7:8], bias=st[:, 0:1])
                inc("sac", i)
            xt_guard[t8] = n_of("sac")

            @block.tensor
            def _(t):
                wt(t, "sac")
                if bank_guard[0]:
                    wt(t, "sdv", bank_guard[0])
                for k in range(KD):
                    b4 = k // 8
                    i = t.transpose(bankbf(b4)[:, (k % 8) * 128:(k % 8) * 128 + 128],
                                    lnout[:, k * 128:(k + 1) * 128], sb["identbf"][:, :])
                    if k == KD - 1:
                        inc("spe", i)
            lnout_guard[0] = n_of("spe")

            @block.vector
            def _(v, t8=t8):
                wt(v, "spe")
                for b4 in range(4):
                    i = v.tensor_copy(
                        dstv.rearrange("p (k c) -> p k c", c=span)[:, b4 * 8:(b4 + 1) * 8,
                                                                  dst_off + t8 * 128: dst_off + t8 * 128 + 128],
                        bankbf(b4).rearrange("p (k c) -> p k c", k=8))
                    if b4 == 3:
                        inc("sdv", i)
            bank_guard[0] = n_of("sdv")

    def emit_gate(half):
        slab_guard = [0, 0]
        evac_of = {}
        spill_guard = [0]
        lnv = rv(LN_XT, KD * HALF, BF16)
        for fb in range(16):
            for kh in range(4):
                buf = (fb * 4 + kh) % 2
                wsl = rv(GWSL, 8 * 512, BF16, off_b=buf * 8 * 1024)

                wsem = "swa" if buf == 0 else "swb"

                @block.gpsimd
                def _(g, fb=fb, kh=kh, buf=buf, wsl=wsl, wsem=wsem):
                    if slab_guard[buf]:
                        wt(g, "spe", slab_guard[buf])
                    wt(g, "sdv", ipb["lnguard"])
                    inc(wsem, g.dma_start(out=wsl.rearrange("p (k f) -> p k f", k=8),
                                          in_=w_gate[fb, kh * 8:(kh + 1) * 8].rearrange("k d f -> d k f")), 16)

                @block.tensor
                def _(t, fb=fb, kh=kh, wsl=wsl, wsem=wsem):
                    wt(t, wsem)
                    if fb == 0 and kh == 0 and ipb["ssd_guard"]:
                        wt(t, "sdv", ipb["ssd_guard"])
                    for tb in range(NCH):
                        if kh == 0:
                            key = (fb - 1, tb)
                            if key in evac_of:
                                wt(t, "sac", evac_of[key])
                        for k8 in range(8):
                            k = kh * 8 + k8
                            i = t.matmul(bank(tb),
                                         lnv.rearrange("p (k c) -> p k c", k=KD)[:, k, tb * 128:tb * 128 + 128],
                                         wsl[:, k8 * 512:(k8 + 1) * 512],
                                         start=(k == 0), stop=(k == KD - 1))
                    inc("spe", i)
                slab_guard[buf] = n_of("spe")
            pe_end = n_of("spe")

            for tb in range(NCH):
                @block.scalar
                def _(sc, tb=tb, fb=fb):
                    wt(sc, "spe", pe_end)
                    if spill_guard[0]:
                        wt(sc, "sga", spill_guard[0])
                    i = sc.activation(rv(GATET, 512, BF16), bank(tb), AF.Silu)
                    evac_of[(fb, tb)] = inc("sac", i)

                @block.sync
                def _(s, tb=tb, fb=fb):
                    wt(s, "sac")
                    spill_guard[0] = inc("sga", s.dma_start(
                        out=gate_sp[half * HALF + tb * 128: half * HALF + (tb + 1) * 128,
                                    fb * 512:(fb + 1) * 512],
                        in_=rv(GATET, 512, BF16)), 16)

    def emit_dt():
        za = rv(DTT, HALF, F32)
        zb = rv(DTT, HALF, F32, off_b=4096)
        dtc = rv(YO, HALF, F32)
        cs = rv(TVEC, HALF, F32)

        @block.vector
        def _(v):
            wt(v, "spe")
            i = v.tensor_scalar(za[:, :], ps[:, 0:HALF], sb["dtb"][:, 0:1], None, OP.add)
            inc("sdv", i)

        @block.scalar
        def _(sc):
            wt(sc, "sdv")
            sc.activation(zb[:, :], za[:, :], AF.Abs)
            sc.drain()
            sc.activation(zb[:, :], zb[:, :], AF.Exp, scale=-1.0)
            sc.drain()
            inc("sac", sc.activation(zb[:, :], zb[:, :], AF.Ln, bias=1.0))

        @block.vector
        def _(v):
            wt(v, "sac")
            v.tensor_scalar(dtc[:, :], za[:, :], 0.0, None, OP.max)
            v.drain()
            v.tensor_tensor(dtc[:, :], dtc[:, :], zb[:, :], OP.add)
            v.drain()
            v.tensor_scalar(dtc[:, :], dtc[:, :], DT_MIN, DT_MAX, OP.max, OP.min)
            v.drain()
            v.tensor_scalar(za[:, :], dtc[:, :], sb["avec"][:, 0:1], None, OP.mult)
            v.drain()
            for c in range(NCH):
                v.tensor_tensor_scan(cs[:, c * 128:(c + 1) * 128],
                                     sb["cpool"][:, 2:3].broadcast_to([128, 128]),
                                     za[:, c * 128:(c + 1) * 128], 0.0, OP.mult, OP.add)
            v.drain()
            for c in range(NCH):
                v.tensor_scalar(zb[:, c * 128:(c + 1) * 128], cs[:, c * 128:(c + 1) * 128],
                                cs[:, c * 128 + 127:c * 128 + 128], None, OP.subtract)
            i = v.tensor_copy(sb["cslast"][:, :], cs[:, 127::128])
            inc("sdv", i)

        @block.scalar
        def _(sc):
            wt(sc, "sdv")
            sc.activation(zb[:, :], zb[:, :], AF.Exp, scale=-1.0)
            i = sc.activation(sb["tiny"][:, 0:NCH], sb["cslast"][:, :], AF.Exp)
            inc("sac", i)

        @block.sync
        def _(s):
            wt(s, "sdv"); wt(s, "sac")
            inc("scd", s.dma_start(out=cs_dram, in_=cs[:, :]), 16)
            wt(s, "scd")
            inc("scd", s.dma_start(out=cdexp_dram, in_=sb["tiny"][:, 0:NCH],
                                   allow_slow_non_contiguous=True), 16)
            wt(s, "scd")
            inc("scd", s.dma_start(out=sb["cdrep"][:, :].rearrange("p (c h) -> p c h", c=NCH),
                                   in_=cdexp_dram.rearrange("(o c) h -> o c h", o=1).broadcast_to([128, NCH, 128])), 16)

        for c in range(NCH):
            @block.tensor
            def _(t, c=c):
                wt(t, "sdv"); wt(t, "sac")
                t.transpose(bank(0)[:, 0:128], dtc[:, c * 128:(c + 1) * 128], sb["identf"][:, :])
                t.transpose(bank(1)[:, 0:128], cs[:, c * 128:(c + 1) * 128], sb["identf"][:, :])
                i = t.transpose(bank(2)[:, 0:128], zb[:, c * 128:(c + 1) * 128], sb["identf"][:, :])
                inc("spe", i)

            @block.vector
            def _(v, c=c):
                wt(v, "spe")
                v.reciprocal(rv(RALL, HALF, F32)[:, c * 128:(c + 1) * 128], bank(0)[:, 0:128])
                v.tensor_copy(sb["dtc_tm"][:, c * 128:(c + 1) * 128], bank(0)[:, 0:128])
                v.tensor_copy(sb["cs_tm"][:, c * 128:(c + 1) * 128], bank(1)[:, 0:128])
                i = v.tensor_copy(sb["dec_tm"][:, c * 128:(c + 1) * 128], bank(2)[:, 0:128])
                inc("sdv", i)

        @block.vector
        def _(v):
            v.drain()
            for c in range(NCH):
                i = v.tensor_tensor(sb["ddt_tm"][:, c * 128:(c + 1) * 128],
                                    rv(RALL, HALF, F32)[:, c * 128:(c + 1) * 128],
                                    sb["drep"][:, :], OP.mult)
            inc("sdv", i)

        @block.scalar
        def _(sc):
            wt(sc, "sdv")
            inc("sac", sc.activation(sb["ecs_tm"][:, :], sb["cs_tm"][:, :], AF.Exp))

    ipb = {"buf": 0, "xbuf": 0, "pe_guard": [0, 0], "evac_guard": 0,
           "conv_pe": [0, 0], "silu_guard": 0, "lnguard": 0, "gate_io": 0, "bank_sac": 0,
           "ssd_guard": 0}

    def emit_inproj_block(blk, dst_kb, is_r, first_half):
        buf = ipb["buf"]; ipb["buf"] ^= 1
        wsl = rv(WBLK, KD * 128, BF16, off_b=buf * KD * 256)
        is_conv = blk != 0
        cb = blk - 1
        lnv = rv(LN_XT, KD * HALF, BF16)

        wsem = "swa" if buf == 0 else "swb"

        @block.gpsimd
        def _(g, buf=buf, wsl=wsl, blk=blk, wsem=wsem):
            if ipb["pe_guard"][buf]:
                wt(g, "spe", ipb["pe_guard"][buf])
            wt(g, "sdv", ipb["lnguard"])
            inc(wsem, g.dma_start(out=wsl.rearrange("p (k f) -> p k f", k=KD), in_=w_xbcdt[blk].rearrange("k d f -> d k f")), 16)

        @block.tensor
        def _(t, wsl=wsl, blk=blk, wsem=wsem):
            wt(t, wsem)
            if ipb["evac_guard"]:
                wt(t, "sdv", ipb["evac_guard"])
            if ipb["ssd_guard"]:
                wt(t, "sdv", ipb["ssd_guard"])
            if ipb["bank_sac"]:
                wt(t, "sac", ipb["bank_sac"])
            for tt in range(2):
                for k in range(KD):
                    i = t.matmul(bank(tt), wsl[:, k * 128:(k + 1) * 128],
                                 lnv.rearrange("p (k c) -> p k c", k=KD)[:, k, tt * 512:(tt + 1) * 512],
                                 start=(k == 0), stop=(k == KD - 1))
            if is_conv and (not is_r) and first_half:
                wt(t, "sac", ipb["silu_guard"])
                for k in range(KD):
                    i = t.matmul(bank(2)[:, 0:3], wsl[:, k * 128:(k + 1) * 128],
                                 sb["halolnT"][:, :].rearrange("p (k c) -> p k c", k=KD)[:, k, 0:3],
                                 start=(k == 0), stop=(k == KD - 1))
            inc("spe", i)
        ipb["pe_guard"][buf] = n_of("spe")

        if not is_conv:
            emit_dt()
            ipb["evac_guard"] = n_of("sdv")
            return

        xbuf = ipb["xbuf"]; ipb["xbuf"] ^= 1
        xb = rv(XBC, HALF + 8, BF16, off_b=xbuf * 2560)

        @block.vector
        def _(v, xb=xb, xbuf=xbuf, blk=blk, cb=cb):
            wt(v, "spe")
            if ipb["conv_pe"][xbuf]:
                wt(v, "spe", ipb["conv_pe"][xbuf])
            if is_r and first_half:
                v.memset(xb[:, 0:3], 0.0)
            elif is_r or (not first_half):
                v.tensor_copy(xb[:, 0:3], sb["halo"][:, blk * 4:blk * 4 + 3])
            else:
                v.tensor_scalar(xb[:, 0:3], bank(2)[:, 0:3], sb["flags"][:, 1:2], None, OP.mult)
            v.tensor_copy(xb[:, 3:3 + HALF], ps[:, 0:HALF])
            v.drain()
            v.tensor_copy(sb["halo"][:, blk * 4:blk * 4 + 3], xb[:, HALF:HALF + 3])
            for k in range(4):
                i = v.tensor_scalar(rv(DIAG, 512, BF16)[:, k * 128:(k + 1) * 128], sb["identbf"][:, :],
                                    sb["convw"][:, cb * 4 + k:cb * 4 + k + 1], None, OP.mult)
            inc("sdv", i)
        ipb["evac_guard"] = n_of("sdv")

        @block.tensor
        def _(t, xb=xb):
            wt(t, "sdv")
            wt(t, "sac", ipb["silu_guard"])
            for tt in range(2):
                for k in range(4):
                    i = t.matmul(bank(2 + tt), rv(DIAG, 512, BF16)[:, k * 128:(k + 1) * 128],
                                 xb[:, tt * 512 + k: tt * 512 + k + 512],
                                 start=(k == 0), stop=(k == 3))
            inc("spe", i)
        ipb["conv_pe"][xbuf] = n_of("spe")

        @block.scalar
        def _(sc, dst_kb=dst_kb, cb=cb):
            wt(sc, "spe")
            if ipb["gate_io"]:
                wt(sc, "sga", ipb["gate_io"])
            dstv = rv(dst_kb[0], HALF, BF16, off_b=dst_kb[1])
            sc.activation(dstv[:, 0:512], bank(2), AF.Silu, bias=sb["convb"][:, cb:cb + 1])
            i = sc.activation(dstv[:, 512:1024], bank(3), AF.Silu, bias=sb["convb"][:, cb:cb + 1])
            inc("sac", i)
        ipb["silu_guard"] = n_of("sac")

    ssd_prev = {"exp": 0, "ydiag": 0, "ygm": 0, "snr": 0, "tvd": 0,
                "states": 0, "yoff_free": 0}

    def emit_group_ssd(g, half, is_r):
        gbc = rv(GBC, 2 * HALF, BF16)
        xg = rv(XG, 8 * HALF, BF16)
        btm = rv(BTM, NCH * 128, BF16)
        scm = rv(SCM, NCH * 128, BF16)
        csrep = rv(CSREP, 2048, F32)
        tvs = rv(TVEC, GRP, F32)
        tvd = rv(TVD, GRP, BF16)
        yo = rv(YO, GRP, F32)
        gt2 = rv(GT2, GRP, BF16)
        cbf = rv(CBF, GRP, BF16)
        cslice = sb["carry"][:, g * GRP:(g + 1) * GRP]

        def lmsb(c):
            return rv(LMS, 2048, BF16, off_b=(c % 2) * 4096)

        def xdtb(c):
            return rv(XDT, GRP, BF16, off_b=(c % 2) * 2048)

        def xdtdb(c):
            return rv(XDTD, GRP, BF16, off_b=(c % 2) * 2048)

        def ygb(c):
            return rv(YG, GRP, BF16, off_b=(c % 2) * 2048)

        def hsl(c):
            return sb["dtc_tm"][:, c * 128 + g * HPG: c * 128 + g * HPG + HPG]

        entry_sdv = n_of("sdv")

        for cq in range(2):
            @block.tensor
            def _(t, cq=cq):
                wt(t, "sac", ipb["silu_guard"])
                wt(t, "sdv")
                for c in range(cq * 4, cq * 4 + 4):
                    i = t.transpose(bankbf(4)[:, (c % 4) * 128:(c % 4) * 128 + 128],
                                    gbc[:, c * 128:(c + 1) * 128], sb["identbf"][:, :])
                    if not is_r:
                        i = t.matmul(bank(5)[:, (c % 4) * 128:(c % 4) * 128 + 128],
                                     gbc[:, c * 128:(c + 1) * 128],
                                     gbc[:, HALF + c * 128: HALF + (c + 1) * 128],
                                     start=True, stop=True)
                inc("spe", i)

            @block.vector
            def _(v, cq=cq):
                wt(v, "spe")
                for c in range(cq * 4, cq * 4 + 4):
                    i = v.tensor_copy(btm[:, c * 128:(c + 1) * 128],
                                      bankbf(4)[:, (c % 4) * 128:(c % 4) * 128 + 128])
                if not is_r:
                    for c in range(cq * 4, cq * 4 + 4):
                        i = v.scalar_tensor_tensor(scm[:, c * 128:(c + 1) * 128],
                                                   bank(5)[:, (c % 4) * 128:(c % 4) * 128 + 128],
                                                   1.0, sb["tril"][:, :], OP.mult, OP.mult)
                inc("sdv", i)

        cbf0 = 0
        if not is_r:
            @block.vector
            def _(v):
                wt(v, "spl"); wt(v, "sdv")
                i = v.tensor_copy(cbf[:, :], cslice)
                inc("sdv", i)
            cbf0 = n_of("sdv")
        head_sdv = n_of("sdv")

        R = {k: {} for k in ["s1", "xdt", "yoff", "states", "cbf", "s7", "sub",
                             "exp", "lms", "ydiag", "s12", "yofree", "ygm",
                             "tvd", "norm", "nrm", "snr"]}

        def emit_tail(cc):
            par = cc % 2

            @block.scalar
            def _(sc, cc=cc, par=par):
                wt(sc, "sdv", R["ygm"][cc])
                q = par * 3
                sc.activation(sb["sink"][:, :], ygb(cc)[:, :], AF.Square,
                              accum_out=sb["ssq"][:, q:q + 1])
                sc.drain()
                sc.activation(sb["ssq"][:, q + 1:q + 2], sb["ssq"][:, q:q + 1],
                              AF.Ln, scale=1.0 / GRP, bias=EPS)
                sc.drain()
                i = sc.activation(sb["ssq"][:, q + 2:q + 3], sb["ssq"][:, q + 1:q + 2],
                                  AF.Exp, scale=-0.5)
                R["norm"][cc] = inc("sac", i)

            @block.vector
            def _(v, cc=cc, par=par):
                wt(v, "sac", R["norm"][cc])
                i = v.tensor_scalar(ygb(cc)[:, :], ygb(cc)[:, :],
                                    sb["ssq"][:, par * 3 + 2:par * 3 + 3], None, OP.mult)
                R["nrm"][cc] = inc("sdv", i)

            @block.sync
            def _(s, cc=cc):
                wt(s, "sdv", R["nrm"][cc])
                R["snr"][cc] = inc("snr", s.dma_start(
                    out=normed_sp[half * HALF + cc * 128: half * HALF + (cc + 1) * 128,
                                  g * GRP:(g + 1) * GRP],
                    in_=ygb(cc)[:, :]), 16)

        for c in range(NCH):
            @block.tensor
            def _(t, c=c):
                wt(t, "sdv", R["xdt"].get(c - 1, head_sdv))
                for j in range(8):
                    i = t.transpose(bankbf(4)[:, j * 128:(j + 1) * 128],
                                    xg[:, j * HALF + c * 128:j * HALF + (c + 1) * 128],
                                    sb["identbf"][:, :])
                R["s1"][c] = inc("spe", i)

            @block.vector
            def _(v, c=c):
                wt(v, "spe", R["s1"][c])
                wt(v, "spe", R["ydiag" if not is_r else "states"].get(
                    c - 2, ssd_prev["ydiag" if not is_r else "states"]))
                if not is_r:
                    wt(v, "spl", R["tvd"].get(c - 2, ssd_prev["tvd"]))
                v.tensor_tensor(xdtb(c).rearrange("p (h r) -> p h r", h=HPG),
                                bankbf(4).rearrange("p (h r) -> p h r", h=HPG),
                                hsl(c).rearrange("p (h o) -> p h o", o=1)
                                .broadcast_to([128, HPG, 64]), OP.mult)
                v.drain()
                i = v.tensor_tensor(xdtdb(c).rearrange("p (h r) -> p h r", h=HPG),
                                    xdtb(c).rearrange("p (h r) -> p h r", h=HPG),
                                    sb["dec_tm"][:, c * 128 + g * HPG: c * 128 + g * HPG + HPG]
                                    .rearrange("p (h o) -> p h o", o=1)
                                    .broadcast_to([128, HPG, 64]), OP.mult)
                R["xdt"][c] = inc("sdv", i)

            @block.tensor
            def _(t, c=c):
                wt(t, "sdv", R["xdt"][c])
                wt(t, "sdv", R["cbf"].get(c - 1, cbf0))
                if not is_r:
                    wt(t, "sdv", R["yofree"].get(c - 1, ssd_prev["yoff_free"]))
                    for hh in range(2):
                        i = t.matmul(bank(hh), gbc[:, HALF + c * 128:HALF + (c + 1) * 128],
                                     cbf[:, hh * 512:(hh + 1) * 512],
                                     start=True, stop=True)
                    R["yoff"][c] = inc("spe", i)
                for hh in range(2):
                    i = t.matmul(bank(2 + hh), btm[:, c * 128:(c + 1) * 128],
                                 xdtdb(c)[:, hh * 512:(hh + 1) * 512],
                                 start=True, stop=True)
                R["states"][c] = inc("spe", i)

            @block.gpsimd
            def _(gp, c=c):
                wt(gp, "spe", R["states"][c])
                if c == 0:
                    wt(gp, "scd")
                i = gp.tensor_tensor(tvs.rearrange("p (h r) -> p h r", h=HPG),
                                     cslice.rearrange("p (h r) -> p h r", h=HPG),
                                     sb["cdrep"][:, c * 128 + g * HPG: c * 128 + g * HPG + HPG]
                                     .rearrange("p (h o) -> p h o", o=1)
                                     .broadcast_to([128, HPG, 64]), OP.mult)
                R["s5"] = inc("spl", i)

            if not is_r:
                @block.gpsimd
                def _(gp, c=c):
                    wt(gp, "sdv", R["xdt"][c])
                    wt(gp, "sdv", R["ygm"].get(c - 1, ssd_prev["ygm"]))
                    i = gp.tensor_tensor(tvd.rearrange("p (h r) -> p h r", h=HPG),
                                         xdtb(c).rearrange("p (h r) -> p h r", h=HPG),
                                         sb["ddt_tm"][:, c * 128 + g * HPG: c * 128 + g * HPG + HPG]
                                         .rearrange("p (h o) -> p h o", o=1)
                                         .broadcast_to([128, HPG, 64]), OP.mult)
                    R["tvd"][c] = inc("spl", i)

            @block.vector
            def _(v, c=c):
                wt(v, "spl", R["s5"])
                v.tensor_tensor(cslice, tvs[:, :], bank2(2), OP.add)
                v.drain()
                if not is_r:
                    i = v.tensor_copy(cbf[:, :], cslice)
                else:
                    i = v.memset(sb["cpool"][:, 3:4], 0.0)
                R["cbf"][c] = inc("sdv", i)

            if is_r:
                continue

            @block.sync
            def _(s, c=c):
                wt(s, "sac", R["exp"].get(c - 1, ssd_prev["exp"]))
                if c == 0:
                    wt(s, "sdv", head_sdv)
                R["s7"][c] = inc("scr", s.dma_start(
                    out=csrep.rearrange("p (h l) -> p h l", h=HPG),
                    in_=cs_dram[g * HPG:(g + 1) * HPG, c * 128:(c + 1) * 128]
                    .rearrange("(o h) l -> o h l", o=1).broadcast_to([128, HPG, 128])), 16)

            @block.vector
            def _(v, c=c):
                wt(v, "scr", R["s7"][c])
                i = v.tensor_tensor(csrep.rearrange("p (h l) -> p h l", h=HPG),
                                    csrep.rearrange("p (h l) -> p h l", h=HPG),
                                    sb["cs_tm"][:, c * 128 + g * HPG: c * 128 + g * HPG + HPG]
                                    .rearrange("p (h o) -> p h o", o=1)
                                    .broadcast_to([128, HPG, 128]), OP.subtract)
                R["sub"][c] = inc("sdv", i)

            @block.scalar
            def _(sc, c=c):
                wt(sc, "sdv", R["sub"][c])
                wt(sc, "spe", R["ydiag"].get(c - 2, ssd_prev["ydiag"]))
                i = sc.activation(lmsb(c)[:, :], csrep[:, :], AF.Exp)
                R["exp"][c] = inc("sac", i)

            @block.vector
            def _(v, c=c):
                wt(v, "sac", R["exp"][c])
                i = v.scalar_tensor_tensor(lmsb(c).rearrange("p (h l) -> p h l", h=HPG),
                                           lmsb(c).rearrange("p (h l) -> p h l", h=HPG), 1.0,
                                           scm[:, c * 128:(c + 1) * 128]
                                           .unsqueeze(1).broadcast_to([128, HPG, 128]),
                                           OP.min, OP.mult)
                R["lms"][c] = inc("sdv", i)

            @block.tensor
            def _(t, c=c):
                wt(t, "sdv", R["lms"][c])
                wt(t, "sdv", R["ygm"].get(c - 1, ssd_prev["ygm"]))
                for h in range(HPG):
                    i = t.matmul(ps[:, 6 * 512 + h * 64: 6 * 512 + (h + 1) * 64],
                                 lmsb(c)[:, h * 128:(h + 1) * 128],
                                 xdtb(c)[:, h * 64:(h + 1) * 64], start=True, stop=True)
                R["ydiag"][c] = inc("spe", i)

            @block.sync
            def _(s, c=c):
                wt(s, "sdv", R["ygm"].get(c - 1, ssd_prev["ygm"]))
                wt(s, "sga", ipb["gate_io"])
                R["s12"][c] = inc("sgt", s.dma_start(
                    out=gt2[:, :],
                    in_=gate_sp[half * HALF + c * 128: half * HALF + (c + 1) * 128,
                                g * GRP:(g + 1) * GRP]), 16)

            @block.vector
            def _(v, c=c):
                wt(v, "spe", R["yoff"][c])
                wt(v, "sdv", R["ygm"].get(c - 1, ssd_prev["ygm"]))
                i = v.tensor_tensor(yo.rearrange("p (h r) -> p h r", h=HPG),
                                    bank2(0).rearrange("p (h r) -> p h r", h=HPG),
                                    sb["ecs_tm"][:, c * 128 + g * HPG: c * 128 + g * HPG + HPG]
                                    .rearrange("p (h o) -> p h o", o=1)
                                    .broadcast_to([128, HPG, 64]), OP.mult)
                R["yofree"][c] = inc("sdv", i)

            @block.vector
            def _(v, c=c):
                wt(v, "spe", R["ydiag"][c])
                wt(v, "sgt", R["s12"][c])
                wt(v, "spl", R["tvd"][c])
                wt(v, "snr", R["snr"].get(c - 2, ssd_prev["snr"]))
                v.tensor_tensor(yo[:, :], yo[:, :], ps[:, 6 * 512:8 * 512], OP.add)
                v.drain()
                v.tensor_tensor(yo[:, :], yo[:, :], tvd[:, :], OP.add)
                v.drain()
                i = v.tensor_tensor(ygb(c)[:, :], yo[:, :], gt2[:, :], OP.mult)
                R["ygm"][c] = inc("sdv", i)

            if c >= 1:
                emit_tail(c - 1)

        if not is_r:
            emit_tail(NCH - 1)
            ssd_prev["exp"] = R["exp"][NCH - 1]
            ssd_prev["ydiag"] = R["ydiag"][NCH - 1]
            ssd_prev["ygm"] = R["ygm"][NCH - 1]
            ssd_prev["snr"] = R["snr"][NCH - 1]
            ssd_prev["tvd"] = R["tvd"][NCH - 1]
            ssd_prev["yoff_free"] = R["yofree"][NCH - 1]
        ssd_prev["states"] = R["states"][NCH - 1]
        if not is_r:
            ipb["ssd_guard"] = max(R["yofree"][NCH - 1], R["cbf"][NCH - 1],
                                   R["xdt"][NCH - 1], R["ygm"][NCH - 1])
        else:
            ipb["ssd_guard"] = max(R["cbf"][NCH - 1], R["xdt"][NCH - 1])

    def do_pass(is_r, half):
        src = x_pre if is_r else x_own
        emit_ln(src, half * HALF, rv(LN_XT, KD * HALF, BF16), HALF, 0)
        ipb["lnguard"] = n_of("sdv")
        ipb["gate_io"] = 0
        if (not is_r) and KSTAGE >= 3:
            emit_gate(half)
            ipb["gate_io"] = n_of("sga")
            ipb["pe_guard"] = [n_of("spe"), n_of("spe")]
            ipb["bank_sac"] = n_of("sac")
        emit_inproj_block(0, None, is_r, half == 0)
        if KSTAGE < 2:
            return
        for g in range(NG):
            base = 1 + g * 10
            emit_inproj_block(base, (GBC, 0), is_r, half == 0)
            if not is_r:
                emit_inproj_block(base + 1, (GBC, HALF * 2), is_r, half == 0)
            for j in range(8):
                emit_inproj_block(base + 2 + j, (XG, j * HALF * 2), is_r, half == 0)
            if KSTAGE >= 4:
                emit_group_ssd(g, half, is_r)
            if is_r:
                emit_cast_chunk(3)

    @block.vector
    def _(v):
        wt(v, "sio")
        i = v.memset(sb["carry"][:, :], 0.0)
        inc("sdv", i)

    if KSTAGE >= 5:
        do_pass(True, 0)
        do_pass(True, 1)

        @block.vector
        def _(v):
            wt(v, "spl"); wt(v, "sdv")
            i = v.tensor_scalar(sb["carry"][:, :], sb["carry"][:, :], sb["flags"][:, 0:1], None, OP.mult)
            inc("sdv", i)

        if "cin" in dbg_taps:
            dbg_dump("cin", sb["carry"][:, 0:4096], 4096)

    do_pass(False, 0)
    if KSTAGE >= 6:
        do_pass(False, 1)

    if "dtc" in dbg_taps:
        dbg_dump("dtc", rv(YO, HALF, F32)[:, :], HALF)
    if "cs" in dbg_taps:
        dbg_dump("cs", rv(TVEC, HALF, F32)[:, :], HALF)
    if "carry" in dbg_taps:
        dbg_dump("carry", sb["carry"][:, 0:4096], 4096)
    if "xg" in dbg_taps:
        dbg_dump("xg", rv(XG, 8 * HALF, BF16).bitcast(F32), 4096)
    if "bc" in dbg_taps:
        dbg_dump("bc", rv(GBC, 2 * HALF, BF16).bitcast(F32), 1024)
    if "y" in dbg_taps:
        dbg_dump("y", rv(YO, GRP, F32), 1024)
    if "nrm" in dbg_taps:
        dbg_dump("nrm", rv(YG, GRP, BF16).bitcast(F32), 512)

    flush_casts()
    NTQ, NSL, WS3, T3A, T3B = 0, 64, 80, 88, 91
    if KSTAGE >= 7:
        slab_guard = [0, 0]
        for tq in range(4):
            nT = rv(NTQ, 64 * 512, BF16)
            tile_guard = [0]
            for tb in range(4):
                @block.sync
                def _(s, tq=tq, tb=tb):
                    if tile_guard[0]:
                        wt(s, "spe", tile_guard[0])
                    else:
                        wt(s, "spe")
                    wt(s, "sdv")
                    wt(s, "snr")
                    inc("snl", s.dma_start(out=rv(NSL, INTER, BF16),
                                           in_=normed_sp[tq * 512 + tb * 128: tq * 512 + (tb + 1) * 128, :]), 16)

                for kq in range(2):
                    @block.tensor
                    def _(t, tb=tb, kq=kq):
                        wt(t, "snl")
                        wt(t, "sdv")
                        for k32 in range(32):
                            k = kq * 32 + k32
                            i = t.transpose(bankbf(k32 // 8)[:, (k % 8) * 128:(k % 8) * 128 + 128],
                                            rv(NSL, INTER, BF16)[:, k * 128:(k + 1) * 128],
                                            sb["identbf"][:, :])
                        inc("spe", i)
                    if kq == 1:
                        tile_guard[0] = n_of("spe")

                    @block.vector
                    def _(v, tb=tb, kq=kq):
                        wt(v, "spe")
                        for k32 in range(32):
                            k = kq * 32 + k32
                            i = v.tensor_scalar(
                                nT.rearrange("p (k c) -> p k c", k=64)[:, k, tb * 128:(tb + 1) * 128],
                                bankbf(k32 // 8)[:, (k % 8) * 128:(k % 8) * 128 + 128],
                                sb["normw"][:, k:k + 1], None, OP.mult)
                        inc("sdv", i)

            for ob in range(8):
                for ks in range(16):
                    buf = ks % 2
                    wsl = rv(WS3, 4 * 512, BF16, off_b=buf * 4 * 1024)

                    wsem = "swa" if buf == 0 else "swb"

                    def ldw3(g, ks=ks, ob=ob, buf=buf, wsl=wsl, wsem=wsem, tq=tq):
                        if tq == 0 and ob == 0 and ks < 2:
                            g.wait_ge(sems["swd"][0], WOUT_SWD)
                        wt(g, "spe", slab_guard[buf] if slab_guard[buf] else n_of("spe"))
                        inc(wsem, g.dma_start(out=wsl.rearrange("p (k f) -> p k f", k=4),
                                              in_=w_out_bf[ob, ks * 4:(ks + 1) * 4].rearrange("k d f -> d k f")), 16)
                    if buf == 0:
                        block.scalar(ldw3)
                    else:
                        block.sync(ldw3)

                    @block.tensor
                    def _(t, ks=ks, ob=ob, wsl=wsl, wsem=wsem):
                        wt(t, wsem)
                        if ks == 0:
                            wt(t, "sdv")
                        for tb in range(4):
                            for k4 in range(4):
                                k = ks * 4 + k4
                                i = t.matmul(bank(tb),
                                             nT.rearrange("p (k c) -> p k c", k=64)[:, k, tb * 128:(tb + 1) * 128],
                                             wsl[:, k4 * 512:(k4 + 1) * 512],
                                             start=(k == 0), stop=(k == 63))
                        inc("spe", i)
                    slab_guard[buf] = n_of("spe")
                pe_last = n_of("spe")

                for tb in range(4):
                    row0 = tq * 512 + tb * 128

                    @block.sync
                    def _(s, tb=tb, ob=ob, row0=row0):
                        wt(s, "sdv")
                        inc("sra", s.dma_start(out=rv(T3A, 512, F32),
                                               in_=x_own[row0:row0 + 128, ob * 512:(ob + 1) * 512]), 16)

                    @block.vector
                    def _(v, tb=tb, pe_last=pe_last):
                        wt(v, "spe", pe_last)
                        wt(v, "sra")
                        wt(v, "srb")
                        i = v.tensor_tensor(rv(T3B, 512, F32), bank(tb), rv(T3A, 512, F32), OP.add)
                        inc("sdv", i)

                    @block.sync
                    def _(s, tb=tb, ob=ob, row0=row0):
                        wt(s, "sdv")
                        inc("srb", s.dma_start(out=x2_sp[row0:row0 + 128, ob * 512:(ob + 1) * 512],
                                               in_=rv(T3B, 512, F32)), 16)

    LN2Q, AFL4, BFL4, LNO4, WS4, T4A, T4B = 0, 64, 72, 80, 88, 120, 123
    BBR, WGT, HWT = 126, 134, 136
    if KSTAGE >= 8:
        ln2 = rv(LN2Q, KD * HALF, BF16)

        @block.gpsimd
        def _(g):
            wt(g, "spe")
            inc("swc", g.dma_start(out=rv(AFL4, KD * 128, BF16).rearrange("p (k e) -> p k e", k=KD),
                                  in_=a_flat.rearrange("k d e -> d k e")), 16)
            wt(g, "swc")
            inc("swc", g.dma_start(out=rv(BFL4, D, BF16), in_=b_flat), 16)
            wt(g, "swc")
            inc("swc", g.dma_start(out=region[0:1, BBR * 1024:BBR * 1024 + D * 2].bitcast(BF16),
                                  in_=b_base.rearrange("(o e) -> o e", o=1)), 16)
            wt(g, "swc")
            inc("swc", g.dma_start(out=rv(WGT, KD * 16, BF16).rearrange("p (k e) -> p k e", k=KD),
                                  in_=wg_thr.rearrange("k d e -> d k e")), 16)
            wt(g, "swc")

        slab4 = [0, 0]
        for tbs in range(2):
            lnog = [0]
            bankg = [0]
            xtg = {}
            lnob = rv(LNO4, D, BF16)
            for tb8 in range(8):
                tb = tbs * 8 + tb8
                buf = tb8 % 2
                xt = rv(T4A, 0, F32) if False else rv(NSL, 2 * D, F32, off_b=buf * D * 4)
                xt = rv(WS4, D, F32, off_b=buf * D * 4)

                st = sb["stats"][:, (tb8 % 2) * 8:(tb8 % 2) * 8 + 8]

                @block.sync
                def _(s, tb=tb, xt=xt, tb8=tb8):
                    if tb8 - 2 in xtg:
                        wt(s, "sac", xtg[tb8 - 2])
                    if tb8 < 2:
                        wt(s, "spe")
                        wt(s, "sdv")
                        wt(s, "sac")
                        wt(s, "srb")
                    inc("sxa" if tb8 % 2 == 0 else "sxb",
                        s.dma_start(out=xt[:, :], in_=x2_sp[tb * 128:(tb + 1) * 128, :]), 16)

                @block.vector
                def _(v, xt=xt, tb8=tb8, st=st):
                    wt(v, "sxa" if tb8 % 2 == 0 else "sxb")
                    v.reduce_sum(st[:, 0:1], xt[:, :], AX.X)
                    v.drain()
                    i = v.tensor_scalar(st[:, 1:2], st[:, 0:1], -1.0 / D, None, OP.mult)
                    inc("sdv", i)

                @block.scalar
                def _(sc, xt=xt, st=st):
                    wt(sc, "sdv")
                    for q in range(4):
                        i = sc.activation(sb["sink"][:, :], xt[:, q * 1024:(q + 1) * 1024],
                                          AF.Square, bias=st[:, 1:2],
                                          accum_out=st[:, 2 + q:3 + q])
                    inc("sac", i)

                @block.vector
                def _(v, st=st):
                    wt(v, "sac")
                    i = v.reduce_sum(st[:, 6:7], st[:, 2:6], AX.X)
                    inc("sdv", i)

                @block.scalar
                def _(sc, st=st):
                    wt(sc, "sdv")
                    sc.activation(st[:, 7:8], st[:, 6:7], AF.Ln, scale=1.0 / D, bias=EPS)
                    sc.drain()
                    inc("sac", sc.activation(st[:, 7:8], st[:, 7:8], AF.Exp, scale=-0.5))

                @block.vector
                def _(v, st=st):
                    wt(v, "sac")
                    i = v.tensor_tensor(st[:, 0:1], st[:, 1:2], st[:, 7:8], OP.mult)
                    inc("sdv", i)

                @block.scalar
                def _(sc, xt=xt, st=st):
                    wt(sc, "sdv")
                    if lnog[0]:
                        wt(sc, "spe", lnog[0])
                    i = sc.activation(lnob[:, :], xt[:, :], AF.Identity,
                                      scale=st[:, 7:8], bias=st[:, 0:1])
                    inc("sac", i)
                xtg[tb8] = n_of("sac")

                @block.tensor
                def _(t):
                    wt(t, "sac")
                    if bankg[0]:
                        wt(t, "sdv", bankg[0])
                    for k in range(KD):
                        i = t.transpose(bankbf(k // 8)[:, (k % 8) * 128:(k % 8) * 128 + 128],
                                        lnob[:, k * 128:(k + 1) * 128], sb["identbf"][:, :])
                        if k == KD - 1:
                            inc("spe", i)
                lnog[0] = n_of("spe")

                @block.vector
                def _(v, tb8=tb8):
                    wt(v, "spe")
                    for b4 in range(4):
                        i = v.tensor_copy(
                            ln2.rearrange("p (k c) -> p k c", k=KD)[:, b4 * 8:(b4 + 1) * 8,
                                                                    tb8 * 128:(tb8 + 1) * 128],
                            bankbf(b4).rearrange("p (k c) -> p k c", k=8))
                        if b4 == 3:
                            inc("sdv", i)
                bankg[0] = n_of("sdv")

                @block.tensor
                def _(t, tb8=tb8):
                    wt(t, "sdv"); wt(t, "swc")
                    for k in range(KD):
                        t.matmul(ps[:, 4 * 512:4 * 512 + 16],
                                 ln2.rearrange("p (k c) -> p k c", k=KD)[:, k, tb8 * 128:(tb8 + 1) * 128],
                                 rv(WGT, KD * 16, BF16)[:, k * 16:(k + 1) * 16],
                                 start=(k == 0), stop=(k == KD - 1))
                    for k in range(KD):
                        i = t.matmul(ps[:, 5 * 512:5 * 512 + 128],
                                     ln2.rearrange("p (k c) -> p k c", k=KD)[:, k, tb8 * 128:(tb8 + 1) * 128],
                                     rv(AFL4, KD * 128, BF16)[:, k * 128:(k + 1) * 128],
                                     start=(k == 0), stop=(k == KD - 1))
                    inc("spe", i)

                @block.scalar
                def _(sc):
                    wt(sc, "spe")
                    sc.activation(sb["ffn"][:, 32:40], ps[:, 4 * 512:4 * 512 + 8], AF.Exp)
                    inc("sac", sc.activation(sb["ffn"][:, 17:18], ps[:, 4 * 512 + 8:4 * 512 + 9], AF.Tanh,
                                             scale=0.5, bias=sb["bthr2"][:, 0:1]))

                @block.vector
                def _(v):
                    wt(v, "sac")
                    f = sb["ffn"]
                    v.tensor_scalar(f[:, 20:21], f[:, 17:18], 0.0625, 0.0625, OP.mult, OP.add)
                    v.reduce_sum(f[:, 18:19], f[:, 32:40], AX.X)
                    v.drain()
                    v.reciprocal(f[:, 19:20], f[:, 18:19])
                    v.drain()
                    v.tensor_scalar(f[:, 40:48], f[:, 32:40], f[:, 19:20], None, OP.mult)
                    v.drain()
                    v.tensor_scalar(f[:, 48:56], f[:, 40:48], f[:, 20:21], None, OP.subtract)
                    v.drain()
                    v.tensor_scalar(f[:, 48:56], f[:, 48:56], 0.0, None, OP.max)
                    v.drain()
                    v.reduce_sum(f[:, 21:22], f[:, 48:56], AX.X)
                    v.drain()
                    v.tensor_scalar(f[:, 22:23], f[:, 21:22], 0.0, None, OP.is_equal)
                    v.drain()
                    v.tensor_tensor(f[:, 21:22], f[:, 21:22], f[:, 22:23], OP.add)
                    v.drain()
                    v.reciprocal(f[:, 23:24], f[:, 21:22])
                    v.drain()
                    v.tensor_scalar(f[:, 48:56], f[:, 48:56], f[:, 23:24], 2.0, OP.mult, OP.mult)
                    v.drain()
                    i = v.tensor_tensor(rv(T4A, 128, BF16)[:, 0:128].rearrange("p (e r) -> p e r", e=8),
                                        ps[:, 5 * 512:5 * 512 + 128].rearrange("p (e r) -> p e r", e=8),
                                        f[:, 48:56].rearrange("p (e o) -> p e o", o=1).broadcast_to([128, 8, 16]), OP.mult)
                    inc("sdv", i)

                @block.tensor
                def _(t):
                    wt(t, "sdv")
                    i = t.transpose(bankbf(6)[:, 0:128], rv(T4A, 128, BF16)[:, 0:128], sb["identbf"][:, :])
                    inc("spe", i)

                @block.vector
                def _(v, tb=tb):
                    wt(v, "spe")
                    i = v.tensor_copy(rv(HWT, 16 * 128, BF16)[:, tb * 128:(tb + 1) * 128], bankbf(6)[:, 0:128])
                    inc("sdv", i)

            for ob in range(8):
                for ks in range(4):
                    buf = ks % 2
                    wsl = rv(WS4, 8 * 512, BF16, off_b=buf * 8 * 1024)

                    wsem = "swa" if buf == 0 else "swb"

                    def ldw4(g, ob=ob, ks=ks, buf=buf, wsl=wsl, wsem=wsem, tbs=tbs):
                        if tbs == 0 and ob == 0 and ks < 2:
                            g.wait_ge(sems["swd"][0], WBASE_SWD)
                        if slab4[buf]:
                            wt(g, "spe", slab4[buf])
                        if ob == 0 and ks < 2:
                            wt(g, "sac")
                            wt(g, "sdv")
                        inc(wsem, g.dma_start(out=wsl.rearrange("p (k f) -> p k f", k=8),
                                              in_=w_base_bf[ob, ks * 8:(ks + 1) * 8].rearrange("k d f -> d k f")), 16)
                    if buf == 0:
                        block.scalar(ldw4)
                    else:
                        block.sync(ldw4)

                    @block.tensor
                    def _(t, ob=ob, tbs=tbs, ks=ks, wsl=wsl, wsem=wsem):
                        wt(t, wsem)
                        if ks == 0:
                            wt(t, "sdv")
                        for tb8 in range(8):
                            for k8 in range(8):
                                k = ks * 8 + k8
                                i = t.matmul(bank(tb8),
                                             ln2.rearrange("p (k c) -> p k c", k=KD)[:, k, tb8 * 128:(tb8 + 1) * 128],
                                             wsl[:, k8 * 512:(k8 + 1) * 512],
                                             start=(k == 0), stop=False)
                        inc("spe", i)
                    slab4[buf] = n_of("spe")

                @block.tensor
                def _(t, ob=ob, tbs=tbs):
                    wt(t, "swc")
                    for tb8 in range(8):
                        tb = tbs * 8 + tb8
                        t.matmul(bank(tb8), sb["onesbf"][0:1, 0:128],
                                 region[0:1, BBR * 1024:BBR * 1024 + D * 2].bitcast(BF16)[0:1, ob * 512:(ob + 1) * 512],
                                 start=False, stop=False)
                        i = t.matmul(bank(tb8), rv(HWT, 16 * 128, BF16)[:, tb * 128:(tb + 1) * 128],
                                     rv(BFL4, D, BF16)[:, ob * 512:(ob + 1) * 512],
                                     start=False, stop=True)
                    inc("spe", i)
                mm_done = n_of("spe")

                for tb8 in range(8):
                    tb = tbs * 8 + tb8
                    row0 = tb * 128

                    @block.sync
                    def _(s, ob=ob, row0=row0):
                        wt(s, "sdv")
                        inc("sra", s.dma_start(out=rv(T4A, 512, F32),
                                               in_=x2_sp[row0:row0 + 128, ob * 512:(ob + 1) * 512]), 16)

                    @block.vector
                    def _(v, tb8=tb8, mm_done=mm_done):
                        wt(v, "spe", mm_done)
                        wt(v, "sra")
                        wt(v, "srb")
                        i = v.tensor_tensor(rv(T4B, 512, F32), bank(tb8), rv(T4A, 512, F32), OP.add)
                        inc("sdv", i)

                    @block.sync
                    def _(s, ob=ob, row0=row0):
                        wt(s, "sdv")
                        inc("srb", s.dma_start(out=y_out[row0:row0 + 128, ob * 512:(ob + 1) * 512],
                                               in_=rv(T4B, 512, F32)), 16)

    @block.sync
    def _(s):
        for nm in ["sio", "sw", "swa", "swb", "swc", "swd", "sga", "snr", "srb", "scd", "sra",
                   "sxa", "sxb", "scr", "sgt", "snl"]:
            wt(s, nm)

    @block.vector
    def _(v):
        wt(v, "srb")

    block_cm.__exit__(None, None, None)
    cm.close()
    return nc, DBG



def _stage(inputs):
    ip = np.ascontiguousarray(inputs["L0_attn_in_proj_w"])
    rows = [np.arange(INTER + CONV_DIM, PROJ)]
    for g in range(NG):
        rows.append(np.arange(2 * INTER + g * SS, 2 * INTER + (g + 1) * SS))
        rows.append(np.arange(2 * INTER + NG * SS + g * SS, 2 * INTER + NG * SS + (g + 1) * SS))
        for j in range(8):
            rows.append(np.arange(INTER + g * GRP + j * 128, INTER + g * GRP + (j + 1) * 128))
    ridx = np.concatenate(rows)
    w_sub = ip[ridx]
    w_xbcdt_s = np.ascontiguousarray(w_sub.reshape(N_BLK, 128, KD, 128).transpose(0, 2, 3, 1))
    w_gate_s = np.ascontiguousarray(ip[0:INTER].reshape(16, 512, KD, 128).transpose(0, 2, 3, 1))
    wo = np.ascontiguousarray(inputs["L0_attn_out_proj_w"])
    w_out_s = np.ascontiguousarray(wo.T.reshape(64, 128, 8, 512).transpose(2, 0, 1, 3))
    wb = np.ascontiguousarray(inputs["L0_ffn_W_base"])
    w_base_s = np.ascontiguousarray(wb.T.reshape(KD, 128, 8, 512).transpose(2, 0, 1, 3))

    cw = np.ascontiguousarray(inputs["L0_attn_conv_w"]).reshape(CONV_DIM, 4)
    cb_ = np.ascontiguousarray(inputs["L0_attn_conv_b"])
    crows = []
    for g in range(NG):
        crows.append(np.arange(INTER + g * SS, INTER + (g + 1) * SS))
        crows.append(np.arange(INTER + NG * SS + g * SS, INTER + NG * SS + (g + 1) * SS))
        for j in range(8):
            crows.append(np.arange(g * GRP + j * 128, g * GRP + (j + 1) * 128))
    cidx = np.concatenate(crows)
    conv_taps_s = np.ascontiguousarray(cw[cidx].reshape(80, 128, 4))
    conv_bias_s = np.ascontiguousarray(cb_[cidx].reshape(80, 128))

    wg = np.ascontiguousarray(inputs["L0_ffn_W_gate"])
    wthr = np.ascontiguousarray(inputs["L0_ffn_W_thr"])
    wgt = np.zeros((D, 16), np.float32)
    wgt[:, 0:8] = wg.T
    wgt[:, 8:9] = wthr.T
    wg_thr_s = np.ascontiguousarray(wgt.reshape(KD, 128, 16))

    A = np.ascontiguousarray(inputs["L0_ffn_A"])
    a_flat_s = np.ascontiguousarray(A.reshape(128, D).T.reshape(KD, 128, 128))
    Bm = np.ascontiguousarray(inputs["L0_ffn_B"])
    b_flat_s = np.ascontiguousarray(Bm.transpose(0, 2, 1).reshape(128, D))

    norm_w_s = np.ascontiguousarray(inputs["L0_attn_norm_w"]).reshape(64, 128)

    common = dict(
        w_xbcdt=w_xbcdt_s, w_gate=w_gate_s, w_out=w_out_s, w_base=w_base_s,
        conv_taps=conv_taps_s, conv_bias=conv_bias_s,
        dt_bias=np.ascontiguousarray(inputs["L0_attn_dt_bias"]),
        d_vec=np.ascontiguousarray(inputs["L0_attn_D"]),
        norm_w=norm_w_s, wg_thr=wg_thr_s,
        b_thr=np.ascontiguousarray(inputs["L0_ffn_b_thr"]),
        b_base=np.ascontiguousarray(inputs["L0_ffn_b_base"]),
        a_flat=a_flat_s, b_flat=b_flat_s,
    )

    x = np.ascontiguousarray(inputs["L0_attn_hidden_states"])
    in_maps = []
    for c in range(8):
        b, h = c // 2, c % 2
        m = dict(common)
        m["x_own"] = np.ascontiguousarray(x[b, h * TOK:(h + 1) * TOK])
        m["x_pre"] = np.ascontiguousarray(x[b, 0:TOK])
        m["x_halo"] = np.ascontiguousarray(x[b, TOK - 3:TOK]) if h == 1 else np.ascontiguousarray(x[b, 0:3])
        fl = np.zeros((128, 4), np.float32)
        fl[:, 0] = float(h)
        fl[:, 1] = float(h)
        m["flags"] = fl
        in_maps.append(m)
    return in_maps


_CACHE = {}


def kernel(**inputs):
    taps = tuple(t for t in os.environ.get("KDBG", "").split(",") if t)
    nc, DBG = build_program(taps)
    in_maps = _stage(inputs)
    res = run_bass_kernel_spmd(nc, in_maps, core_ids=list(range(8)))
    _CACHE["res"] = res
    _CACHE["DBG"] = DBG
    out = np.zeros((4, 4096, D), np.float32)
    for c in range(8):
        b, h = c // 2, c % 2
        out[b, h * TOK:(h + 1) * TOK] = res.results[c]["y_out"]
    return out


# version 05 (final)
# speedup vs baseline: 1.4146x; 1.2690x over previous
import os
import numpy as np
import ml_dtypes
import concourse.bass as bass
import concourse.mybir as mybir
from concourse.bass_utils import run_bass_kernel_spmd

F32 = mybir.dt.float32
BF16 = mybir.dt.bfloat16
U8 = mybir.dt.uint8
AF = mybir.ActivationFunctionType
OP = mybir.AluOpType
AX = mybir.AxisListType

BF = ml_dtypes.bfloat16

D = 4096
NH, HD, SS, NG, CHUNK = 128, 64, 128, 8, 128
INTER = NH * HD
CONV_DIM = INTER + 2 * NG * SS
PROJ = INTER + CONV_DIM + NH
TOK = 2048
HALF = 1024
NCH = HALF // CHUNK
GRP = INTER // NG
HPG = NH // NG
KD = D // 128
EPS = 1e-5
DT_MIN, DT_MAX = 1e-3, 100.0
N_BLK = 1 + NG * 10



def build_program(dbg_taps=()):
    nc = bass.Bass("TRN2", target_bir_lowering=False)
    AP_IN = dict(kind="ExternalInput")

    x_own = nc.dram_tensor("x_own", [TOK, D], F32, **AP_IN).ap()
    x_pre = nc.dram_tensor("x_pre", [TOK, D], F32, **AP_IN).ap()
    x_halo = nc.dram_tensor("x_halo", [3, D], F32, **AP_IN).ap()
    flags = nc.dram_tensor("flags", [128, 4], F32, **AP_IN).ap()
    w_xbcdt = nc.dram_tensor("w_xbcdt", [N_BLK, KD, 128, 128], F32, **AP_IN).ap()
    w_gate = nc.dram_tensor("w_gate", [16, KD, 128, 512], F32, **AP_IN).ap()
    w_out = nc.dram_tensor("w_out", [8, 64, 128, 512], F32, **AP_IN).ap()
    w_base = nc.dram_tensor("w_base", [8, KD, 128, 512], F32, **AP_IN).ap()
    conv_taps = nc.dram_tensor("conv_taps", [80, 128, 4], F32, **AP_IN).ap()
    conv_bias = nc.dram_tensor("conv_bias", [80, 128], F32, **AP_IN).ap()
    dt_bias = nc.dram_tensor("dt_bias", [128], F32, **AP_IN).ap()
    d_vec = nc.dram_tensor("d_vec", [128], F32, **AP_IN).ap()
    norm_w = nc.dram_tensor("norm_w", [64, 128], F32, **AP_IN).ap()
    wg_thr = nc.dram_tensor("wg_thr", [KD, 128, 16], F32, **AP_IN).ap()
    b_thr = nc.dram_tensor("b_thr", [1], F32, **AP_IN).ap()
    b_base = nc.dram_tensor("b_base", [D], F32, **AP_IN).ap()
    a_flat = nc.dram_tensor("a_flat", [KD, 128, 128], F32, **AP_IN).ap()
    b_flat = nc.dram_tensor("b_flat", [128, D], F32, **AP_IN).ap()

    y_out = nc.dram_tensor("y_out", [TOK, D], F32, kind="ExternalOutput").ap()
    dbg = nc.dram_tensor("dbg", [128, 8192], F32, kind="ExternalOutput").ap()

    w_out_bf = nc.dram_tensor("w_out_bf", [8, 64, 128, 512], BF16, kind="Internal").ap()
    w_base_bf = nc.dram_tensor("w_base_bf", [8, KD, 128, 512], BF16, kind="Internal").ap()
    gate_sp = nc.dram_tensor("gate_sp", [TOK, INTER], BF16, kind="Internal").ap()
    normed_sp = nc.dram_tensor("normed_sp", [TOK, INTER], BF16, kind="Internal").ap()
    x2_sp = nc.dram_tensor("x2_sp", [TOK, D], F32, kind="Internal").ap()
    cs_dram = nc.dram_tensor("cs_dram", [NH, HALF], F32, kind="Internal").ap()
    cdexp_dram = nc.dram_tensor("cdexp_dram", [2, NCH, NH], F32, kind="Internal").ap()
    yo_sp = nc.dram_tensor("yo_sp", [TOK, INTER], BF16, kind="Internal").ap()
    c_sp = nc.dram_tensor("c_sp", [NG, 128, TOK], BF16, kind="Internal").ap()
    ecs_sp = nc.dram_tensor("ecs_sp", [2, 128, NCH * 128], BF16, kind="Internal").ap()
    dm_dram = nc.dram_tensor("dm_dram", [16, 128], F32, kind="Internal").ap()
    cc_in = nc.dram_tensor("cc_in", [128, NG * GRP], F32, kind="Internal")
    cc_out = nc.dram_tensor("cc_out", [128, NG * GRP], F32, kind="Internal")

    ident_bf_d = nc.inline_tensor(np.eye(128, dtype=BF), name="identbf_c").ap()
    ident_f_d = nc.inline_tensor(np.eye(128, dtype=np.float32), name="identf_c").ap()
    tril_d = nc.inline_tensor(
        (np.arange(128)[None, :] >= np.arange(128)[:, None]).astype(BF), name="tril_c").ap()
    avec_d = nc.inline_tensor(-(np.arange(128, dtype=np.float32) + 1.0)[:, None], name="avec_c").ap()
    onesbf_d = nc.inline_tensor(np.ones((1, 128), dtype=BF), name="onesbf_c").ap()

    sb = {}

    def alloc(name, cols, dtype):
        sb[name] = nc.alloc_sbuf_tensor("s_" + name, [128, cols], dtype).ap()
        return sb[name]

    alloc("identbf", 128, BF16)
    alloc("identf", 128, F32)
    alloc("tril", 128, BF16)
    alloc("avec", 1, F32)
    alloc("onesbf", 128, BF16)
    alloc("flags", 4, F32)
    alloc("dtb", 1, F32)
    alloc("bthr2", 1, F32)
    alloc("drep", 128, F32)
    alloc("convw", 80 * 4, F32)
    alloc("convb", 80, F32)
    alloc("normw", 64, F32)
    alloc("carry", NG * GRP, F32)
    alloc("halo", N_BLK * 4, BF16)
    alloc("halolnT", KD * 4, BF16)
    alloc("dtc_tm", NCH * 128, BF16)
    alloc("cs_tm", NCH * 128, F32)
    alloc("dec_tm", NCH * 128, BF16)
    alloc("ecs_tm", NCH * 128, BF16)
    alloc("ddt_tm", NCH * 128, BF16)
    alloc("cdrep", NCH * 128, F32)
    alloc("cslast", NCH, F32)
    alloc("stats", 16, F32)
    alloc("cpool", 4, F32)
    alloc("tiny", 64, F32)
    alloc("ssq", 8, F32)
    alloc("ffn", 64, F32)
    alloc("sink", 1024, BF16)
    sb["bthr_raw"] = nc.alloc_sbuf_tensor("s_bthr_raw", [128, 1], F32).ap()

    avail_kb = (nc.sbuf_bytes_remaining - 512) // 1024
    assert avail_kb >= 142, avail_kb
    region = nc.alloc_sbuf_tensor("s_region", [128, avail_kb * 1024], U8).ap()

    def rv(off_kb, cols, dtype, off_b=0):
        esz = {F32: 4, BF16: 2}[dtype]
        o = off_kb * 1024 + off_b
        return region[:, o:o + cols * esz].bitcast(dtype)

    LN_XT = 0
    XTILE = 64
    LNOUT = 96
    GWSL = 64
    GATET = 80
    WBLK = 64
    XG = 80
    GBC = 96
    BTM = 100
    SCM = 102
    XBC = 104
    DIAG = 109
    CSREP = 110
    LMS = 118
    XDT = 126
    XDTD = 130
    TVEC = 134
    TVD = 138
    YO = 140
    YG = 144
    GT2 = 148
    CBF = 150
    DTT = 110
    RALL = 118

    ps = nc.alloc_psum_tensor("ps", [128, 4096], F32).ap()

    def bank(i):
        return ps[:, i * 512:(i + 1) * 512]

    def bank2(i):
        return ps[:, i * 512:(i + 2) * 512]

    def bankbf(i, cols=1024):
        return ps.bitcast(BF16)[:, i * 1024:i * 1024 + cols]

    import contextlib
    cm = contextlib.ExitStack()
    sems = {}
    for n in ["sio", "sw", "spe", "sdv", "sac", "spl",
              "swa", "swb", "swc", "swd", "scl", "scc",
              "sxa", "sxb", "scr", "scd", "sgt", "snr", "sga", "sra", "srb", "snl"]:
        sems[n] = [cm.enter_context(nc.semaphore(n)), 0]

    def inc(name, instr, k=1):
        instr.then_inc(sems[name][0], k)
        sems[name][1] += k
        return sems[name][1]

    def n_of(name):
        return sems[name][1]

    def wt(eng, name, val=None):
        v = n_of(name) if val is None else val
        if v > 0:
            eng.wait_ge(sems[name][0], v)

    block_cm = nc.Block()
    block = block_cm.__enter__()

    nc.const_aps.register(F32, EPS, sb["cpool"][:, 0:1])

    @block.vector
    def _(v):
        v.memset(sb["cpool"][:, 0:1], EPS)
        v.drain()
        v.memset(sb["cpool"][:, 2:3], 1.0)
        v.drain()

    DBG = {}
    dbg_used = [0]

    def dbg_dump(tag, ap_src, cols):
        if tag not in dbg_taps:
            return
        off = dbg_used[0]
        assert off + cols <= 8192

        @block.sync
        def _(s):
            wt(s, "sdv"); wt(s, "sac"); wt(s, "spe"); wt(s, "spl")
            inc("sio", s.dma_start(out=dbg[:, off:off + cols], in_=ap_src), 16)

        @block.vector
        def _(v):
            wt(v, "sio")
            i = v.memset(sb["cpool"][:, 3:4], 0.0)
            v.drain()
            inc("sdv", i)
        dbg_used[0] = off + cols
        DBG[tag] = (off, cols)

    @block.sync
    def _(s):
        def ser(instr):
            inc("sio", instr, 16)
            wt(s, "sio")
        ser(s.dma_start(out=sb["identbf"][:, :], in_=ident_bf_d))
        ser(s.dma_start(out=sb["identf"][:, :], in_=ident_f_d))
        ser(s.dma_start(out=sb["tril"][:, :], in_=tril_d))
        ser(s.dma_start(out=sb["avec"][:, :], in_=avec_d))
        ser(s.dma_start(out=sb["onesbf"][0:1, :], in_=onesbf_d))
        ser(s.dma_start(out=sb["flags"][:, :], in_=flags))
        ser(s.dma_start(out=sb["dtb"][:, 0:1], in_=dt_bias.rearrange("(p o) -> p o", o=1)))
        ser(s.dma_start(out=sb["drep"][:, :], in_=d_vec.rearrange("(o h) -> o h", o=1).broadcast_to([128, 128])))
        ser(s.dma_start(out=sb["normw"][:, :], in_=norm_w.rearrange("k p -> p k"), allow_slow_non_contiguous=True))
        ser(s.dma_start(out=sb["bthr_raw"][:, 0:1], in_=b_thr.rearrange("(o e) -> o e", o=1).broadcast_to([128, 1])))
        ser(s.dma_start(out=sb["convb"][:, :], in_=conv_bias.rearrange("b p -> p b"), allow_slow_non_contiguous=True))

    @block.gpsimd
    def _(g):
        inc("swc", g.dma_start(out=sb["convw"][:, :].rearrange("p (b k) -> p b k", b=80), in_=conv_taps.rearrange("b p k -> p b k")), 16)
        wt(g, "swc")

    @block.vector
    def _(v):
        wt(v, "sio")
        i = v.tensor_scalar(sb["bthr2"][:, 0:1], sb["bthr_raw"][:, 0:1], 0.5, None, OP.mult)
        inc("sdv", i)

    xt_h = rv(XTILE, D, F32)
    lnout = rv(LNOUT, D, BF16)

    @block.sync
    def _(s):
        wt(s, "sio")
        inc("sio", s.dma_start(out=xt_h[0:3, :], in_=x_halo), 16)

    @block.vector
    def _(v):
        wt(v, "sio")
        v.reduce_sum(sb["tiny"][0:3, 0:1], xt_h[0:3, :], AX.X)
        v.drain()
        v.tensor_scalar(sb["tiny"][0:3, 1:2], sb["tiny"][0:3, 0:1], -1.0 / D, None, OP.mult)
        v.drain()
        i = v.tensor_scalar(xt_h[0:3, :], xt_h[0:3, :], sb["tiny"][0:3, 1:2], None, OP.add)
        v.drain()
        inc("sdv", i)

    @block.scalar
    def _(sc):
        wt(sc, "sdv")
        sc.activation(lnout[0:3, :], xt_h[0:3, :], AF.Square, accum_out=sb["tiny"][0:3, 2:3])
        sc.drain()
        sc.activation(sb["tiny"][0:3, 3:4], sb["tiny"][0:3, 2:3], AF.Ln, scale=1.0 / D, bias=EPS)
        sc.drain()
        inc("sac", sc.activation(sb["tiny"][0:3, 4:5], sb["tiny"][0:3, 3:4], AF.Exp, scale=-0.5))

    @block.vector
    def _(v):
        wt(v, "sac")
        i = v.tensor_scalar(lnout[0:3, :], xt_h[0:3, :], sb["tiny"][0:3, 4:5], None, OP.mult)
        v.drain()
        inc("sdv", i)

    @block.vector
    def _(v):
        i = v.memset(ps[:, 0:KD * 2], 0.0)
        inc("sdv", i)

    @block.tensor
    def _(t):
        wt(t, "sdv")
        for k in range(KD):
            i = t.transpose(bankbf(0)[:, k * 4:k * 4 + 4], lnout[0:4, k * 128:(k + 1) * 128],
                            sb["identbf"][0:4, 0:4])
        inc("spe", i)

    @block.vector
    def _(v):
        wt(v, "spe")
        i = v.tensor_copy(sb["halolnT"][:, :], bankbf(0)[:, 0:KD * 4])
        inc("sdv", i)

    @block.sync
    def _(s):
        wt(s, "sio")

    KSTAGE = int(os.environ.get("KSTAGE", "99"))

    cast_chunks = ([("wout", ob, q) for ob in range(8) for q in range(4)]
                   + [("wbase", ob, q) for ob in range(8) for q in range(2)])
    WOUT_SWD = 32 * 16
    WBASE_SWD = 48 * 16
    cast_emitted = [0]

    def emit_cast_chunk(n=1):
        for _ in range(n):
            if cast_emitted[0] >= len(cast_chunks):
                return
            kind, ob, q = cast_chunks[cast_emitted[0]]
            cast_emitted[0] += 1

            @block.gpsimd
            def _(g, kind=kind, ob=ob, q=q):
                if kind == "wout":
                    inc("swd", g.dma_start(out=w_out_bf[ob, q * 16:(q + 1) * 16],
                                           in_=w_out[ob, q * 16:(q + 1) * 16]), 16)
                else:
                    inc("swd", g.dma_start(out=w_base_bf[ob, q * 16:(q + 1) * 16],
                                           in_=w_base[ob, q * 16:(q + 1) * 16]), 16)

    def flush_casts():
        emit_cast_chunk(len(cast_chunks))

    def emit_ln(src_dram, src_row0, dstv, span, dst_off):
        entry_sdv = n_of("sdv")
        entry_spe = n_of("spe")
        entry_sac = n_of("sac")
        xt_guard = {}
        lnout_guard = [0]
        bank_guard = [0]
        for t8 in range(NCH):
            buf = t8 % 2
            xt = rv(XTILE, D, F32, off_b=buf * D * 4)
            st = sb["stats"][:, buf * 8:buf * 8 + 8]
            row0 = src_row0 + t8 * 128

            @block.sync
            def _(s, t8=t8, row0=row0, xt=xt):
                if t8 - 2 in xt_guard:
                    wt(s, "sac", xt_guard[t8 - 2])
                if t8 < 2:
                    wt(s, "sdv", entry_sdv)
                    wt(s, "spe", entry_spe)
                    wt(s, "sac", entry_sac)
                inc("sxa" if t8 % 2 == 0 else "sxb", s.dma_start(out=xt[:, :], in_=src_dram[row0:row0 + 128, :]), 16)

            @block.vector
            def _(v, xt=xt, t8=t8, st=st):
                wt(v, "sxa" if t8 % 2 == 0 else "sxb")
                v.reduce_sum(st[:, 0:1], xt[:, :], AX.X)
                v.drain()
                i = v.tensor_scalar(st[:, 1:2], st[:, 0:1], -1.0 / D, None, OP.mult)
                inc("sdv", i)

            @block.scalar
            def _(sc, xt=xt, st=st):
                wt(sc, "sdv")
                for q in range(4):
                    i = sc.activation(sb["sink"][:, :], xt[:, q * 1024:(q + 1) * 1024],
                                      AF.Square, bias=st[:, 1:2],
                                      accum_out=st[:, 2 + q:3 + q])
                inc("sac", i)

            @block.vector
            def _(v, st=st):
                wt(v, "sac")
                i = v.reduce_sum(st[:, 6:7], st[:, 2:6], AX.X)
                inc("sdv", i)

            @block.scalar
            def _(sc, st=st):
                wt(sc, "sdv")
                sc.activation(st[:, 7:8], st[:, 6:7], AF.Ln, scale=1.0 / D, bias=EPS)
                sc.drain()
                inc("sac", sc.activation(st[:, 7:8], st[:, 7:8], AF.Exp, scale=-0.5))

            @block.vector
            def _(v, st=st):
                wt(v, "sac")
                i = v.tensor_tensor(st[:, 0:1], st[:, 1:2], st[:, 7:8], OP.mult)
                inc("sdv", i)

            @block.scalar
            def _(sc, xt=xt, st=st):
                wt(sc, "sdv")
                if lnout_guard[0]:
                    wt(sc, "spe", lnout_guard[0])
                i = sc.activation(lnout[:, :], xt[:, :], AF.Identity,
                                  scale=st[:, 7:8], bias=st[:, 0:1])
                inc("sac", i)
            xt_guard[t8] = n_of("sac")

            @block.tensor
            def _(t):
                wt(t, "sac")
                if bank_guard[0]:
                    wt(t, "sdv", bank_guard[0])
                for k in range(KD):
                    b4 = k // 8
                    i = t.transpose(bankbf(b4)[:, (k % 8) * 128:(k % 8) * 128 + 128],
                                    lnout[:, k * 128:(k + 1) * 128], sb["identbf"][:, :])
                    if k == KD - 1:
                        inc("spe", i)
            lnout_guard[0] = n_of("spe")

            @block.vector
            def _(v, t8=t8):
                wt(v, "spe")
                for b4 in range(4):
                    i = v.tensor_copy(
                        dstv.rearrange("p (k c) -> p k c", c=span)[:, b4 * 8:(b4 + 1) * 8,
                                                                  dst_off + t8 * 128: dst_off + t8 * 128 + 128],
                        bankbf(b4).rearrange("p (k c) -> p k c", k=8))
                    if b4 == 3:
                        inc("sdv", i)
            bank_guard[0] = n_of("sdv")

    def emit_gate(half):
        slab_guard = [0, 0]
        evac_of = {}
        spill_guard = [0]
        lnv = rv(LN_XT, KD * HALF, BF16)
        for fb in range(16):
            for kh in range(4):
                buf = (fb * 4 + kh) % 2
                wsl = rv(GWSL, 8 * 512, BF16, off_b=buf * 8 * 1024)

                wsem = "swa" if buf == 0 else "swb"

                @block.gpsimd
                def _(g, fb=fb, kh=kh, buf=buf, wsl=wsl, wsem=wsem):
                    if slab_guard[buf]:
                        wt(g, "spe", slab_guard[buf])
                    wt(g, "sdv", ipb["lnguard"])
                    inc(wsem, g.dma_start(out=wsl.rearrange("p (k f) -> p k f", k=8),
                                          in_=w_gate[fb, kh * 8:(kh + 1) * 8].rearrange("k d f -> d k f")), 16)

                @block.tensor
                def _(t, fb=fb, kh=kh, wsl=wsl, wsem=wsem):
                    wt(t, wsem)
                    if fb == 0 and kh == 0 and ipb["ssd_guard"]:
                        wt(t, "sdv", ipb["ssd_guard"])
                    for tb in range(NCH):
                        if kh == 0:
                            key = (fb - 1, tb)
                            if key in evac_of:
                                wt(t, "sac", evac_of[key])
                        for k8 in range(8):
                            k = kh * 8 + k8
                            i = t.matmul(bank(tb),
                                         lnv.rearrange("p (k c) -> p k c", k=KD)[:, k, tb * 128:tb * 128 + 128],
                                         wsl[:, k8 * 512:(k8 + 1) * 512],
                                         start=(k == 0), stop=(k == KD - 1))
                    inc("spe", i)
                slab_guard[buf] = n_of("spe")
            pe_end = n_of("spe")

            for tb in range(NCH):
                @block.scalar
                def _(sc, tb=tb, fb=fb):
                    wt(sc, "spe", pe_end)
                    if spill_guard[0]:
                        wt(sc, "sga", spill_guard[0])
                    i = sc.activation(rv(GATET, 512, BF16), bank(tb), AF.Silu)
                    evac_of[(fb, tb)] = inc("sac", i)

                @block.sync
                def _(s, tb=tb, fb=fb):
                    wt(s, "sac")
                    spill_guard[0] = inc("sga", s.dma_start(
                        out=gate_sp[half * HALF + tb * 128: half * HALF + (tb + 1) * 128,
                                    fb * 512:(fb + 1) * 512],
                        in_=rv(GATET, 512, BF16)), 16)

    def emit_dt(half):
        za = rv(DTT, HALF, F32)
        zb = rv(DTT, HALF, F32, off_b=4096)
        dtc = rv(YO, HALF, F32)
        cs = rv(TVEC, HALF, F32)

        @block.vector
        def _(v):
            wt(v, "spe")
            i = v.tensor_scalar(za[:, :], ps[:, 0:HALF], sb["dtb"][:, 0:1], None, OP.add)
            inc("sdv", i)

        @block.scalar
        def _(sc):
            wt(sc, "sdv")
            sc.activation(zb[:, :], za[:, :], AF.Abs)
            sc.drain()
            sc.activation(zb[:, :], zb[:, :], AF.Exp, scale=-1.0)
            sc.drain()
            inc("sac", sc.activation(zb[:, :], zb[:, :], AF.Ln, bias=1.0))

        @block.vector
        def _(v):
            wt(v, "sac")
            v.tensor_scalar(dtc[:, :], za[:, :], 0.0, None, OP.max)
            v.drain()
            v.tensor_tensor(dtc[:, :], dtc[:, :], zb[:, :], OP.add)
            v.drain()
            v.tensor_scalar(dtc[:, :], dtc[:, :], DT_MIN, DT_MAX, OP.max, OP.min)
            v.drain()
            v.tensor_scalar(za[:, :], dtc[:, :], sb["avec"][:, 0:1], None, OP.mult)
            v.drain()
            for c in range(NCH):
                v.tensor_tensor_scan(cs[:, c * 128:(c + 1) * 128],
                                     sb["cpool"][:, 2:3].broadcast_to([128, 128]),
                                     za[:, c * 128:(c + 1) * 128], 0.0, OP.mult, OP.add)
            v.drain()
            for c in range(NCH):
                v.tensor_scalar(zb[:, c * 128:(c + 1) * 128], cs[:, c * 128:(c + 1) * 128],
                                cs[:, c * 128 + 127:c * 128 + 128], None, OP.subtract)
            i = v.tensor_copy(sb["cslast"][:, :], cs[:, 127::128])
            inc("sdv", i)

        @block.scalar
        def _(sc):
            wt(sc, "sdv")
            sc.activation(zb[:, :], zb[:, :], AF.Exp, scale=-1.0)
            i = sc.activation(sb["tiny"][:, 0:NCH], sb["cslast"][:, :], AF.Exp)
            inc("sac", i)

        @block.sync
        def _(s, half=half):
            wt(s, "sdv"); wt(s, "sac")
            inc("scd", s.dma_start(out=cs_dram, in_=cs[:, :]), 16)
            wt(s, "scd")
            inc("scd", s.dma_start(out=cdexp_dram[half], in_=sb["tiny"][:, 0:NCH],
                                   allow_slow_non_contiguous=True), 16)
            wt(s, "scd")
            inc("scd", s.dma_start(out=sb["cdrep"][:, :].rearrange("p (c h) -> p c h", c=NCH),
                                   in_=cdexp_dram[half].rearrange("(o c) h -> o c h", o=1).broadcast_to([128, NCH, 128])), 16)

        for c in range(NCH):
            @block.tensor
            def _(t, c=c):
                wt(t, "sdv"); wt(t, "sac")
                t.transpose(bank(0)[:, 0:128], dtc[:, c * 128:(c + 1) * 128], sb["identf"][:, :])
                t.transpose(bank(1)[:, 0:128], cs[:, c * 128:(c + 1) * 128], sb["identf"][:, :])
                i = t.transpose(bank(2)[:, 0:128], zb[:, c * 128:(c + 1) * 128], sb["identf"][:, :])
                inc("spe", i)

            @block.vector
            def _(v, c=c):
                wt(v, "spe")
                v.reciprocal(rv(RALL, HALF, F32)[:, c * 128:(c + 1) * 128], bank(0)[:, 0:128])
                v.tensor_copy(sb["dtc_tm"][:, c * 128:(c + 1) * 128], bank(0)[:, 0:128])
                v.tensor_copy(sb["cs_tm"][:, c * 128:(c + 1) * 128], bank(1)[:, 0:128])
                i = v.tensor_copy(sb["dec_tm"][:, c * 128:(c + 1) * 128], bank(2)[:, 0:128])
                inc("sdv", i)

        @block.vector
        def _(v):
            v.drain()
            for c in range(NCH):
                i = v.tensor_tensor(sb["ddt_tm"][:, c * 128:(c + 1) * 128],
                                    rv(RALL, HALF, F32)[:, c * 128:(c + 1) * 128],
                                    sb["drep"][:, :], OP.mult)
            inc("sdv", i)

        @block.scalar
        def _(sc):
            wt(sc, "sdv")
            inc("sac", sc.activation(sb["ecs_tm"][:, :], sb["cs_tm"][:, :], AF.Exp))

        @block.sync
        def _(s, half=half):
            wt(s, "sac")
            inc("scd", s.dma_start(out=ecs_sp[half], in_=sb["ecs_tm"][:, :]), 16)

    ipb = {"buf": 0, "xbuf": 0, "pe_guard": [0, 0], "evac_guard": 0,
           "conv_pe": [0, 0], "silu_guard": 0, "lnguard": 0, "gate_io": 0, "bank_sac": 0,
           "ssd_guard": 0}

    def emit_inproj_block(blk, dst_kb, is_r, first_half):
        buf = ipb["buf"]; ipb["buf"] ^= 1
        wsl = rv(WBLK, KD * 128, BF16, off_b=buf * KD * 256)
        is_conv = blk != 0
        cb = blk - 1
        lnv = rv(LN_XT, KD * HALF, BF16)

        wsem = "swa" if buf == 0 else "swb"

        @block.gpsimd
        def _(g, buf=buf, wsl=wsl, blk=blk, wsem=wsem):
            if ipb["pe_guard"][buf]:
                wt(g, "spe", ipb["pe_guard"][buf])
            wt(g, "sdv", ipb["lnguard"])
            inc(wsem, g.dma_start(out=wsl.rearrange("p (k f) -> p k f", k=KD), in_=w_xbcdt[blk].rearrange("k d f -> d k f")), 16)

        @block.tensor
        def _(t, wsl=wsl, blk=blk, wsem=wsem):
            wt(t, wsem)
            if ipb["evac_guard"]:
                wt(t, "sdv", ipb["evac_guard"])
            if ipb["ssd_guard"]:
                wt(t, "sdv", ipb["ssd_guard"])
            if ipb["bank_sac"]:
                wt(t, "sac", ipb["bank_sac"])
            for tt in range(2):
                for k in range(KD):
                    i = t.matmul(bank(tt), wsl[:, k * 128:(k + 1) * 128],
                                 lnv.rearrange("p (k c) -> p k c", k=KD)[:, k, tt * 512:(tt + 1) * 512],
                                 start=(k == 0), stop=(k == KD - 1))
            if is_conv and (not is_r) and first_half:
                wt(t, "sac", ipb["silu_guard"])
                for k in range(KD):
                    i = t.matmul(bank(2)[:, 0:3], wsl[:, k * 128:(k + 1) * 128],
                                 sb["halolnT"][:, :].rearrange("p (k c) -> p k c", k=KD)[:, k, 0:3],
                                 start=(k == 0), stop=(k == KD - 1))
            inc("spe", i)
        ipb["pe_guard"][buf] = n_of("spe")

        if not is_conv:
            emit_dt(0 if first_half else 1)
            ipb["evac_guard"] = n_of("sdv")
            return

        xbuf = ipb["xbuf"]; ipb["xbuf"] ^= 1
        xb = rv(XBC, HALF + 8, BF16, off_b=xbuf * 2560)

        @block.vector
        def _(v, xb=xb, xbuf=xbuf, blk=blk, cb=cb):
            wt(v, "spe")
            if ipb["conv_pe"][xbuf]:
                wt(v, "spe", ipb["conv_pe"][xbuf])
            if is_r and first_half:
                v.memset(xb[:, 0:3], 0.0)
            elif is_r or (not first_half):
                v.tensor_copy(xb[:, 0:3], sb["halo"][:, blk * 4:blk * 4 + 3])
            else:
                v.tensor_scalar(xb[:, 0:3], bank(2)[:, 0:3], sb["flags"][:, 1:2], None, OP.mult)
            v.tensor_copy(xb[:, 3:3 + HALF], ps[:, 0:HALF])
            v.drain()
            v.tensor_copy(sb["halo"][:, blk * 4:blk * 4 + 3], xb[:, HALF:HALF + 3])
            for k in range(4):
                i = v.tensor_scalar(rv(DIAG, 512, BF16)[:, k * 128:(k + 1) * 128], sb["identbf"][:, :],
                                    sb["convw"][:, cb * 4 + k:cb * 4 + k + 1], None, OP.mult)
            inc("sdv", i)
        ipb["evac_guard"] = n_of("sdv")

        @block.tensor
        def _(t, xb=xb):
            wt(t, "sdv")
            wt(t, "sac", ipb["silu_guard"])
            for tt in range(2):
                for k in range(4):
                    i = t.matmul(bank(2 + tt), rv(DIAG, 512, BF16)[:, k * 128:(k + 1) * 128],
                                 xb[:, tt * 512 + k: tt * 512 + k + 512],
                                 start=(k == 0), stop=(k == 3))
            inc("spe", i)
        ipb["conv_pe"][xbuf] = n_of("spe")

        @block.scalar
        def _(sc, dst_kb=dst_kb, cb=cb):
            wt(sc, "spe")
            wt(sc, "scc")
            if ipb["gate_io"]:
                wt(sc, "sga", ipb["gate_io"])
            dstv = rv(dst_kb[0], HALF, BF16, off_b=dst_kb[1])
            sc.activation(dstv[:, 0:512], bank(2), AF.Silu, bias=sb["convb"][:, cb:cb + 1])
            i = sc.activation(dstv[:, 512:1024], bank(3), AF.Silu, bias=sb["convb"][:, cb:cb + 1])
            inc("sac", i)
        ipb["silu_guard"] = n_of("sac")

    ssd_prev = {"exp": 0, "ydiag": 0, "ygm": 0, "snr": 0, "tvd": 0,
                "states": 0, "yoff_free": 0}

    def emit_group_ssd(g, half, is_r):
        gbc = rv(GBC, 2 * HALF, BF16)
        xg = rv(XG, 8 * HALF, BF16)
        btm = rv(BTM, NCH * 128, BF16)
        scm = rv(SCM, NCH * 128, BF16)
        csrep = rv(CSREP, 2048, F32)
        tvs = rv(TVEC, GRP, F32)
        tvd = rv(TVD, GRP, BF16)
        yo = rv(YO, GRP, F32)
        gt2 = rv(GT2, GRP, BF16)
        cbf = rv(CBF, GRP, BF16)
        cslice = sb["carry"][:, g * GRP:(g + 1) * GRP]

        def lmsb(c):
            return rv(LMS, 2048, BF16, off_b=(c % 2) * 4096)

        def xdtb(c):
            return rv(XDT, GRP, BF16, off_b=(c % 2) * 2048)

        def xdtdb(c):
            return rv(XDTD, GRP, BF16, off_b=(c % 2) * 2048)

        def ygb(c):
            return rv(YG, GRP, BF16, off_b=(c % 2) * 2048)

        def hsl(c):
            return sb["dtc_tm"][:, c * 128 + g * HPG: c * 128 + g * HPG + HPG]

        entry_sdv = n_of("sdv")

        for cq in range(2):
            @block.tensor
            def _(t, cq=cq):
                wt(t, "sac", ipb["silu_guard"])
                wt(t, "sdv")
                for c in range(cq * 4, cq * 4 + 4):
                    i = t.transpose(bankbf(4)[:, (c % 4) * 128:(c % 4) * 128 + 128],
                                    gbc[:, c * 128:(c + 1) * 128], sb["identbf"][:, :])
                    if not is_r:
                        i = t.matmul(bank(5)[:, (c % 4) * 128:(c % 4) * 128 + 128],
                                     gbc[:, c * 128:(c + 1) * 128],
                                     gbc[:, HALF + c * 128: HALF + (c + 1) * 128],
                                     start=True, stop=True)
                inc("spe", i)

            @block.vector
            def _(v, cq=cq):
                wt(v, "spe")
                for c in range(cq * 4, cq * 4 + 4):
                    i = v.tensor_copy(btm[:, c * 128:(c + 1) * 128],
                                      bankbf(4)[:, (c % 4) * 128:(c % 4) * 128 + 128])
                if not is_r:
                    for c in range(cq * 4, cq * 4 + 4):
                        i = v.scalar_tensor_tensor(scm[:, c * 128:(c + 1) * 128],
                                                   bank(5)[:, (c % 4) * 128:(c % 4) * 128 + 128],
                                                   1.0, sb["tril"][:, :], OP.mult, OP.mult)
                inc("sdv", i)

        cbf0 = 0
        if not is_r:
            @block.sync
            def _(s):
                wt(s, "sac", ipb["silu_guard"])
                inc("scc", s.dma_start(out=c_sp[g, :, half * HALF:(half + 1) * HALF],
                                       in_=gbc[:, HALF:2 * HALF]), 16)

            @block.vector
            def _(v):
                wt(v, "spl"); wt(v, "sdv")
                i = v.tensor_copy(cbf[:, :], cslice)
                inc("sdv", i)
            cbf0 = n_of("sdv")
        head_sdv = n_of("sdv")

        R = {k: {} for k in ["s1", "xdt", "yoff", "states", "cbf", "s7", "sub",
                             "exp", "lms", "ydiag", "s12", "yofree", "ygm",
                             "tvd", "norm", "nrm", "snr"]}

        def emit_tail(cc):
            @block.sync
            def _(s, cc=cc):
                wt(s, "sdv", R["ygm"][cc])
                R["snr"][cc] = inc("snr", s.dma_start(
                    out=yo_sp[half * HALF + cc * 128: half * HALF + (cc + 1) * 128,
                              g * GRP:(g + 1) * GRP],
                    in_=ygb(cc)[:, :]), 16)

        for c in range(NCH):
            @block.tensor
            def _(t, c=c):
                wt(t, "sdv", R["xdt"].get(c - 1, head_sdv))
                for j in range(8):
                    i = t.transpose(bankbf(4)[:, j * 128:(j + 1) * 128],
                                    xg[:, j * HALF + c * 128:j * HALF + (c + 1) * 128],
                                    sb["identbf"][:, :])
                R["s1"][c] = inc("spe", i)

            @block.vector
            def _(v, c=c):
                wt(v, "spe", R["s1"][c])
                wt(v, "spe", R["ydiag" if not is_r else "states"].get(
                    c - 2, ssd_prev["ydiag" if not is_r else "states"]))
                if not is_r:
                    wt(v, "spl", R["tvd"].get(c - 2, ssd_prev["tvd"]))
                v.tensor_tensor(xdtb(c).rearrange("p (h r) -> p h r", h=HPG),
                                bankbf(4).rearrange("p (h r) -> p h r", h=HPG),
                                hsl(c).rearrange("p (h o) -> p h o", o=1)
                                .broadcast_to([128, HPG, 64]), OP.mult)
                v.drain()
                i = v.tensor_tensor(xdtdb(c).rearrange("p (h r) -> p h r", h=HPG),
                                    xdtb(c).rearrange("p (h r) -> p h r", h=HPG),
                                    sb["dec_tm"][:, c * 128 + g * HPG: c * 128 + g * HPG + HPG]
                                    .rearrange("p (h o) -> p h o", o=1)
                                    .broadcast_to([128, HPG, 64]), OP.mult)
                R["xdt"][c] = inc("sdv", i)

            @block.tensor
            def _(t, c=c):
                wt(t, "sdv", R["xdt"][c])
                wt(t, "sdv", R["cbf"].get(c - 1, cbf0))
                if not is_r:
                    wt(t, "sdv", R["yofree"].get(c - 1, ssd_prev["yoff_free"]))
                    for hh in range(2):
                        i = t.matmul(bank(hh), gbc[:, HALF + c * 128:HALF + (c + 1) * 128],
                                     cbf[:, hh * 512:(hh + 1) * 512],
                                     start=True, stop=True)
                    R["yoff"][c] = inc("spe", i)
                for hh in range(2):
                    i = t.matmul(bank(2 + hh), btm[:, c * 128:(c + 1) * 128],
                                 xdtdb(c)[:, hh * 512:(hh + 1) * 512],
                                 start=True, stop=True)
                R["states"][c] = inc("spe", i)

            @block.gpsimd
            def _(gp, c=c):
                wt(gp, "spe", R["states"][c])
                if c == 0:
                    wt(gp, "scd")
                i = gp.tensor_tensor(tvs.rearrange("p (h r) -> p h r", h=HPG),
                                     cslice.rearrange("p (h r) -> p h r", h=HPG),
                                     sb["cdrep"][:, c * 128 + g * HPG: c * 128 + g * HPG + HPG]
                                     .rearrange("p (h o) -> p h o", o=1)
                                     .broadcast_to([128, HPG, 64]), OP.mult)
                R["s5"] = inc("spl", i)

            if not is_r:
                @block.gpsimd
                def _(gp, c=c):
                    wt(gp, "sdv", R["xdt"][c])
                    wt(gp, "sdv", R["ygm"].get(c - 1, ssd_prev["ygm"]))
                    i = gp.tensor_tensor(tvd.rearrange("p (h r) -> p h r", h=HPG),
                                         xdtb(c).rearrange("p (h r) -> p h r", h=HPG),
                                         sb["ddt_tm"][:, c * 128 + g * HPG: c * 128 + g * HPG + HPG]
                                         .rearrange("p (h o) -> p h o", o=1)
                                         .broadcast_to([128, HPG, 64]), OP.mult)
                    R["tvd"][c] = inc("spl", i)

            @block.vector
            def _(v, c=c):
                wt(v, "spl", R["s5"])
                v.tensor_tensor(cslice, tvs[:, :], bank2(2), OP.add)
                v.drain()
                if not is_r:
                    i = v.tensor_copy(cbf[:, :], cslice)
                else:
                    i = v.memset(sb["cpool"][:, 3:4], 0.0)
                R["cbf"][c] = inc("sdv", i)

            if is_r:
                continue

            @block.sync
            def _(s, c=c):
                wt(s, "sac", R["exp"].get(c - 1, ssd_prev["exp"]))
                if c == 0:
                    wt(s, "sdv", head_sdv)
                R["s7"][c] = inc("scr", s.dma_start(
                    out=csrep.rearrange("p (h l) -> p h l", h=HPG),
                    in_=cs_dram[g * HPG:(g + 1) * HPG, c * 128:(c + 1) * 128]
                    .rearrange("(o h) l -> o h l", o=1).broadcast_to([128, HPG, 128])), 16)

            @block.vector
            def _(v, c=c):
                wt(v, "scr", R["s7"][c])
                i = v.tensor_tensor(csrep.rearrange("p (h l) -> p h l", h=HPG),
                                    csrep.rearrange("p (h l) -> p h l", h=HPG),
                                    sb["cs_tm"][:, c * 128 + g * HPG: c * 128 + g * HPG + HPG]
                                    .rearrange("p (h o) -> p h o", o=1)
                                    .broadcast_to([128, HPG, 128]), OP.subtract)
                R["sub"][c] = inc("sdv", i)

            @block.scalar
            def _(sc, c=c):
                wt(sc, "sdv", R["sub"][c])
                wt(sc, "spe", R["ydiag"].get(c - 2, ssd_prev["ydiag"]))
                i = sc.activation(lmsb(c)[:, :], csrep[:, :], AF.Exp)
                R["exp"][c] = inc("sac", i)

            @block.vector
            def _(v, c=c):
                wt(v, "sac", R["exp"][c])
                i = v.scalar_tensor_tensor(lmsb(c).rearrange("p (h l) -> p h l", h=HPG),
                                           lmsb(c).rearrange("p (h l) -> p h l", h=HPG), 1.0,
                                           scm[:, c * 128:(c + 1) * 128]
                                           .unsqueeze(1).broadcast_to([128, HPG, 128]),
                                           OP.min, OP.mult)
                R["lms"][c] = inc("sdv", i)

            @block.tensor
            def _(t, c=c):
                wt(t, "sdv", R["lms"][c])
                wt(t, "sdv", R["ygm"].get(c - 1, ssd_prev["ygm"]))
                for h in range(HPG):
                    i = t.matmul(ps[:, 6 * 512 + h * 64: 6 * 512 + (h + 1) * 64],
                                 lmsb(c)[:, h * 128:(h + 1) * 128],
                                 xdtb(c)[:, h * 64:(h + 1) * 64], start=True, stop=True)
                R["ydiag"][c] = inc("spe", i)

            @block.vector
            def _(v, c=c):
                wt(v, "spe", R["yoff"][c])
                wt(v, "sdv", R["ygm"].get(c - 1, ssd_prev["ygm"]))
                i = v.tensor_tensor(yo.rearrange("p (h r) -> p h r", h=HPG),
                                    bank2(0).rearrange("p (h r) -> p h r", h=HPG),
                                    sb["ecs_tm"][:, c * 128 + g * HPG: c * 128 + g * HPG + HPG]
                                    .rearrange("p (h o) -> p h o", o=1)
                                    .broadcast_to([128, HPG, 64]), OP.mult)
                R["yofree"][c] = inc("sdv", i)

            @block.vector
            def _(v, c=c):
                wt(v, "spe", R["ydiag"][c])
                wt(v, "spl", R["tvd"][c])
                wt(v, "snr", R["snr"].get(c - 2, ssd_prev["snr"]))
                v.tensor_tensor(yo[:, :], yo[:, :], ps[:, 6 * 512:8 * 512], OP.add)
                v.drain()
                v.tensor_tensor(yo[:, :], yo[:, :], tvd[:, :], OP.add)
                v.drain()
                i = v.tensor_copy(ygb(c)[:, :], yo[:, :])
                R["ygm"][c] = inc("sdv", i)

            if c >= 1:
                emit_tail(c - 1)

        if not is_r:
            emit_tail(NCH - 1)
            ssd_prev["exp"] = R["exp"][NCH - 1]
            ssd_prev["ydiag"] = R["ydiag"][NCH - 1]
            ssd_prev["ygm"] = R["ygm"][NCH - 1]
            ssd_prev["snr"] = R["snr"][NCH - 1]
            ssd_prev["tvd"] = R["tvd"][NCH - 1]
            ssd_prev["yoff_free"] = R["yofree"][NCH - 1]
        ssd_prev["states"] = R["states"][NCH - 1]
        if not is_r:
            ipb["ssd_guard"] = max(R["yofree"][NCH - 1], R["cbf"][NCH - 1],
                                   R["xdt"][NCH - 1], R["ygm"][NCH - 1])
        else:
            ipb["ssd_guard"] = max(R["cbf"][NCH - 1], R["xdt"][NCH - 1])

    def do_pass(is_r, half):
        src = x_pre if is_r else x_own
        emit_ln(src, half * HALF, rv(LN_XT, KD * HALF, BF16), HALF, 0)
        ipb["lnguard"] = n_of("sdv")
        ipb["gate_io"] = 0
        if (not is_r) and KSTAGE >= 3:
            emit_gate(half)
            ipb["gate_io"] = n_of("sga")
            ipb["pe_guard"] = [n_of("spe"), n_of("spe")]
            ipb["bank_sac"] = n_of("sac")
        emit_inproj_block(0, None, is_r, half == 0)
        if KSTAGE < 2:
            return
        for g in range(NG):
            base = 1 + g * 10
            emit_inproj_block(base, (GBC, 0), is_r, half == 0)
            if not is_r:
                emit_inproj_block(base + 1, (GBC, HALF * 2), is_r, half == 0)
            for j in range(8):
                emit_inproj_block(base + 2 + j, (XG, j * HALF * 2), is_r, half == 0)
            if KSTAGE >= 4:
                emit_group_ssd(g, half, is_r)
            emit_cast_chunk(3)

    @block.vector
    def _(v):
        wt(v, "sio")
        i = v.memset(sb["carry"][:, :], 0.0)
        inc("sdv", i)

    do_pass(False, 0)
    do_pass(False, 1)

    s0f = rv(0, NG * GRP, F32)
    s0bf = rv(32, NG * GRP, BF16)
    dmrep = rv(48, 16 * 128, F32)
    dmx = rv(56, 64, F32)

    @block.vector
    def _(v):
        wt(v, "spe"); wt(v, "spl")
        i = v.tensor_scalar(s0f[:, :], sb["carry"][:, :], sb["flags"][:, 2:3], None, OP.mult)
        inc("sdv", i)

    @block.sync
    def _(s):
        wt(s, "sdv")
        inc("scd", s.dma_start(out=cc_in.ap(), in_=s0f[:, :]), 16)

    @block.gpsimd
    def _(g):
        wt(g, "scd")
        i = g.collective_compute(
            "AllReduce", OP.add,
            ins=[cc_in.ap().opt()], outs=[cc_out.ap().opt()],
            replica_groups=[[0, 1], [2, 3], [4, 5], [6, 7]])
        inc("scl", i)
        wt(g, "scl")
        inc("swc", g.dma_start(out=s0f[:, :], in_=cc_out.ap()), 16)

    @block.vector
    def _(v):
        wt(v, "swc")
        i = v.tensor_scalar(s0bf[:, :], s0f[:, :], sb["flags"][:, 0:1], None, OP.mult)
        inc("sdv", i)
    s0_rdy = n_of("sdv")

    @block.sync
    def _(s):
        wt(s, "scd")
        inc("scr", s.dma_start(out=dmx[:, 0:16],
                               in_=cdexp_dram.rearrange("a c h -> h (a c)"),
                               allow_slow_non_contiguous=True), 16)

    @block.vector
    def _(v):
        wt(v, "scr")
        v.memset(dmx[:, 16:31], 0.0)
        v.memset(dmx[:, 48:49], 1.0)
        v.drain()
        v.tensor_tensor_scan(dmx[:, 49:64], dmx[:, 0:15],
                             dmx[:, 16:31], 1.0, OP.mult, OP.add)
        v.drain()
        i = v.memset(sb["cpool"][:, 3:4], 0.0)
        inc("sdv", i)

    @block.tensor
    def _(t):
        wt(t, "sdv")
        i = t.transpose(bank(6)[0:16, 0:128], dmx[:, 48:64], sb["identf"][:, :])
        inc("spe", i)
    dmt_spe = n_of("spe")

    @block.vector
    def _(v):
        wt(v, "spe", dmt_spe)
        i = v.tensor_copy(rv(57, 128, F32)[0:16, :], bank(6)[0:16, 0:128])
        inc("sdv", i)
    dmt_sdv = n_of("sdv")

    @block.sync
    def _(s):
        wt(s, "sdv", dmt_sdv)
        inc("scr", s.dma_start(out=dm_dram, in_=rv(57, 128, F32)[0:16, :]), 16)
        wt(s, "scr")
        inc("scr", s.dma_start(out=dmrep.rearrange("p (c h) -> p c h", c=16),
                               in_=dm_dram.rearrange("(o c) h -> o c h", o=1)
                               .broadcast_to([128, 16, 128])), 16)
    dm_rdy = n_of("scr")

    CYO, CGT, CCT, CT1, CYG, CEC = 60, 64, 68, 70, 78, 82

    def yob(i_):
        return rv(CYO, GRP, BF16, off_b=(i_ % 2) * 2048)

    def gtb(i_):
        return rv(CGT, GRP, BF16, off_b=(i_ % 2) * 2048)

    def ctb(i_):
        return rv(CCT, 128, BF16, off_b=(i_ % 2) * 1024)

    def t1b(i_):
        return rv(CT1, GRP, F32, off_b=(i_ % 2) * 4096)

    def ygc(i_):
        return rv(CYG, GRP, BF16, off_b=(i_ % 2) * 2048)

    CR = {k: {} for k in ["yo", "gt", "ct", "mm", "t1", "gp", "nor", "nrm", "out"]}
    cmeta = {}

    def emit_ctail(j):
        row0, gq = cmeta[j]

        @block.scalar
        def _(sc, j=j):
            wt(sc, "spl", CR["gp"][j])
            q = (j % 2) * 3
            sc.activation(sb["sink"][:, :], ygc(j)[:, :], AF.Square,
                          accum_out=sb["ssq"][:, q:q + 1])
            sc.drain()
            sc.activation(sb["ssq"][:, q + 1:q + 2], sb["ssq"][:, q:q + 1],
                          AF.Ln, scale=1.0 / GRP, bias=EPS)
            sc.drain()
            i = sc.activation(sb["ssq"][:, q + 2:q + 3], sb["ssq"][:, q + 1:q + 2],
                              AF.Exp, scale=-0.5)
            CR["nor"][j] = inc("sac", i)

        @block.vector
        def _(v, j=j):
            wt(v, "sac", CR["nor"][j])
            i = v.tensor_scalar(ygc(j)[:, :], ygc(j)[:, :],
                                sb["ssq"][:, (j % 2) * 3 + 2:(j % 2) * 3 + 3],
                                None, OP.mult)
            CR["nrm"][j] = inc("sdv", i)

        @block.sync
        def _(s, j=j, row0=row0, gq=gq):
            wt(s, "sdv", CR["nrm"][j])
            CR["out"][j] = inc("snr", s.dma_start(
                out=normed_sp[row0:row0 + 128, gq * GRP:(gq + 1) * GRP],
                in_=ygc(j)[:, :]), 16)

    it = 0
    for chalf in range(2):
        ecsd = rv(CEC, NCH * 128, BF16, off_b=chalf * 2048)

        @block.sync
        def _(s, chalf=chalf):
            wt(s, "scd")
            inc("scr", s.dma_start(out=rv(CEC, NCH * 128, BF16, off_b=chalf * 2048)[:, :],
                                   in_=ecs_sp[chalf]), 16)

        @block.vector
        def _(v, chalf=chalf, ecsd=ecsd):
            wt(v, "scr")
            wt(v, "scr", dm_rdy)
            i = v.tensor_tensor(ecsd.rearrange("p (c h) -> p c h", c=NCH),
                                ecsd.rearrange("p (c h) -> p c h", c=NCH),
                                dmrep.rearrange("p (c h) -> p c h", c=16)
                                [:, chalf * NCH:(chalf + 1) * NCH, :], OP.mult)
            inc("sdv", i)
        ecs_rdy = n_of("sdv")

        for gq in range(NG):
            for cc in range(NCH):
                row0 = chalf * HALF + cc * 128
                pb = (it % 2) * 2
                cmeta[it] = (row0, gq)

                @block.sync
                def _(s, it=it, row0=row0, gq=gq):
                    wt(s, "spl", CR["gp"].get(it - 2, 0))
                    R0 = s.dma_start(out=yob(it)[:, :],
                                     in_=yo_sp[row0:row0 + 128, gq * GRP:(gq + 1) * GRP])
                    CR["yo"][it] = inc("sra", R0, 16)
                    i = s.dma_start(out=gtb(it)[:, :],
                                    in_=gate_sp[row0:row0 + 128, gq * GRP:(gq + 1) * GRP])
                    CR["gt"][it] = inc("sgt", i, 16)

                @block.scalar
                def _(sc, it=it, row0=row0, gq=gq):
                    wt(sc, "spe", CR["mm"].get(it - 2, 0))
                    i = sc.dma_start(out=ctb(it)[:, :],
                                     in_=c_sp[gq, :, row0:row0 + 128])
                    CR["ct"][it] = inc("snl", i, 16)

                @block.tensor
                def _(t, it=it, pb=pb, gq=gq):
                    wt(t, "snl", CR["ct"][it])
                    wt(t, "sdv", s0_rdy)
                    wt(t, "sdv", CR["t1"].get(it - 2, 0))
                    for hh in range(2):
                        i = t.matmul(bank(pb + hh), ctb(it)[:, :],
                                     s0bf[:, gq * GRP + hh * 512: gq * GRP + (hh + 1) * 512],
                                     start=True, stop=True)
                    CR["mm"][it] = inc("spe", i)

                @block.vector
                def _(v, it=it, pb=pb, cc=cc, gq=gq, ecsd=ecsd, ecs_rdy=ecs_rdy):
                    wt(v, "spe", CR["mm"][it])
                    wt(v, "sdv", ecs_rdy)
                    wt(v, "spl", CR["gp"].get(it - 2, 0))
                    i = v.tensor_tensor(t1b(it).rearrange("p (h r) -> p h r", h=HPG),
                                        bank2(pb).rearrange("p (h r) -> p h r", h=HPG),
                                        ecsd[:, cc * 128 + gq * HPG: cc * 128 + gq * HPG + HPG]
                                        .rearrange("p (h o) -> p h o", o=1)
                                        .broadcast_to([128, HPG, 64]), OP.mult)
                    CR["t1"][it] = inc("sdv", i)

                @block.gpsimd
                def _(gp, it=it):
                    wt(gp, "sdv", CR["t1"][it])
                    wt(gp, "sra", CR["yo"][it])
                    wt(gp, "sgt", CR["gt"][it])
                    wt(gp, "snr", CR["out"].get(it - 2, 0))
                    gp.tensor_tensor(t1b(it)[:, :], t1b(it)[:, :], yob(it)[:, :], OP.add)
                    gp.drain()
                    i = gp.tensor_tensor(ygc(it)[:, :], t1b(it)[:, :], gtb(it)[:, :], OP.mult)
                    CR["gp"][it] = inc("spl", i)

                if it > 0:
                    emit_ctail(it - 1)
                it += 1
    emit_ctail(it - 1)

    flush_casts()
    NTQ, NSL, WS3, T3A, T3B = 0, 64, 80, 88, 91
    if KSTAGE >= 7:
        slab_guard = [0, 0]
        for tq in range(4):
            nT = rv(NTQ, 64 * 512, BF16)
            tile_guard = [0]
            for tb in range(4):
                @block.sync
                def _(s, tq=tq, tb=tb):
                    if tile_guard[0]:
                        wt(s, "spe", tile_guard[0])
                    else:
                        wt(s, "spe")
                    wt(s, "sdv")
                    wt(s, "snr")
                    inc("snl", s.dma_start(out=rv(NSL, INTER, BF16),
                                           in_=normed_sp[tq * 512 + tb * 128: tq * 512 + (tb + 1) * 128, :]), 16)

                for kq in range(2):
                    @block.tensor
                    def _(t, tb=tb, kq=kq):
                        wt(t, "snl")
                        wt(t, "sdv")
                        for k32 in range(32):
                            k = kq * 32 + k32
                            i = t.transpose(bankbf(k32 // 8)[:, (k % 8) * 128:(k % 8) * 128 + 128],
                                            rv(NSL, INTER, BF16)[:, k * 128:(k + 1) * 128],
                                            sb["identbf"][:, :])
                        inc("spe", i)
                    if kq == 1:
                        tile_guard[0] = n_of("spe")

                    @block.vector
                    def _(v, tb=tb, kq=kq):
                        wt(v, "spe")
                        for k32 in range(32):
                            k = kq * 32 + k32
                            i = v.tensor_scalar(
                                nT.rearrange("p (k c) -> p k c", k=64)[:, k, tb * 128:(tb + 1) * 128],
                                bankbf(k32 // 8)[:, (k % 8) * 128:(k % 8) * 128 + 128],
                                sb["normw"][:, k:k + 1], None, OP.mult)
                        inc("sdv", i)

            for ob in range(8):
                for ks in range(16):
                    buf = ks % 2
                    wsl = rv(WS3, 4 * 512, BF16, off_b=buf * 4 * 1024)

                    wsem = "swa" if buf == 0 else "swb"

                    def ldw3(g, ks=ks, ob=ob, buf=buf, wsl=wsl, wsem=wsem, tq=tq):
                        if tq == 0 and ob == 0 and ks < 2:
                            g.wait_ge(sems["swd"][0], WOUT_SWD)
                            wt(g, "snr")
                        wt(g, "spe", slab_guard[buf] if slab_guard[buf] else n_of("spe"))
                        inc(wsem, g.dma_start(out=wsl.rearrange("p (k f) -> p k f", k=4),
                                              in_=w_out_bf[ob, ks * 4:(ks + 1) * 4].rearrange("k d f -> d k f")), 16)
                    if buf == 0:
                        block.scalar(ldw3)
                    else:
                        block.sync(ldw3)

                    @block.tensor
                    def _(t, ks=ks, ob=ob, wsl=wsl, wsem=wsem):
                        wt(t, wsem)
                        if ks == 0:
                            wt(t, "sdv")
                        for tb in range(4):
                            for k4 in range(4):
                                k = ks * 4 + k4
                                i = t.matmul(bank(tb),
                                             nT.rearrange("p (k c) -> p k c", k=64)[:, k, tb * 128:(tb + 1) * 128],
                                             wsl[:, k4 * 512:(k4 + 1) * 512],
                                             start=(k == 0), stop=(k == 63))
                        inc("spe", i)
                    slab_guard[buf] = n_of("spe")
                pe_last = n_of("spe")

                for tb in range(4):
                    row0 = tq * 512 + tb * 128

                    @block.sync
                    def _(s, tb=tb, ob=ob, row0=row0):
                        wt(s, "sdv")
                        inc("sra", s.dma_start(out=rv(T3A, 512, F32),
                                               in_=x_own[row0:row0 + 128, ob * 512:(ob + 1) * 512]), 16)

                    @block.vector
                    def _(v, tb=tb, pe_last=pe_last):
                        wt(v, "spe", pe_last)
                        wt(v, "sra")
                        wt(v, "srb")
                        i = v.tensor_tensor(rv(T3B, 512, F32), bank(tb), rv(T3A, 512, F32), OP.add)
                        inc("sdv", i)

                    @block.sync
                    def _(s, tb=tb, ob=ob, row0=row0):
                        wt(s, "sdv")
                        inc("srb", s.dma_start(out=x2_sp[row0:row0 + 128, ob * 512:(ob + 1) * 512],
                                               in_=rv(T3B, 512, F32)), 16)

    LN2Q, AFL4, BFL4, LNO4, WS4, T4A, T4B = 0, 64, 72, 80, 88, 120, 123
    BBR, WGT, HWT = 126, 134, 136
    if KSTAGE >= 8:
        ln2 = rv(LN2Q, KD * HALF, BF16)

        @block.gpsimd
        def _(g):
            wt(g, "spe")
            inc("swc", g.dma_start(out=rv(AFL4, KD * 128, BF16).rearrange("p (k e) -> p k e", k=KD),
                                  in_=a_flat.rearrange("k d e -> d k e")), 16)
            wt(g, "swc")
            inc("swc", g.dma_start(out=rv(BFL4, D, BF16), in_=b_flat), 16)
            wt(g, "swc")
            inc("swc", g.dma_start(out=region[0:1, BBR * 1024:BBR * 1024 + D * 2].bitcast(BF16),
                                  in_=b_base.rearrange("(o e) -> o e", o=1)), 16)
            wt(g, "swc")
            inc("swc", g.dma_start(out=rv(WGT, KD * 16, BF16).rearrange("p (k e) -> p k e", k=KD),
                                  in_=wg_thr.rearrange("k d e -> d k e")), 16)
            wt(g, "swc")

        slab4 = [0, 0]
        for tbs in range(2):
            lnog = [0]
            bankg = [0]
            xtg = {}
            lnob = rv(LNO4, D, BF16)
            for tb8 in range(8):
                tb = tbs * 8 + tb8
                buf = tb8 % 2
                xt = rv(T4A, 0, F32) if False else rv(NSL, 2 * D, F32, off_b=buf * D * 4)
                xt = rv(WS4, D, F32, off_b=buf * D * 4)

                st = sb["stats"][:, (tb8 % 2) * 8:(tb8 % 2) * 8 + 8]

                @block.sync
                def _(s, tb=tb, xt=xt, tb8=tb8):
                    if tb8 - 2 in xtg:
                        wt(s, "sac", xtg[tb8 - 2])
                    if tb8 < 2:
                        wt(s, "spe")
                        wt(s, "sdv")
                        wt(s, "sac")
                        wt(s, "srb")
                    inc("sxa" if tb8 % 2 == 0 else "sxb",
                        s.dma_start(out=xt[:, :], in_=x2_sp[tb * 128:(tb + 1) * 128, :]), 16)

                @block.vector
                def _(v, xt=xt, tb8=tb8, st=st):
                    wt(v, "sxa" if tb8 % 2 == 0 else "sxb")
                    v.reduce_sum(st[:, 0:1], xt[:, :], AX.X)
                    v.drain()
                    i = v.tensor_scalar(st[:, 1:2], st[:, 0:1], -1.0 / D, None, OP.mult)
                    inc("sdv", i)

                @block.scalar
                def _(sc, xt=xt, st=st):
                    wt(sc, "sdv")
                    for q in range(4):
                        i = sc.activation(sb["sink"][:, :], xt[:, q * 1024:(q + 1) * 1024],
                                          AF.Square, bias=st[:, 1:2],
                                          accum_out=st[:, 2 + q:3 + q])
                    inc("sac", i)

                @block.vector
                def _(v, st=st):
                    wt(v, "sac")
                    i = v.reduce_sum(st[:, 6:7], st[:, 2:6], AX.X)
                    inc("sdv", i)

                @block.scalar
                def _(sc, st=st):
                    wt(sc, "sdv")
                    sc.activation(st[:, 7:8], st[:, 6:7], AF.Ln, scale=1.0 / D, bias=EPS)
                    sc.drain()
                    inc("sac", sc.activation(st[:, 7:8], st[:, 7:8], AF.Exp, scale=-0.5))

                @block.vector
                def _(v, st=st):
                    wt(v, "sac")
                    i = v.tensor_tensor(st[:, 0:1], st[:, 1:2], st[:, 7:8], OP.mult)
                    inc("sdv", i)

                @block.scalar
                def _(sc, xt=xt, st=st):
                    wt(sc, "sdv")
                    if lnog[0]:
                        wt(sc, "spe", lnog[0])
                    i = sc.activation(lnob[:, :], xt[:, :], AF.Identity,
                                      scale=st[:, 7:8], bias=st[:, 0:1])
                    inc("sac", i)
                xtg[tb8] = n_of("sac")

                @block.tensor
                def _(t):
                    wt(t, "sac")
                    if bankg[0]:
                        wt(t, "sdv", bankg[0])
                    for k in range(KD):
                        i = t.transpose(bankbf(k // 8)[:, (k % 8) * 128:(k % 8) * 128 + 128],
                                        lnob[:, k * 128:(k + 1) * 128], sb["identbf"][:, :])
                        if k == KD - 1:
                            inc("spe", i)
                lnog[0] = n_of("spe")

                @block.vector
                def _(v, tb8=tb8):
                    wt(v, "spe")
                    for b4 in range(4):
                        i = v.tensor_copy(
                            ln2.rearrange("p (k c) -> p k c", k=KD)[:, b4 * 8:(b4 + 1) * 8,
                                                                    tb8 * 128:(tb8 + 1) * 128],
                            bankbf(b4).rearrange("p (k c) -> p k c", k=8))
                        if b4 == 3:
                            inc("sdv", i)
                bankg[0] = n_of("sdv")

                @block.tensor
                def _(t, tb8=tb8):
                    wt(t, "sdv"); wt(t, "swc")
                    for k in range(KD):
                        t.matmul(ps[:, 4 * 512:4 * 512 + 16],
                                 ln2.rearrange("p (k c) -> p k c", k=KD)[:, k, tb8 * 128:(tb8 + 1) * 128],
                                 rv(WGT, KD * 16, BF16)[:, k * 16:(k + 1) * 16],
                                 start=(k == 0), stop=(k == KD - 1))
                    for k in range(KD):
                        i = t.matmul(ps[:, 5 * 512:5 * 512 + 128],
                                     ln2.rearrange("p (k c) -> p k c", k=KD)[:, k, tb8 * 128:(tb8 + 1) * 128],
                                     rv(AFL4, KD * 128, BF16)[:, k * 128:(k + 1) * 128],
                                     start=(k == 0), stop=(k == KD - 1))
                    inc("spe", i)

                @block.scalar
                def _(sc):
                    wt(sc, "spe")
                    sc.activation(sb["ffn"][:, 32:40], ps[:, 4 * 512:4 * 512 + 8], AF.Exp)
                    inc("sac", sc.activation(sb["ffn"][:, 17:18], ps[:, 4 * 512 + 8:4 * 512 + 9], AF.Tanh,
                                             scale=0.5, bias=sb["bthr2"][:, 0:1]))

                @block.vector
                def _(v):
                    wt(v, "sac")
                    f = sb["ffn"]
                    v.tensor_scalar(f[:, 20:21], f[:, 17:18], 0.0625, 0.0625, OP.mult, OP.add)
                    v.reduce_sum(f[:, 18:19], f[:, 32:40], AX.X)
                    v.drain()
                    v.reciprocal(f[:, 19:20], f[:, 18:19])
                    v.drain()
                    v.tensor_scalar(f[:, 40:48], f[:, 32:40], f[:, 19:20], None, OP.mult)
                    v.drain()
                    v.tensor_scalar(f[:, 48:56], f[:, 40:48], f[:, 20:21], None, OP.subtract)
                    v.drain()
                    v.tensor_scalar(f[:, 48:56], f[:, 48:56], 0.0, None, OP.max)
                    v.drain()
                    v.reduce_sum(f[:, 21:22], f[:, 48:56], AX.X)
                    v.drain()
                    v.tensor_scalar(f[:, 22:23], f[:, 21:22], 0.0, None, OP.is_equal)
                    v.drain()
                    v.tensor_tensor(f[:, 21:22], f[:, 21:22], f[:, 22:23], OP.add)
                    v.drain()
                    v.reciprocal(f[:, 23:24], f[:, 21:22])
                    v.drain()
                    v.tensor_scalar(f[:, 48:56], f[:, 48:56], f[:, 23:24], 2.0, OP.mult, OP.mult)
                    v.drain()
                    i = v.tensor_tensor(rv(T4A, 128, BF16)[:, 0:128].rearrange("p (e r) -> p e r", e=8),
                                        ps[:, 5 * 512:5 * 512 + 128].rearrange("p (e r) -> p e r", e=8),
                                        f[:, 48:56].rearrange("p (e o) -> p e o", o=1).broadcast_to([128, 8, 16]), OP.mult)
                    inc("sdv", i)

                @block.tensor
                def _(t):
                    wt(t, "sdv")
                    i = t.transpose(bankbf(6)[:, 0:128], rv(T4A, 128, BF16)[:, 0:128], sb["identbf"][:, :])
                    inc("spe", i)

                @block.vector
                def _(v, tb=tb):
                    wt(v, "spe")
                    i = v.tensor_copy(rv(HWT, 16 * 128, BF16)[:, tb * 128:(tb + 1) * 128], bankbf(6)[:, 0:128])
                    inc("sdv", i)

            for ob in range(8):
                for ks in range(4):
                    buf = ks % 2
                    wsl = rv(WS4, 8 * 512, BF16, off_b=buf * 8 * 1024)

                    wsem = "swa" if buf == 0 else "swb"

                    def ldw4(g, ob=ob, ks=ks, buf=buf, wsl=wsl, wsem=wsem, tbs=tbs):
                        if tbs == 0 and ob == 0 and ks < 2:
                            g.wait_ge(sems["swd"][0], WBASE_SWD)
                        if slab4[buf]:
                            wt(g, "spe", slab4[buf])
                        if ob == 0 and ks < 2:
                            wt(g, "sac")
                            wt(g, "sdv")
                        inc(wsem, g.dma_start(out=wsl.rearrange("p (k f) -> p k f", k=8),
                                              in_=w_base_bf[ob, ks * 8:(ks + 1) * 8].rearrange("k d f -> d k f")), 16)
                    if buf == 0:
                        block.scalar(ldw4)
                    else:
                        block.sync(ldw4)

                    @block.tensor
                    def _(t, ob=ob, tbs=tbs, ks=ks, wsl=wsl, wsem=wsem):
                        wt(t, wsem)
                        if ks == 0:
                            wt(t, "sdv")
                        for tb8 in range(8):
                            for k8 in range(8):
                                k = ks * 8 + k8
                                i = t.matmul(bank(tb8),
                                             ln2.rearrange("p (k c) -> p k c", k=KD)[:, k, tb8 * 128:(tb8 + 1) * 128],
                                             wsl[:, k8 * 512:(k8 + 1) * 512],
                                             start=(k == 0), stop=False)
                        inc("spe", i)
                    slab4[buf] = n_of("spe")

                @block.tensor
                def _(t, ob=ob, tbs=tbs):
                    wt(t, "swc")
                    for tb8 in range(8):
                        tb = tbs * 8 + tb8
                        t.matmul(bank(tb8), sb["onesbf"][0:1, 0:128],
                                 region[0:1, BBR * 1024:BBR * 1024 + D * 2].bitcast(BF16)[0:1, ob * 512:(ob + 1) * 512],
                                 start=False, stop=False)
                        i = t.matmul(bank(tb8), rv(HWT, 16 * 128, BF16)[:, tb * 128:(tb + 1) * 128],
                                     rv(BFL4, D, BF16)[:, ob * 512:(ob + 1) * 512],
                                     start=False, stop=True)
                    inc("spe", i)
                mm_done = n_of("spe")

                for tb8 in range(8):
                    tb = tbs * 8 + tb8
                    row0 = tb * 128

                    @block.sync
                    def _(s, ob=ob, row0=row0):
                        wt(s, "sdv")
                        inc("sra", s.dma_start(out=rv(T4A, 512, F32),
                                               in_=x2_sp[row0:row0 + 128, ob * 512:(ob + 1) * 512]), 16)

                    @block.vector
                    def _(v, tb8=tb8, mm_done=mm_done):
                        wt(v, "spe", mm_done)
                        wt(v, "sra")
                        wt(v, "srb")
                        i = v.tensor_tensor(rv(T4B, 512, F32), bank(tb8), rv(T4A, 512, F32), OP.add)
                        inc("sdv", i)

                    @block.sync
                    def _(s, ob=ob, row0=row0):
                        wt(s, "sdv")
                        inc("srb", s.dma_start(out=y_out[row0:row0 + 128, ob * 512:(ob + 1) * 512],
                                               in_=rv(T4B, 512, F32)), 16)

    @block.sync
    def _(s):
        for nm in ["sio", "sw", "swa", "swb", "swc", "swd", "scl", "scc", "sga", "snr",
                   "srb", "scd", "sra", "sxa", "sxb", "scr", "sgt", "snl"]:
            wt(s, nm)

    @block.vector
    def _(v):
        wt(v, "srb")

    block_cm.__exit__(None, None, None)
    cm.close()
    return nc, DBG



def _stage(inputs):
    ip = np.ascontiguousarray(inputs["L0_attn_in_proj_w"])
    rows = [np.arange(INTER + CONV_DIM, PROJ)]
    for g in range(NG):
        rows.append(np.arange(2 * INTER + g * SS, 2 * INTER + (g + 1) * SS))
        rows.append(np.arange(2 * INTER + NG * SS + g * SS, 2 * INTER + NG * SS + (g + 1) * SS))
        for j in range(8):
            rows.append(np.arange(INTER + g * GRP + j * 128, INTER + g * GRP + (j + 1) * 128))
    ridx = np.concatenate(rows)
    w_sub = ip[ridx]
    w_xbcdt_s = np.ascontiguousarray(w_sub.reshape(N_BLK, 128, KD, 128).transpose(0, 2, 3, 1))
    w_gate_s = np.ascontiguousarray(ip[0:INTER].reshape(16, 512, KD, 128).transpose(0, 2, 3, 1))
    wo = np.ascontiguousarray(inputs["L0_attn_out_proj_w"])
    w_out_s = np.ascontiguousarray(wo.T.reshape(64, 128, 8, 512).transpose(2, 0, 1, 3))
    wb = np.ascontiguousarray(inputs["L0_ffn_W_base"])
    w_base_s = np.ascontiguousarray(wb.T.reshape(KD, 128, 8, 512).transpose(2, 0, 1, 3))

    cw = np.ascontiguousarray(inputs["L0_attn_conv_w"]).reshape(CONV_DIM, 4)
    cb_ = np.ascontiguousarray(inputs["L0_attn_conv_b"])
    crows = []
    for g in range(NG):
        crows.append(np.arange(INTER + g * SS, INTER + (g + 1) * SS))
        crows.append(np.arange(INTER + NG * SS + g * SS, INTER + NG * SS + (g + 1) * SS))
        for j in range(8):
            crows.append(np.arange(g * GRP + j * 128, g * GRP + (j + 1) * 128))
    cidx = np.concatenate(crows)
    conv_taps_s = np.ascontiguousarray(cw[cidx].reshape(80, 128, 4))
    conv_bias_s = np.ascontiguousarray(cb_[cidx].reshape(80, 128))

    wg = np.ascontiguousarray(inputs["L0_ffn_W_gate"])
    wthr = np.ascontiguousarray(inputs["L0_ffn_W_thr"])
    wgt = np.zeros((D, 16), np.float32)
    wgt[:, 0:8] = wg.T
    wgt[:, 8:9] = wthr.T
    wg_thr_s = np.ascontiguousarray(wgt.reshape(KD, 128, 16))

    A = np.ascontiguousarray(inputs["L0_ffn_A"])
    a_flat_s = np.ascontiguousarray(A.reshape(128, D).T.reshape(KD, 128, 128))
    Bm = np.ascontiguousarray(inputs["L0_ffn_B"])
    b_flat_s = np.ascontiguousarray(Bm.transpose(0, 2, 1).reshape(128, D))

    norm_w_s = np.ascontiguousarray(inputs["L0_attn_norm_w"]).reshape(64, 128)

    common = dict(
        w_xbcdt=w_xbcdt_s, w_gate=w_gate_s, w_out=w_out_s, w_base=w_base_s,
        conv_taps=conv_taps_s, conv_bias=conv_bias_s,
        dt_bias=np.ascontiguousarray(inputs["L0_attn_dt_bias"]),
        d_vec=np.ascontiguousarray(inputs["L0_attn_D"]),
        norm_w=norm_w_s, wg_thr=wg_thr_s,
        b_thr=np.ascontiguousarray(inputs["L0_ffn_b_thr"]),
        b_base=np.ascontiguousarray(inputs["L0_ffn_b_base"]),
        a_flat=a_flat_s, b_flat=b_flat_s,
    )

    x = np.ascontiguousarray(inputs["L0_attn_hidden_states"])
    in_maps = []
    for c in range(8):
        b, h = c // 2, c % 2
        m = dict(common)
        m["x_own"] = np.ascontiguousarray(x[b, h * TOK:(h + 1) * TOK])
        m["x_pre"] = np.ascontiguousarray(x[b, 0:TOK])
        m["x_halo"] = np.ascontiguousarray(x[b, TOK - 3:TOK]) if h == 1 else np.ascontiguousarray(x[b, 0:3])
        fl = np.zeros((128, 4), np.float32)
        fl[:, 0] = float(h)
        fl[:, 1] = float(h)
        fl[:, 2] = 1.0 - float(h)
        m["flags"] = fl
        in_maps.append(m)
    return in_maps


_CACHE = {}


def kernel(**inputs):
    taps = tuple(t for t in os.environ.get("KDBG", "").split(",") if t)
    nc, DBG = build_program(taps)
    in_maps = _stage(inputs)
    res = run_bass_kernel_spmd(nc, in_maps, core_ids=list(range(8)))
    _CACHE["res"] = res
    _CACHE["DBG"] = DBG
    out = np.zeros((4, 4096, D), np.float32)
    for c in range(8):
        b, h = c // 2, c % 2
        out[b, h * TOK:(h + 1) * TOK] = res.results[c]["y_out"]
    return out
